# version 03
# baseline (speedup 1.0000x reference)
import math
from contextlib import ExitStack

import numpy as np
import ml_dtypes

import concourse.bass as bass
import concourse.mybir as mybir
from concourse.bass_utils import run_bass_kernel_spmd
from concourse.alu_op_type import AluOpType as ALU
from concourse.bass_core_rust import IndirectOffsetOnAxis

F32 = mybir.dt.float32
BF16 = mybir.dt.bfloat16
U32 = mybir.dt.uint32
AF = mybir.ActivationFunctionType
AX = mybir.AxisListType

T = 4096
D = 1024
DI = 2048
DS = 16
DTR = 64
NCORES = 8

ENGINES = ("sync", "vector", "scalar", "tensor", "gpsimd")

ACT_CONSTS = [1e-5, 1e-12, 1e-6, 1e-20, -0.5, -1.0, 0.5, 1.0, 2.5, 32.0,
              1.0 / D, 1.0 / DI]




_NAME_UID = [0]


def sbuf_u(nc, name, shape, dtype):
    _NAME_UID[0] += 1
    return nc.sbuf_tensor(f"{name}_{_NAME_UID[0]}", shape, dtype)


def psum_u(nc, name, shape, dtype):
    _NAME_UID[0] += 1
    return nc.psum_tensor(f"{name}_{_NAME_UID[0]}", shape, dtype)

class Prog:

    def __init__(self, nc, ctx):
        self.nc = nc
        self.ctx = ctx
        self.q = {e: [] for e in ENGINES}
        self._sems = {}
        self._cnt = {}
        self._dirty = {e: False for e in ENGINES}
        self._last_inst = {e: None for e in ENGINES}
        self.ph = self.sem("ph")
        self.esem = {e: self.sem("fin_" + e) for e in ENGINES}

    def sem(self, name):
        if name not in self._cnt:
            h = self.ctx.enter_context(self.nc.semaphore(name))
            self._cnt[name] = 0
            self._sems[name] = h
        return name

    def cnt(self, sem):
        return self._cnt[sem]

    def _bump(self, sem, n):
        self._cnt[sem] += n
        return self._cnt[sem]

    def emit(self, eng, fn, wait=(), inc=(), is_dma=False, dirty=True):
        sems = self._sems
        targets = [self._bump(s, n) for (s, n) in inc]
        inc2 = list(inc)
        wait2 = list(wait)

        def thunk(e):
            for s, v in wait2:
                e.wait_ge(sems[s], v)
            inst = fn(e)
            for s, n in inc2:
                inst = inst.then_inc(sems[s], n)
            return inst

        self.q[eng].append(thunk)
        if dirty:
            self._dirty[eng] = True
        return targets

    def wait(self, eng, sem, value):
        sems = self._sems
        self.q[eng].append(lambda e: e.wait_ge(sems[sem], value))

    def dma(self, eng, out_ap, in_ap, sem=None, **kw):
        if sem is None:
            sem = "dma_" + eng
        if sem not in self._cnt:
            h = self.ctx.enter_context(self.nc.semaphore(sem))
            self._cnt[sem] = 0
            self._sems[sem] = h
        tgt = self.emit(eng, lambda e: e.dma_start(out=out_ap, in_=in_ap, **kw),
                        inc=[(sem, 16)])
        return sem, tgt[0]

    def barrier(self):
        sems = self._sems
        for e in ENGINES:
            if self._dirty[e]:
                if e in ("vector", "scalar", "tensor", "gpsimd"):
                    t = self.emit(e, lambda en: en.engine_nop() if hasattr(en, "engine_nop") else en.nop(),
                                  inc=[(self.esem[e], 1)], dirty=False)[0]
                    self.wait(e, self.esem[e], t)
                tgt = self._bump(self.ph, 1)
                self.q[e].append(lambda en, s=sems[self.ph]: en.nop().then_inc(s, 1))
        total = self._cnt[self.ph]
        for e in ENGINES:
            self.wait(e, self.ph, total)
            self._dirty[e] = False
        return total

    def finalize(self):
        nc = self.nc
        with nc.Block() as block:
            @block.sync
            def _(e):
                for th in self.q["sync"]:
                    th(e)

            @block.vector
            def _(e):
                for th in self.q["vector"]:
                    th(e)

            @block.scalar
            def _(e):
                for th in self.q["scalar"]:
                    th(e)

            @block.tensor
            def _(e):
                for th in self.q["tensor"]:
                    th(e)

            @block.gpsimd
            def _(e):
                for th in self.q["gpsimd"]:
                    th(e)


def _cols(v):
    n = v.shape[0] // 128
    return np.ascontiguousarray(v.reshape(n, 128).T)


def _prew(w):
    K, I = w.shape
    nk = K // 128
    return np.ascontiguousarray(w.reshape(nk, 128, I).transpose(1, 0, 2).reshape(128, nk * I))


def host_prep(inputs):
    f = np.float32
    sh = {}
    a = {k: np.asarray(v) for k, v in inputs.items()}

    sh["c_actc"] = np.tile(np.asarray(ACT_CONSTS, f)[None, :], (128, 1))
    sh["c_ident"] = np.eye(128, dtype=f)
    sh["c_ident_bf"] = np.eye(128, dtype=ml_dtypes.bfloat16)
    sh["c_ones_col"] = np.ones((128, 1), f)
    sh["c_ones_col_bf"] = np.ones((128, 1), ml_dtypes.bfloat16)
    sh["c_ones_row"] = np.ones((1, 128), f)
    for K in (16, 32):
        s = np.zeros((K, K * 128), f)
        for n in range(K):
            s[n, n * 128:(n + 1) * 128] = 1.0
        sh[f"c_sel{K}"] = s
    half = D // 2
    freq = np.exp(np.arange(half, dtype=np.float64) * (-math.log(10000.0) / (half - 1)))
    pos = (np.arange(T, dtype=np.float64) + 1.0)[:, None] * freq[None, :]
    pe = np.concatenate([np.sin(pos), np.cos(pos)], axis=-1).astype(f)
    sh["c_pe"] = np.ascontiguousarray(pe.T)

    w_inproj = np.ascontiguousarray(a["L0_attn_in_proj_w"].T)
    sh["w_ipu_p"] = _prew(w_inproj[:, 0:2048])
    sh["w_ipz_p"] = _prew(w_inproj[:, 2048:4096])
    sh["w_conv"] = _cols(np.ascontiguousarray(a["L0_attn_conv_w"].T).reshape(-1))
    for j in range(4):
        sh[f"w_conv{j}"] = _cols(np.ascontiguousarray(a["L0_attn_conv_w"][:, j]))
    sh["b_conv"] = _cols(a["L0_attn_conv_b"])
    sh["w_xproj_p"] = _prew(np.ascontiguousarray(a["L0_attn_x_proj_w"].T))
    sh["w_dtproj"] = np.ascontiguousarray(a["L0_attn_dt_proj_w"].T)
    sh["b_dtproj"] = _cols(a["L0_attn_dt_proj_b"])
    sh["a_log"] = np.ascontiguousarray(
        a["L0_attn_A_log"].reshape(16, 128, DS).transpose(1, 0, 2).reshape(128, 16 * DS))
    sh["d_vec"] = _cols(a["L0_attn_D"])
    sh["w_outproj_p"] = _prew(np.ascontiguousarray(a["L0_attn_out_proj_w"].T))
    sh["g_norm"] = _cols(a["L0_attn_norm_w"])
    sh["w_l0ff1_p"] = _prew(np.ascontiguousarray(a["L0_attn_ffn_w1"].T))
    sh["b_l0ff1"] = _cols(a["L0_attn_ffn_b1"])
    sh["w_l0ff2_p"] = _prew(np.ascontiguousarray(a["L0_attn_ffn_w2"].T))
    sh["b_l0ff2"] = _cols(a["L0_attn_ffn_b2"])
    sh["g_ffnorm"] = _cols(a["L0_attn_ffn_norm_w"])

    sh["w_l0gate"] = np.ascontiguousarray(a["L0_ffn_gate_w"].T)
    sh["b_l0gate"] = np.ascontiguousarray(a["L0_ffn_gate_bias"].reshape(1, 32))
    eg = a["L0_ffn_w_gate"].transpose(0, 2, 1)
    eu = a["L0_ffn_w_up"].transpose(0, 2, 1)
    sh["w_l0egp"] = np.ascontiguousarray(
        eg.reshape(32, 8, 128, 512).transpose(0, 2, 1, 3).reshape(32, 128, 8 * 512))
    sh["w_l0eup"] = np.ascontiguousarray(
        eu.reshape(32, 8, 128, 512).transpose(0, 2, 1, 3).reshape(32, 128, 8 * 512))
    wd = a["L0_ffn_w_down"].transpose(0, 2, 1)
    sh["w_l0edp"] = np.ascontiguousarray(
        wd.reshape(4, 8, 4, 128, 8, 128).transpose(4, 0, 3, 1, 2, 5)
        .reshape(8, 4, 128, 8 * 4 * 128))
    sg_ = np.ascontiguousarray(a["L0_ffn_sh_gate"].T)
    su_ = np.ascontiguousarray(a["L0_ffn_sh_up"].T)
    sh["w_l0sgp"] = _prew(sg_)
    sh["w_l0sup0"] = _prew(np.ascontiguousarray(su_[:, 0:512]))
    sh["w_l0sup1"] = _prew(np.ascontiguousarray(su_[:, 512:1024]))
    sd_ = a["L0_ffn_sh_down"].T
    sdr = sd_.reshape(1024, 8, 128).transpose(1, 0, 2)
    sh["w_l0sdp"] = np.ascontiguousarray(
        sdr.reshape(8, 8, 128, 128).transpose(0, 2, 1, 3).reshape(8, 128, 8 * 128))

    for nm in ("Wq", "Wk", "Wv", "Wo", "W1", "W2"):
        wl = a["L1_attn_" + nm]
        sh[f"w_{nm}_p"] = np.ascontiguousarray(np.stack([_prew(wl[l]) for l in range(4)]))
    for nm in ("bq", "bk", "bv", "bo", "b1", "b2", "ln1_s", "ln1_b", "ln2_s", "ln2_b"):
        v = a["L1_attn_" + nm]
        sh["p_" + nm] = np.ascontiguousarray(v)
    sh["p_lnf_s"] = _cols(a["L1_attn_lnf_s"])
    sh["p_lnf_b"] = _cols(a["L1_attn_lnf_b"])
    for nm, tgt in (("bq", "bq_cols"), ("bo", "bo_cols"), ("b2", "b2_cols"),
                    ("ln1_s", "ln1_s_cols"), ("ln1_b", "ln1_b_cols"),
                    ("ln2_s", "ln2_s_cols"), ("ln2_b", "ln2_b_cols")):
        sh[tgt] = np.concatenate([_cols(a["L1_attn_" + nm][l]) for l in range(4)], axis=1)
    sh["b1_cols"] = np.concatenate([_cols(a["L1_attn_b1"][l]) for l in range(4)], axis=1)
    s8 = np.zeros((8, 8 * 128), f)
    for n in range(8):
        s8[n, n * 128:(n + 1) * 128] = 1.0
    sh["c_sel8"] = s8
    sh["l1_b1_cols"] = np.concatenate(
        [_cols(a["L1_ffn_b1"][e]) for e in range(8)] +
        [_cols(a["L1_ffn_sb1"][s_]) for s_ in range(2)], axis=1)
    zsel = np.zeros((16, 8 * 128), f)
    for s_ in range(8):
        zsel[2 * s_, s_ * 128:s_ * 128 + 64] = 1.0
        zsel[2 * s_ + 1, s_ * 128 + 64:(s_ + 1) * 128] = 1.0
    sh["c_zsel"] = zsel

    sh["w_l1r"] = np.ascontiguousarray(a["L1_ffn_rW"])
    sh["b_l1r"] = np.ascontiguousarray(a["L1_ffn_rb"].reshape(1, 8))
    sh["w_l1e1t"] = np.ascontiguousarray(
        a["L1_ffn_W1"].reshape(8, 8, 128, 32, 128).transpose(0, 3, 2, 1, 4)
        .reshape(8, 32, 128, 8 * 128))
    sh["b_l1e1"] = np.ascontiguousarray(a["L1_ffn_b1"])
    sh["w_l1e2t"] = np.ascontiguousarray(
        a["L1_ffn_W2"].reshape(8, 32, 128, 8, 128).transpose(0, 3, 2, 1, 4)
        .reshape(8, 8, 128, 32 * 128))
    sh["b_l1e2"] = np.ascontiguousarray(a["L1_ffn_b2"])
    sh["w_l1s1t"] = np.ascontiguousarray(
        a["L1_ffn_sW1"].reshape(2, 8, 128, 32, 128).transpose(0, 3, 2, 1, 4)
        .reshape(2, 32, 128, 8 * 128))
    sh["b_l1s1"] = np.ascontiguousarray(a["L1_ffn_sb1"])
    sh["w_l1s2t"] = np.ascontiguousarray(
        a["L1_ffn_sW2"].reshape(2, 32, 128, 8, 128).transpose(0, 3, 2, 1, 4)
        .reshape(2, 8, 128, 32 * 128))
    sh["b_l1s2"] = np.ascontiguousarray(a["L1_ffn_sb2"])
    CPv = 1280
    lst8 = np.zeros((8, 8), f)
    for e in range(8):
        lst8[:e, e] = 1.0
    sh["c_lst8"] = lst8
    sh["c_ecol8"] = (np.arange(8, dtype=f) * CPv).reshape(8, 1)
    g2 = np.zeros((128, 2), f)
    g2[0:8, 0] = 1.0
    g2[64:72, 1] = 1.0
    sh["c_grp2"] = g2
    sh["c_dump8"] = np.tile((8 * CPv + (np.arange(4096) % 128)).astype(f)[None, :], (8, 1))
    sh["c_b2e"] = np.concatenate([_cols(a["L1_ffn_b2"][e]) for e in range(8)], axis=1)
    sh["c_sb2"] = np.concatenate([_cols(a["L1_ffn_sb2"][s_]) for s_ in range(2)], axis=1)

    x = a["L0_attn_input_tensor"]
    per_core = []
    for c in range(NCORES):
        d = {"x0": np.ascontiguousarray(x[c].T)}
        per_core.append(d)
    return sh, per_core


class Tensors:

    def __init__(self, nc, shared, dump=()):
        self.nc = nc
        self.dump = set(dump)
        self.t = {}
        for k, v in shared.items():
            dt = BF16 if v.dtype == ml_dtypes.bfloat16 else F32
            self.t[k] = nc.dram_tensor(k, list(v.shape), dt, kind="ExternalInput").ap()
        self.t["x0"] = nc.dram_tensor("x0", [D, T], F32, kind="ExternalInput").ap()
        self.t["out"] = nc.dram_tensor("out", [D, T], F32, kind="ExternalOutput").ap()

    def scratch(self, name, shape, dtype):
        kind = "ExternalOutput" if name in self.dump else "Internal"
        self.t[name] = self.nc.dram_tensor(name, list(shape), dtype, kind=kind).ap()
        return self.t[name]

    def __getitem__(self, k):
        return self.t[k]


def boot(P, ts, ctx):
    nc = P.nc
    n = len(ACT_CONSTS)
    actc = ctx.enter_context(sbuf_u(nc, "actc", [128, n], F32))
    P.dma("sync", actc[:], ts["c_actc"])
    for i, v in enumerate(ACT_CONSTS):
        nc.const_aps.register(F32, v, actc[:, i:i + 1])
    P.barrier()


def emit_ln_fm(P, ts, ctx, src, out_bf=None, out_f32=None, eps=1e-5,
               extra=None, tc=512):
    nc = P.nc
    nblk = D // 128
    nch = T // tc
    with ExitStack() as sctx:
        xc = sctx.enter_context(sbuf_u(nc, "ln_xc", [128, nblk * tc], F32))
        xb = sctx.enter_context(sbuf_u(nc, "ln_xb", [128, nblk * tc], BF16))
        sq = sctx.enter_context(sbuf_u(nc, "ln_sq", [128, nblk * tc], BF16))
        rows = sctx.enter_context(sbuf_u(nc, "ln_rows", [1, 6 * tc], F32))
        bcast = sctx.enter_context(sbuf_u(nc, "ln_bc", [128, 2 * tc], F32))
        onesc = sctx.enter_context(sbuf_u(nc, "ln_ones", [128, 1], BF16))
        onesr = sctx.enter_context(sbuf_u(nc, "ln_onesr", [1, 128], F32))
        obf = sctx.enter_context(sbuf_u(nc, "ln_obf", [128, nblk * tc], BF16))
        of32 = sctx.enter_context(sbuf_u(nc, "ln_of32", [128, nblk * tc], F32))
        pec = sctx.enter_context(sbuf_u(nc, "ln_pec", [128, nblk * tc], F32)) \
            if extra else None
        ps_s = sctx.enter_context(psum_u(nc, "ln_ps_s", [1, tc], F32))
        ps_s2 = sctx.enter_context(psum_u(nc, "ln_ps_s2", [1, tc], F32))
        ps_bc = sctx.enter_context(psum_u(nc, "ln_ps_bc", [128, 2 * tc], F32))

        ld = P.sem("gh0")
        pe_ld = P.sem("gd0")
        ld_base = P.cnt(ld)
        peld_base = P.cnt(pe_ld)
        ca = P.sem("gc0")
        mmA = P.sem("gc1")
        rowd = P.sem("gc2")
        bcd = P.sem("gc3")
        bce = P.sem("gc4")
        app = P.sem("gc5")
        cst = P.sem("gc6")
        st = P.sem("gd1")

        P.dma("sync", onesc[:], ts["c_ones_col_bf"], sem=ld)
        P.dma("sync", onesr[:], ts["c_ones_row"], sem=ld)

        for c in range(nch):
            t0 = c * tc
            for g in range(nblk):
                P.dma("sync", xc[:, g * tc:(g + 1) * tc],
                      src[g * 128:(g + 1) * 128, t0:t0 + tc], sem=ld)
            if extra:
                for g in range(nblk):
                    P.dma("gpsimd", pec[:, g * tc:(g + 1) * tc],
                          ts[extra[3]][g * 128:(g + 1) * 128, t0:t0 + tc], sem=pe_ld)
            ld_tgt = ld_base + 32 + (c + 1) * nblk * 16
            for g in range(nblk):
                sl = slice(g * tc, (g + 1) * tc)
                P.emit("vector", lambda e, a=xb[:, sl], b=xc[:, sl]: e.tensor_copy(out=a, in_=b),
                       wait=[(ld, ld_tgt)] if g == 0 else (), inc=[(ca, 1)])
                P.emit("scalar", lambda e, a=sq[:, sl], b=xc[:, sl]:
                       e.activation(out=a, in_=b, func=AF.Square),
                       wait=[(ld, ld_tgt)] if g == 0 else (), inc=[(ca, 1)])
            for g in range(nblk):
                sl = slice(g * tc, (g + 1) * tc)
                P.emit("tensor", lambda e, r=xb[:, sl], o=ps_s[:], s=(g == 0), p=(g == nblk - 1):
                       e.matmul(out=o, lhsT=onesc[:], rhs=r, start=s, stop=p),
                       wait=[(ca, P.cnt(ca))] if g == 0 else ())
            for g in range(nblk):
                sl = slice(g * tc, (g + 1) * tc)
                P.emit("tensor", lambda e, r=sq[:, sl], o=ps_s2[:], s=(g == 0), p=(g == nblk - 1):
                       e.matmul(out=o, lhsT=onesc[:], rhs=r, start=s, stop=p),
                       inc=[(mmA, 1)] if g == nblk - 1 else ())
            mrow = rows[:, 0 * tc:1 * tc]; ex2 = rows[:, 1 * tc:2 * tc]
            var = rows[:, 2 * tc:3 * tc]; rstd = rows[:, 3 * tc:4 * tc]
            brow = rows[:, 4 * tc:5 * tc]; tmp = rows[:, 5 * tc:6 * tc]
            P.emit("vector", lambda e, o=mrow, i=ps_s[:]:
                   e.tensor_scalar(out=o, in0=i, scalar1=1.0 / D, scalar2=None, op0=ALU.mult),
                   wait=[(mmA, P.cnt(mmA))])
            P.emit("vector", lambda e, o=ex2, i=ps_s2[:]:
                   e.tensor_scalar(out=o, in0=i, scalar1=1.0 / D, scalar2=None, op0=ALU.mult))
            P.emit("vector", lambda e, o=tmp, i=mrow: e.tensor_tensor(out=o, in0=i, in1=i, op=ALU.mult))
            P.emit("vector", lambda e, o=var, i=ex2, j=tmp: e.tensor_tensor(out=o, in0=i, in1=j, op=ALU.subtract),
                   inc=[(rowd, 1)])
            P.emit("scalar", lambda e, o=tmp, i=var: e.activation(out=o, in_=i, func=AF.Ln, bias=eps),
                   wait=[(rowd, P.cnt(rowd))])
            P.emit("scalar", lambda e, o=rstd, i=tmp: e.activation(out=o, in_=i, func=AF.Exp, scale=-0.5),
                   inc=[(rowd, 1)])
            P.emit("vector", lambda e, o=brow, i=mrow, j=rstd:
                   e.scalar_tensor_tensor(out=o, in0=i, scalar=-1.0, in1=j, op0=ALU.mult, op1=ALU.mult),
                   wait=[(rowd, P.cnt(rowd))], inc=[(rowd, 1)])
            P.emit("tensor", lambda e, o=ps_bc[:, 0:tc], r=rstd:
                   e.matmul(out=o, lhsT=onesr[:], rhs=r, start=True, stop=True),
                   wait=[(rowd, P.cnt(rowd))])
            P.emit("tensor", lambda e, o=ps_bc[:, tc:2 * tc], r=brow:
                   e.matmul(out=o, lhsT=onesr[:], rhs=r, start=True, stop=True),
                   inc=[(bcd, 1)])
            P.emit("scalar", lambda e, o=bcast[:, 0:tc], i=ps_bc[:, 0:tc]: e.copy(out=o, in_=i),
                   wait=[(bcd, P.cnt(bcd))])
            P.emit("scalar", lambda e, o=bcast[:, tc:2 * tc], i=ps_bc[:, tc:2 * tc]: e.copy(out=o, in_=i),
                   inc=[(bce, 1)])
            for g in range(nblk):
                sl = slice(g * tc, (g + 1) * tc)
                P.emit("vector", lambda e, o=of32[:, sl], i=xc[:, sl], ab=bcast[:, 0:tc]:
                       e.tensor_tensor(out=o, in0=i, in1=ab, op=ALU.mult),
                       wait=[(bce, P.cnt(bce))] if g == 0 else ())
                P.emit("vector", lambda e, o=of32[:, sl], i=of32[:, sl], bb=bcast[:, tc:2 * tc]:
                       e.tensor_tensor(out=o, in0=i, in1=bb, op=ALU.add),
                       inc=[(app, 1)])
                if out_bf is not None:
                    P.emit("scalar", lambda e, o=obf[:, sl], i=of32[:, sl], w=P.cnt(app):
                           e.copy(out=o, in_=i),
                           wait=[(app, P.cnt(app))], inc=[(cst, 1)])
            if extra:
                _, out_name, alpha, _ = extra
                P.wait("vector", app, P.cnt(app))
                for g in range(nblk):
                    sl = slice(g * tc, (g + 1) * tc)
                    P.emit("vector", lambda e, o=xc[:, sl], i=of32[:, sl], p=pec[:, sl], al=alpha:
                           e.scalar_tensor_tensor(out=o, in0=i, scalar=al, in1=p, op0=ALU.mult, op1=ALU.add),
                           wait=[(pe_ld, peld_base + (c + 1) * nblk * 16)] if g == 0 else (),
                           inc=[(app, 1)])
            app_now = P.cnt(app)
            cst_now = P.cnt(cst)
            for g in range(nblk):
                sl = slice(g * tc, (g + 1) * tc)
                if out_f32 is not None:
                    P.wait("gpsimd", app, app_now)
                    P.dma("gpsimd", out_f32[g * 128:(g + 1) * 128, t0:t0 + tc], of32[:, sl], sem=st)
                if out_bf is not None:
                    P.wait("gpsimd", cst, cst_now)
                    P.dma("gpsimd", out_bf[g * 128:(g + 1) * 128, t0:t0 + tc], obf[:, sl], sem=st)
                if extra:
                    P.wait("gpsimd", app, app_now)
                    P.dma("gpsimd", ts[extra[1]][g * 128:(g + 1) * 128, t0:t0 + tc], xc[:, sl], sem=st)
            P.wait("sync", st, P.cnt(st))
            if extra:
                P.wait("gpsimd", st, P.cnt(st))
        P.wait("sync", st, P.cnt(st))
        P.barrier()


def emit_proj(P, ctx, *, w, Kdim, Idim, rhs, outs, bias=None, func=None,
              res=None, tc=512, rhs_is_bf=True, tag="pj"):
    nc = P.nc
    nk = (Kdim + 127) // 128
    ni = (Idim + 127) // 128
    nch = T // tc
    with ExitStack() as sctx:
        rhs_sb = sctx.enter_context(sbuf_u(nc, tag + "_rhs", [128, nk * tc], BF16))
        w_sb = [sctx.enter_context(sbuf_u(nc, f"{tag}_w{b}", [128, nk * 128], BF16))
                for b in range(2)]
        o_sb = sctx.enter_context(sbuf_u(nc, tag + "_o", [128, 2 * tc], F32))
        ob_sb = sctx.enter_context(sbuf_u(nc, tag + "_ob", [128, 2 * tc], BF16))
        r_sb = sctx.enter_context(sbuf_u(nc, tag + "_r", [128, 2 * tc], F32)) \
            if res is not None else None
        e1_sb = sctx.enter_context(sbuf_u(nc, tag + "_e1", [128, 2 * tc], F32)) \
            if func == "elu1" else None
        b_sb = sctx.enter_context(sbuf_u(nc, tag + "_b", [128, max(ni, 1)], F32)) \
            if bias is not None else None
        ps = [sctx.enter_context(psum_u(nc, f"{tag}_ps{b}", [128, tc], F32))
              for b in range(2)]

        lw2 = [P.sem("gd0"), P.sem("gd1")]
        pbs = P.sem("gh0")
        lr = P.sem("gd3")
        lres2 = [P.sem("gh1"), P.sem("gh2")]
        mm = P.sem("gc1")
        ev = P.sem("gc2")
        st = P.sem("gd2")

        pbs_t = 0
        if bias is not None:
            _, pbs_t = P.dma("sync", b_sb[:, 0:ni], bias[:, 0:ni], sem=pbs)

        psum_free = [P.cnt(ev)] * 2
        wbuf_free = [P.cnt(ev)] * 2
        osb_free = [P.cnt(st)] * 2
        res_free = [P.cnt(ev)] * 2
        last_mm_chunk = P.cnt(mm)
        first_evac = True
        for c in range(nch):
            t0 = c * tc
            if c > 0:
                P.wait("gpsimd", mm, last_mm_chunk)
            if Kdim >= 128:
                P.dma("gpsimd", rhs_sb[:].rearrange("p (c t) -> p c t", c=nk),
                      rhs[:, t0:t0 + tc].rearrange("(c p) t -> p c t", p=128), sem=lr)
            else:
                P.dma("gpsimd", rhs_sb[0:Kdim, 0:tc], rhs[:, t0:t0 + tc], sem=lr)
            lr_t = P.cnt(lr)
            for i in range(ni):
                isz = min(128, Idim - i * 128)
                wb = i % 2
                P.wait("gpsimd", ev, wbuf_free[wb])
                if Kdim >= 128:
                    P.dma("gpsimd",
                          w_sb[wb][:].rearrange("p (c i) -> p c i", c=nk)[:, :, 0:isz],
                          w[:, i * 128:i * 128 + isz].rearrange("(c p) i -> p c i", p=128),
                          sem=lw2[wb])
                else:
                    P.dma("gpsimd", w_sb[wb][0:Kdim, 0:isz],
                          w[:, i * 128:i * 128 + isz], sem=lw2[wb])
                lw_s, lw_t = lw2[wb], P.cnt(lw2[wb])
                if res is not None:
                    P.wait("sync", ev, res_free[i % 2])
                    P.dma("sync", r_sb[:, (i % 2) * tc:(i % 2) * tc + tc][0:isz, :],
                          res[i * 128:i * 128 + isz, t0:t0 + tc], sem=lres2[i % 2])
                pb = i % 2
                P.wait("tensor", ev, psum_free[pb])
                for k in range(nk):
                    ksz = min(128, Kdim - k * 128)
                    P.emit("tensor", lambda e, o=ps[pb][0:isz, :],
                           lt=w_sb[wb][0:ksz, k * 128:k * 128 + isz],
                           rt=rhs_sb[0:ksz, k * tc:(k + 1) * tc],
                           s=(k == 0), p_=(k == nk - 1):
                           e.matmul(out=o, lhsT=lt, rhs=rt, start=s, stop=p_),
                           wait=[(lw_s, lw_t), (lr, lr_t)] if k == 0 else (),
                           inc=[(mm, 1)] if k == nk - 1 else ())
                last_mm_chunk = P.cnt(mm)
                bcol = b_sb[0:isz, i:i + 1] if bias is not None else None
                osl = o_sb[:, pb * tc:(pb + 1) * tc][0:isz, :]
                obsl = ob_sb[:, pb * tc:(pb + 1) * tc][0:isz, :]
                wait_ev = [(mm, P.cnt(mm)), (st, osb_free[pb])]
                if first_evac and bias is not None:
                    wait_ev.append((pbs, pbs_t))
                first_evac = False
                if res is not None:
                    wait_ev.append((lres2[i % 2], P.cnt(lres2[i % 2])))
                want_f32 = any(dt == F32 for (_, dt) in outs)
                want_bf = any(dt == BF16 for (_, dt) in outs)
                ffunc = func if func is not None else AF.Identity
                first_out = osl if want_f32 else obsl
                if func == "elu1":
                    vs = osl
                    es = e1_sb[:, pb * tc:(pb + 1) * tc][0:isz, :]
                    if bcol is not None:
                        P.emit("vector", lambda e, o=vs, i_=ps[pb][0:isz, :], b=bcol:
                               e.tensor_scalar(out=o, in0=i_, scalar1=b, scalar2=None, op0=ALU.add),
                               wait=wait_ev)
                    else:
                        P.emit("vector", lambda e, o=vs, i_=ps[pb][0:isz, :]:
                               e.tensor_copy(out=o, in_=i_), wait=wait_ev)
                    P.emit("vector", lambda e, o=es, i_=vs:
                           e.tensor_scalar(out=o, in0=i_, scalar1=0.0, scalar2=None, op0=ALU.min),
                           inc=[(ev, 1)])
                    P.emit("scalar", lambda e, o=es, i_=es:
                           e.activation(out=o, in_=i_, func=AF.Exp),
                           wait=[(ev, P.cnt(ev))], inc=[(ev, 1)])
                    P.emit("vector", lambda e, o=obsl, i_=vs, l=es:
                           e.scalar_tensor_tensor(out=o, in0=i_, scalar=0.0, in1=l,
                                                  op0=ALU.max, op1=ALU.add),
                           wait=[(ev, P.cnt(ev))], inc=[(ev, 1)])
                elif res is None:
                    if bcol is not None:
                        P.emit("scalar", lambda e, o=first_out, i_=ps[pb][0:isz, :], bc=bcol, fn=ffunc:
                               e.activation(out=o, in_=i_, func=fn, bias=bc),
                               wait=wait_ev, inc=[(ev, 1)])
                    else:
                        P.emit("scalar", lambda e, o=first_out, i_=ps[pb][0:isz, :], fn=ffunc:
                               e.activation(out=o, in_=i_, func=fn),
                               wait=wait_ev, inc=[(ev, 1)])
                else:
                    rsl = r_sb[:, (i % 2) * tc:(i % 2) * tc + tc][0:isz, :]
                    if bcol is not None:
                        P.emit("vector", lambda e, o=first_out, i_=ps[pb][0:isz, :], bc=bcol, rr=rsl:
                               e.scalar_tensor_tensor(out=o, in0=i_, scalar=bc, in1=rr,
                                                      op0=ALU.add, op1=ALU.add),
                               wait=wait_ev, inc=[(ev, 1)])
                    else:
                        P.emit("vector", lambda e, o=first_out, i_=ps[pb][0:isz, :], rr=rsl:
                               e.tensor_tensor(out=o, in0=i_, in1=rr, op=ALU.add),
                               wait=wait_ev, inc=[(ev, 1)])
                if want_f32 and want_bf:
                    P.emit("scalar", lambda e, o=obsl, i_=osl:
                           e.copy(out=o, in_=i_),
                           wait=[(ev, P.cnt(ev))], inc=[(ev, 1)])
                psum_free[pb] = P.cnt(ev)
                wbuf_free[wb] = P.cnt(ev)
                res_free[i % 2] = P.cnt(ev)
                ev_t = P.cnt(ev)
                P.wait("gpsimd", ev, ev_t)
                for (ap, dt_) in outs:
                    src = osl if dt_ == F32 else obsl
                    P.dma("gpsimd", ap[i * 128:i * 128 + isz, t0:t0 + tc], src, sem=st)
                osb_free[pb] = P.cnt(st)
        P.wait("sync", st, P.cnt(st))
        P.wait("gpsimd", st, P.cnt(st))
        P.barrier()


def emit_proj2(P, ctx, *, w, Kdim, Idim, rhs, outs, bias=None, func=None,
               res=None, tc=512, tag="pk"):
    nc = P.nc
    nk = (Kdim + 127) // 128
    ni = (Idim + 127) // 128
    nch = T // tc
    rhs_res = Kdim <= 1024
    with ExitStack() as sctx:
        w_sb = sctx.enter_context(sbuf_u(nc, tag + "_w", [128, nk * Idim], BF16))
        if rhs_res:
            rhs_sb = [sctx.enter_context(sbuf_u(nc, tag + "_rhsr", [128, nk * T], BF16))]
        else:
            rhs_sb = [sctx.enter_context(sbuf_u(nc, f"{tag}_rhs{b}", [128, nk * tc], BF16))
                      for b in range(2)]
        o_sb = sctx.enter_context(sbuf_u(nc, tag + "_o", [128, 2 * tc], F32))
        ob_sb = sctx.enter_context(sbuf_u(nc, tag + "_ob", [128, 2 * tc], BF16))
        r_sb = sctx.enter_context(sbuf_u(nc, tag + "_r", [128, 2 * tc], F32)) \
            if res is not None else None
        e1_sb = sctx.enter_context(sbuf_u(nc, tag + "_e1", [128, 2 * tc], F32)) \
            if func == "elu1" else None
        b_sb = sctx.enter_context(sbuf_u(nc, tag + "_b", [128, max(ni, 1)], F32)) \
            if bias is not None else None
        ps = [sctx.enter_context(psum_u(nc, f"{tag}_ps{b}", [128, tc], F32))
              for b in range(2)]

        lw = P.sem("gd0")
        lr2 = [P.sem("gh3"), P.sem("gh4")]
        pbs = P.sem("gh0")
        lres2 = [P.sem("gh1"), P.sem("gh2")]
        mm = P.sem("gc1")
        ev = P.sem("gc2")
        st = P.sem("gd2")

        _, lw_t = P.dma("gpsimd", w_sb[:], w, sem=lw)
        pbs_t = 0
        if bias is not None:
            _, pbs_t = P.dma("sync", b_sb[:, 0:ni], bias[:, 0:ni], sem=pbs)

        psum_free = [P.cnt(ev)] * 2
        osb_free = [P.cnt(st)] * 2
        res_free = [P.cnt(ev)] * 2
        rhs_free = [P.cnt(mm)] * 2
        lr_t = [0, 0]
        first_evac = True

        def load_rhs(c):
            cb = c % 2
            P.wait("sync", mm, rhs_free[cb])
            P.dma("sync", rhs_sb[cb][:].rearrange("p (c t) -> p c t", c=nk),
                  rhs[:, c * tc:(c + 1) * tc].rearrange("(c p) t -> p c t", p=128),
                  sem=lr2[cb])
            lr_t[cb] = P.cnt(lr2[cb])

        if rhs_res:
            P.dma("sync", rhs_sb[0][:].rearrange("p (c t) -> p c t", c=nk),
                  rhs.rearrange("(c p) t -> p c t", p=128), sem=lr2[0])
            lr_t[0] = P.cnt(lr2[0])
        else:
            load_rhs(0)
        for c in range(nch):
            cb = 0 if rhs_res else c % 2
            t0 = c * tc
            if not rhs_res and c + 1 < nch:
                load_rhs(c + 1)
            for i in range(ni):
                isz = min(128, Idim - i * 128)
                if res is not None:
                    P.wait("sync", ev, res_free[i % 2])
                    P.dma("sync", r_sb[:, (i % 2) * tc:(i % 2) * tc + tc][0:isz, :],
                          res[i * 128:i * 128 + isz, t0:t0 + tc], sem=lres2[i % 2])
                pb = i % 2
                P.wait("tensor", ev, psum_free[pb])
                for k in range(nk):
                    ksz = min(128, Kdim - k * 128)
                    if rhs_res:
                        rt_ = rhs_sb[0][0:ksz, k * T + t0:k * T + t0 + tc]
                    else:
                        rt_ = rhs_sb[cb][0:ksz, k * tc:(k + 1) * tc]
                    P.emit("tensor", lambda e, o=ps[pb][0:isz, :],
                           lt=w_sb[0:ksz, k * Idim + i * 128:k * Idim + i * 128 + isz],
                           rt=rt_,
                           s=(k == 0), p_=(k == nk - 1):
                           e.matmul(out=o, lhsT=lt, rhs=rt, start=s, stop=p_),
                           wait=[(lw, lw_t), (lr2[cb], lr_t[cb])] if k == 0 else (),
                           inc=[(mm, 1)] if k == nk - 1 else ())
                rhs_free[cb] = P.cnt(mm)
                bcol = b_sb[0:isz, i:i + 1] if bias is not None else None
                osl = o_sb[:, pb * tc:(pb + 1) * tc][0:isz, :]
                obsl = ob_sb[:, pb * tc:(pb + 1) * tc][0:isz, :]
                wait_ev = [(mm, P.cnt(mm)), (st, osb_free[pb])]
                if first_evac and bias is not None:
                    wait_ev.append((pbs, pbs_t))
                first_evac = False
                if res is not None:
                    wait_ev.append((lres2[i % 2], P.cnt(lres2[i % 2])))
                want_f32 = any(dt == F32 for (_, dt) in outs)
                want_bf = any(dt == BF16 for (_, dt) in outs)
                ffunc = func if func is not None else AF.Identity
                first_out = osl if want_f32 else obsl
                if func == "elu1":
                    vs = osl
                    es = e1_sb[:, pb * tc:(pb + 1) * tc][0:isz, :]
                    if bcol is not None:
                        P.emit("vector", lambda e, o=vs, i_=ps[pb][0:isz, :], b=bcol:
                               e.tensor_scalar(out=o, in0=i_, scalar1=b, scalar2=None, op0=ALU.add),
                               wait=wait_ev)
                    else:
                        P.emit("vector", lambda e, o=vs, i_=ps[pb][0:isz, :]:
                               e.tensor_copy(out=o, in_=i_), wait=wait_ev)
                    P.emit("vector", lambda e, o=es, i_=vs:
                           e.tensor_scalar(out=o, in0=i_, scalar1=0.0, scalar2=None, op0=ALU.min),
                           inc=[(ev, 1)])
                    P.emit("scalar", lambda e, o=es, i_=es:
                           e.activation(out=o, in_=i_, func=AF.Exp),
                           wait=[(ev, P.cnt(ev))], inc=[(ev, 1)])
                    P.emit("vector", lambda e, o=obsl, i_=vs, l=es:
                           e.scalar_tensor_tensor(out=o, in0=i_, scalar=0.0, in1=l,
                                                  op0=ALU.max, op1=ALU.add),
                           wait=[(ev, P.cnt(ev))], inc=[(ev, 1)])
                elif res is None:
                    if bcol is not None:
                        P.emit("scalar", lambda e, o=first_out, i_=ps[pb][0:isz, :], bc=bcol, fn=ffunc:
                               e.activation(out=o, in_=i_, func=fn, bias=bc),
                               wait=wait_ev, inc=[(ev, 1)])
                    else:
                        P.emit("scalar", lambda e, o=first_out, i_=ps[pb][0:isz, :], fn=ffunc:
                               e.activation(out=o, in_=i_, func=fn),
                               wait=wait_ev, inc=[(ev, 1)])
                else:
                    rsl = r_sb[:, (i % 2) * tc:(i % 2) * tc + tc][0:isz, :]
                    if bcol is not None:
                        P.emit("vector", lambda e, o=first_out, i_=ps[pb][0:isz, :], bc=bcol, rr=rsl:
                               e.scalar_tensor_tensor(out=o, in0=i_, scalar=bc, in1=rr,
                                                      op0=ALU.add, op1=ALU.add),
                               wait=wait_ev, inc=[(ev, 1)])
                    else:
                        P.emit("vector", lambda e, o=first_out, i_=ps[pb][0:isz, :], rr=rsl:
                               e.tensor_tensor(out=o, in0=i_, in1=rr, op=ALU.add),
                               wait=wait_ev, inc=[(ev, 1)])
                if want_f32 and want_bf:
                    P.emit("scalar", lambda e, o=obsl, i_=osl:
                           e.copy(out=o, in_=i_),
                           wait=[(ev, P.cnt(ev))], inc=[(ev, 1)])
                psum_free[pb] = P.cnt(ev)
                res_free[i % 2] = P.cnt(ev)
                ev_t = P.cnt(ev)
                P.wait("gpsimd", ev, ev_t)
                for (ap, dt_) in outs:
                    src = osl if dt_ == F32 else obsl
                    P.dma("gpsimd", ap[i * 128:i * 128 + isz, t0:t0 + tc], src, sem=st)
                osb_free[pb] = P.cnt(st)
        P.wait("sync", st, P.cnt(st))
        P.wait("gpsimd", st, P.cnt(st))
        P.barrier()


def emit_l0_conv(P, ts, ctx, tc=512):
    nc = P.nc
    nblk = DI // 128
    nch = T // tc
    with ExitStack() as sctx:
        bufz = sctx.enter_context(sbuf_u(nc, "cv_bz", [128, tc + 3], F32))
        bufn = [sctx.enter_context(sbuf_u(nc, f"cv_bn{b}", [128, tc + 3], F32))
                for b in range(2)]
        acc = sctx.enter_context(sbuf_u(nc, "cv_acc", [128, 2 * tc], F32))
        uo = sctx.enter_context(sbuf_u(nc, "cv_uo", [128, 2 * tc], F32))
        ub = sctx.enter_context(sbuf_u(nc, "cv_ub", [128, 2 * tc], BF16))
        wc = sctx.enter_context(sbuf_u(nc, "cv_w", [128, 5 * (DI // 128)], F32))
        ldc = P.sem("gh0")
        ldb = {}
        dn = P.sem("gc0")
        sl_ = P.sem("gc1")
        st = P.sem("gd0")

        for j in range(4):
            P.dma("sync", wc[:, j * nblk:(j + 1) * nblk], ts[f"w_conv{j}"], sem=ldc)
        _, ldc_t = P.dma("sync", wc[:, 4 * nblk:5 * nblk], ts["b_conv"], sem=ldc)
        for i_, b_ in enumerate((id(bufz), id(bufn[0]), id(bufn[1]))):
            ldb[b_] = P.sem(f"cv_lb{i_}")
        P.emit("vector", lambda e: e.memset(bufz[:, 0:3], 0.0), inc=[(dn, 1)])
        P.barrier()

        buf_free = {}
        out_free = [0, 0]
        out_st = [0, 0]
        for g in range(nblk):
            for c in range(nch):
                t0 = c * tc
                if c == 0:
                    buf = bufz
                    dst = bufz[:, 3:3 + tc]
                    srcap = ts["u_pre"][g * 128:(g + 1) * 128, 0:tc]
                else:
                    buf = bufn[c % 2]
                    dst = buf[:]
                    srcap = ts["u_pre"][g * 128:(g + 1) * 128, t0 - 3:t0 + tc]
                bid = id(buf)
                P.wait("sync", dn, buf_free.get(bid, 0))
                P.dma("sync", dst, srcap, sem=ldb[bid])
                ld_s, ld_t = ldb[bid], P.cnt(ldb[bid])
                ob = c % 2
                asl = acc[:, ob * tc:(ob + 1) * tc]
                P.emit("vector", lambda e, o=asl, i_=buf[:, 0:tc], w0=wc[:, 0 * nblk + g:0 * nblk + g + 1]:
                       e.tensor_scalar(out=o, in0=i_, scalar1=w0, scalar2=None, op0=ALU.mult),
                       wait=[(ld_s, ld_t), (ldc, ldc_t), (sl_, out_free[ob])])
                for j in range(1, 4):
                    P.emit("vector", lambda e, o=asl, i_=buf[:, j:j + tc],
                           wj=wc[:, j * nblk + g:j * nblk + g + 1], a=asl:
                           e.scalar_tensor_tensor(out=o, in0=i_, scalar=wj, in1=a,
                                                  op0=ALU.mult, op1=ALU.add),
                           inc=[(dn, 1)] if j == 3 else ())
                buf_free[bid] = P.cnt(dn)
                usl = uo[:, ob * tc:(ob + 1) * tc]
                ubsl = ub[:, ob * tc:(ob + 1) * tc]
                P.emit("scalar", lambda e, o=usl, i_=asl, b=wc[:, 4 * nblk + g:4 * nblk + g + 1]:
                       e.activation(out=o, in_=i_, func=AF.Silu, bias=b),
                       wait=[(dn, P.cnt(dn)), (st, out_st[ob])])
                P.emit("scalar", lambda e, o=ubsl, i_=usl: e.copy(out=o, in_=i_),
                       inc=[(sl_, 1)])
                out_free[ob] = P.cnt(sl_)
                sl_t = P.cnt(sl_)
                P.wait("gpsimd", sl_, sl_t)
                P.dma("gpsimd", ts["u_f32"][g * 128:(g + 1) * 128, t0:t0 + tc], usl, sem=st)
                P.dma("gpsimd", ts["u_bf"][g * 128:(g + 1) * 128, t0:t0 + tc], ubsl, sem=st)
                out_st[ob] = P.cnt(st)
        P.wait("gpsimd", st, P.cnt(st))
        P.barrier()


def emit_l0_delta(P, ts, ctx, tc=512):
    nc = P.nc
    nblk = DI // 128
    nch = T // tc
    with ExitStack() as sctx:
        rhs_sb = sctx.enter_context(sbuf_u(nc, "dl_rhs", [DTR, tc], BF16))
        w_sb = sctx.enter_context(sbuf_u(nc, "dl_w", [DTR, DI], BF16))
        bcol = sctx.enter_context(sbuf_u(nc, "dl_b", [128, nblk], F32))
        v_sb = sctx.enter_context(sbuf_u(nc, "dl_v", [128, 2 * tc], F32))
        av_sb = sctx.enter_context(sbuf_u(nc, "dl_av", [128, 2 * tc], F32))
        e_sb = sctx.enter_context(sbuf_u(nc, "dl_e", [128, 2 * tc], F32))
        sp_sb = sctx.enter_context(sbuf_u(nc, "dl_sp", [128, 2 * tc], F32))
        du_sb = sctx.enter_context(sbuf_u(nc, "dl_du", [128, 2 * tc], F32))
        u_sb = sctx.enter_context(sbuf_u(nc, "dl_u", [128, 2 * tc], F32))
        ps = [sctx.enter_context(psum_u(nc, f"dl_ps{b}", [128, tc], F32)) for b in range(2)]
        ld = P.sem("gd0")
        lu2 = [P.sem("gh0"), P.sem("gh1")]
        mm = P.sem("gc0")
        ev = P.sem("gc1")
        sc = P.sem("gc2")
        fin = P.sem("gc3")
        st = P.sem("gd1")

        P.dma("gpsimd", w_sb[:], ts["w_dtproj"], sem=ld)
        P.dma("gpsimd", bcol[:], ts["b_dtproj"], sem=ld)
        ps_free = [0, 0]
        buf_free = [0, 0]
        u_free = [0, 0]
        for c in range(nch):
            t0 = c * tc
            P.wait("gpsimd", mm, P.cnt(mm))
            P.dma("gpsimd", rhs_sb[:], ts["dtbc"][0:DTR, t0:t0 + tc], sem=ld)
            ld_t = P.cnt(ld)
            for g in range(nblk):
                pb = g % 2
                P.wait("tensor", ev, ps_free[pb])
                P.emit("tensor", lambda e, o=ps[pb][:], lt=w_sb[:, g * 128:(g + 1) * 128], rt=rhs_sb[:]:
                       e.matmul(out=o, lhsT=lt, rhs=rt, start=True, stop=True),
                       wait=[(ld, ld_t)], inc=[(mm, 1)])
                P.wait("sync", fin, u_free[pb])
                P.dma("sync", u_sb[:, pb * tc:(pb + 1) * tc],
                      ts["u_f32"][g * 128:(g + 1) * 128, t0:t0 + tc], sem=lu2[pb])
                bc = bcol[:, g:g + 1]
                vs = v_sb[:, pb * tc:(pb + 1) * tc]
                avs = av_sb[:, pb * tc:(pb + 1) * tc]
                es = e_sb[:, pb * tc:(pb + 1) * tc]
                sps = sp_sb[:, pb * tc:(pb + 1) * tc]
                dus = du_sb[:, pb * tc:(pb + 1) * tc]
                us = u_sb[:, pb * tc:(pb + 1) * tc]
                mm_t = P.cnt(mm)
                st_guard = buf_free[pb]
                P.emit("vector", lambda e, o=vs, i_=ps[pb][:], b=bc:
                       e.tensor_scalar(out=o, in0=i_, scalar1=b, scalar2=None, op0=ALU.add),
                       wait=[(mm, mm_t), (st, st_guard)])
                P.emit("vector", lambda e, o=avs, i_=vs:
                       e.scalar_tensor_tensor(out=o, in0=i_, scalar=-1.0, in1=i_,
                                              op0=ALU.mult, op1=ALU.max),
                       inc=[(ev, 1)])
                ps_free[pb] = P.cnt(ev)
                P.emit("scalar", lambda e, o=es, i_=avs:
                       e.activation(out=o, in_=i_, func=AF.Exp, scale=-1.0),
                       wait=[(ev, P.cnt(ev))])
                P.emit("scalar", lambda e, o=es, i_=es:
                       e.activation(out=o, in_=i_, func=AF.Ln, bias=1.0),
                       inc=[(sc, 1)])
                P.emit("vector", lambda e, o=sps, i_=vs, l=es:
                       e.scalar_tensor_tensor(out=o, in0=i_, scalar=0.0, in1=l,
                                              op0=ALU.max, op1=ALU.add),
                       wait=[(sc, P.cnt(sc))])
                P.emit("vector", lambda e, o=dus, i_=sps, u_=us:
                       e.tensor_tensor(out=o, in0=i_, in1=u_, op=ALU.mult),
                       wait=[(lu2[pb], P.cnt(lu2[pb]))], inc=[(fin, 1)])
                u_free[pb] = P.cnt(fin)
                fin_t = P.cnt(fin)
                P.wait("gpsimd", fin, fin_t)
                P.dma("gpsimd", ts["delta"][g * 128:(g + 1) * 128, t0:t0 + tc], sps, sem=st)
                P.dma("gpsimd", ts["du"][g * 128:(g + 1) * 128, t0:t0 + tc], dus, sem=st)
                buf_free[pb] = P.cnt(st)
        P.wait("gpsimd", st, P.cnt(st))
        P.barrier()


def emit_l0_scan(P, ts, ctx, tc=512, dbg=False):
    nc = P.nc
    nblk = DI // 128
    nch = T // tc
    with ExitStack() as sctx:
        A_sb = sctx.enter_context(sbuf_u(nc, "sc_A", [128, nblk * DS], F32))
        hcol = sctx.enter_context(sbuf_u(nc, "sc_h0", [128, nblk * DS], F32))
        sel = sctx.enter_context(sbuf_u(nc, "sc_sel", [32, 32 * 128], F32))
        bcB = sctx.enter_context(sbuf_u(nc, "sc_bcB", [128, DS * tc], F32))
        bcC = sctx.enter_context(sbuf_u(nc, "sc_bcC", [128, DS * tc], F32))
        BC = sctx.enter_context(sbuf_u(nc, "sc_BC", [32, tc], F32))
        dts = sctx.enter_context(sbuf_u(nc, "sc_dt", [128, 2 * tc], F32))
        dus = sctx.enter_context(sbuf_u(nc, "sc_du", [128, 2 * tc], F32))
        a_sb = sctx.enter_context(sbuf_u(nc, "sc_a", [128, 2 * tc], F32))
        b_sb = sctx.enter_context(sbuf_u(nc, "sc_b", [128, 2 * tc], F32))
        h_sb = sctx.enter_context(sbuf_u(nc, "sc_hh", [128, 2 * tc], F32))
        hc_sb = sctx.enter_context(sbuf_u(nc, "sc_hc", [128, tc], F32))
        y_sb = sctx.enter_context(sbuf_u(nc, "sc_y", [128, 2 * tc], F32))
        ps = [sctx.enter_context(psum_u(nc, f"sc_ps{b}", [128, tc], F32)) for b in range(2)]
        ldboot = P.sem("gh0")
        lbc = P.sem("gh1")
        ldg2 = [P.sem("gd0"), P.sem("gd1")]
        bca = P.sem("gc0")
        bcm = P.sem("gc1")
        adn = P.sem("gc2")
        sdn = P.sem("gc3")
        ydn = P.sem("gc4")
        st = P.sem("gd2")

        P.dma("sync", A_sb[:], ts["a_log"], sem=ldboot)
        P.dma("sync", sel[:], ts["c_sel32"], sem=ldboot)
        P.wait("scalar", ldboot, P.cnt(ldboot))
        P.wait("vector", ldboot, P.cnt(ldboot))
        P.emit("scalar", lambda e: e.activation(out=A_sb[:], in_=A_sb[:], func=AF.Exp))
        P.emit("vector", lambda e: e.memset(hcol[:], 0.0))
        P.barrier()
        P.emit("vector", lambda e: e.tensor_scalar(out=A_sb[:], in0=A_sb[:], scalar1=-1.0,
                                                   scalar2=None, op0=ALU.mult))
        P.barrier()

        dt_free = [0, 0]
        ab_free = [0, 0]
        y_free = [0, 0]
        for c in range(nch):
            t0 = c * tc
            P.wait("sync", bcm, P.cnt(bcm))
            P.dma("sync", BC[:], ts["dtbc"][DTR:DTR + 32, t0:t0 + tc], sem=lbc)
            ld_bc = P.cnt(lbc)
            if c > 0:
                P.wait("scalar", ydn, P.cnt(ydn))
            bca_base = P.cnt(bca)
            for q in range(2 * DS):
                n, isC = q // 2, q % 2
                which = (DS + n) if isC else n
                dst = bcC if isC else bcB
                pb = q % 2
                wt = [(lbc, ld_bc)] if q < 2 else [(bca, bca_base + q - 1)]
                P.emit("tensor", lambda e, o=ps[pb][:], lt=sel[:, which * 128:(which + 1) * 128], r=BC[:]:
                       e.matmul(out=o, lhsT=lt, rhs=r, start=True, stop=True),
                       wait=wt, inc=[(bcm, 1)])
                P.emit("scalar", lambda e, o=dst[:, n * tc:(n + 1) * tc], i_=ps[pb][:]:
                       e.copy(out=o, in_=i_),
                       wait=[(bcm, P.cnt(bcm))], inc=[(bca, 1)])
            bca_t = P.cnt(bca)
            for g in range(nblk):
                gb = g % 2
                P.wait("gpsimd", sdn, dt_free[gb])
                P.dma("gpsimd", dts[:, gb * tc:(gb + 1) * tc],
                      ts["delta"][g * 128:(g + 1) * 128, t0:t0 + tc], sem=ldg2[gb])
                P.dma("gpsimd", dus[:, gb * tc:(gb + 1) * tc],
                      ts["du"][g * 128:(g + 1) * 128, t0:t0 + tc], sem=ldg2[gb])
                ld_s, ld_t = ldg2[gb], P.cnt(ldg2[gb])
                for n in range(DS):
                    p = g * DS + n
                    ab = (g * DS + n) % 2
                    asl = a_sb[:, ab * tc:(ab + 1) * tc]
                    bsl = b_sb[:, ab * tc:(ab + 1) * tc]
                    hsl = h_sb[:, ab * tc:(ab + 1) * tc]
                    P.emit("scalar", lambda e, o=asl, i_=dts[:, gb * tc:(gb + 1) * tc],
                           s=A_sb[:, p:p + 1]:
                           e.activation(out=o, in_=i_, func=AF.Exp, scale=s),
                           wait=[(ld_s, ld_t), (sdn, ab_free[ab])], inc=[(adn, 1)])
                    P.emit("vector", lambda e, o=bsl, i_=dus[:, gb * tc:(gb + 1) * tc],
                           bb=bcB[:, n * tc:(n + 1) * tc]:
                           e.tensor_tensor(out=o, in0=i_, in1=bb, op=ALU.mult),
                           wait=[(bca, bca_t), (ld_s, ld_t), (sdn, ab_free[ab])])
                    P.emit("vector", lambda e, o=hsl, a=asl, b=bsl, init=hcol[:, p:p + 1]:
                           e.tensor_tensor_scan(out=o, data0=a, data1=b, initial=init,
                                                op0=ALU.mult, op1=ALU.add),
                           wait=[(adn, P.cnt(adn))], inc=[(sdn, 1)])
                    P.wait("vector", sdn, P.cnt(sdn))
                    ab_free[ab] = P.cnt(sdn)
                    if dbg and c == 0 and g == 1 and n < 4:
                        for slot, src in ((4 * n + 0, asl), (4 * n + 1, bsl), (4 * n + 2, hsl),
                                          (4 * n + 3, bcB[:, n * tc:(n + 1) * tc])):
                            P.dma("gpsimd", ts["dbg"][slot * 128:slot * 128 + 128, 0:tc], src, sem=st)
                        P.wait("vector", st, P.cnt(st))
                        P.wait("scalar", st, P.cnt(st))
                    P.emit("vector", lambda e, o=hcol[:, p:p + 1], i_=hsl:
                           e.tensor_copy(out=o, in_=i_[:, tc - 1:tc]))
                    yb = g % 2
                    ysl = y_sb[:, yb * tc:(yb + 1) * tc]
                    P.emit("vector", lambda e, o=hc_sb[:], i_=hsl, cc=bcC[:, n * tc:(n + 1) * tc]:
                           e.tensor_tensor(out=o, in0=i_, in1=cc, op=ALU.mult),
                           wait=[(st, y_free[yb])] if n == 0 else ())
                    if n == 0:
                        P.emit("vector", lambda e, o=ysl, i_=hc_sb[:]:
                               e.tensor_copy(out=o, in_=i_))
                    else:
                        P.emit("vector", lambda e, o=ysl, i_=hc_sb[:]:
                               e.tensor_tensor(out=o, in0=o, in1=i_, op=ALU.add),
                               inc=[(ydn, 1)] if n == DS - 1 else ())
                yb = g % 2
                P.wait("gpsimd", ydn, P.cnt(ydn))
                P.dma("gpsimd", ts["y_scan"][g * 128:(g + 1) * 128, t0:t0 + tc],
                      y_sb[:, yb * tc:(yb + 1) * tc], sem=st)
                y_free[yb] = P.cnt(st)
                dt_free[gb] = P.cnt(sdn)
        P.wait("gpsimd", st, P.cnt(st))
        P.barrier()


def emit_l0_gate(P, ts, ctx, tc=512):
    nc = P.nc
    nblk = DI // 128
    nch = T // tc
    with ExitStack() as sctx:
        y_sb = sctx.enter_context(sbuf_u(nc, "gt_y", [128, 2 * tc], F32))
        u_sb = sctx.enter_context(sbuf_u(nc, "gt_u", [128, 2 * tc], F32))
        z_sb = sctx.enter_context(sbuf_u(nc, "gt_z", [128, 2 * tc], F32))
        sz_sb = sctx.enter_context(sbuf_u(nc, "gt_sz", [128, 2 * tc], F32))
        t_sb = sctx.enter_context(sbuf_u(nc, "gt_t", [128, 2 * tc], F32))
        o_sb = sctx.enter_context(sbuf_u(nc, "gt_o", [128, 2 * tc], BF16))
        dcol = sctx.enter_context(sbuf_u(nc, "gt_d", [128, nblk], F32))
        ld2 = [P.sem("gh0"), P.sem("gh1")]
        ldz2 = [P.sem("gd0"), P.sem("gd1")]
        ldc = P.sem("gh2")
        vd = P.sem("gc0")
        sd = P.sem("gc1")
        st = P.sem("gd2")
        _, gldc_t = P.dma("sync", dcol[:], ts["d_vec"], sem=ldc)
        slot_free = [0, 0]
        for g in range(nblk):
            for c in range(nch):
                t0 = c * tc
                sb = (g * nch + c) % 2
                sl = slice(sb * tc, (sb + 1) * tc)
                rows = slice(g * 128, (g + 1) * 128)
                P.wait("sync", st, slot_free[sb])
                P.dma("sync", y_sb[:, sl], ts["y_scan"][rows, t0:t0 + tc], sem=ld2[sb])
                P.dma("sync", u_sb[:, sl], ts["u_f32"][rows, t0:t0 + tc], sem=ld2[sb])
                P.wait("gpsimd", st, slot_free[sb])
                P.dma("gpsimd", z_sb[:, sl], ts["z_f32"][rows, t0:t0 + tc], sem=ldz2[sb])
                P.emit("scalar", lambda e, o=sz_sb[:, sl], i_=z_sb[:, sl]:
                       e.activation(out=o, in_=i_, func=AF.Silu),
                       wait=[(ldz2[sb], P.cnt(ldz2[sb]))], inc=[(sd, 1)])
                P.emit("vector", lambda e, o=t_sb[:, sl], u_=u_sb[:, sl], d=dcol[:, g:g + 1], y=y_sb[:, sl]:
                       e.scalar_tensor_tensor(out=o, in0=u_, scalar=d, in1=y, op0=ALU.mult, op1=ALU.add),
                       wait=[(ld2[sb], P.cnt(ld2[sb])), (ldc, gldc_t)])
                P.emit("vector", lambda e, o=o_sb[:, sl], a=t_sb[:, sl], b=sz_sb[:, sl]:
                       e.tensor_tensor(out=o, in0=a, in1=b, op=ALU.mult),
                       wait=[(sd, P.cnt(sd))], inc=[(vd, 1)])
                P.wait("gpsimd", vd, P.cnt(vd))
                P.dma("gpsimd", ts["yg_bf"][rows, t0:t0 + tc], o_sb[:, sl], sem=st)
                slot_free[sb] = P.cnt(st)
        P.wait("gpsimd", st, P.cnt(st))
        P.barrier()


def emit_rms_fm(P, ts, ctx, src, gain_key, eps, out_bf=None, out_f32=None,
                ndim=D, tc=512, tag="rms"):
    nc = P.nc
    nblk = ndim // 128
    nch = T // tc
    with ExitStack() as sctx:
        xc = sctx.enter_context(sbuf_u(nc, tag + "_xc", [128, nblk * tc], F32))
        sq = sctx.enter_context(sbuf_u(nc, tag + "_sq", [128, nblk * tc], BF16))
        rows = sctx.enter_context(sbuf_u(nc, tag + "_rows", [1, 2 * tc], F32))
        bcast = sctx.enter_context(sbuf_u(nc, tag + "_bc", [128, tc], F32))
        onesc = sctx.enter_context(sbuf_u(nc, tag + "_ones", [128, 1], BF16))
        onesr = sctx.enter_context(sbuf_u(nc, tag + "_onesr", [1, 128], F32))
        gcol = sctx.enter_context(sbuf_u(nc, tag + "_g", [128, nblk], F32))
        obf = sctx.enter_context(sbuf_u(nc, tag + "_obf", [128, nblk * tc], BF16))
        of32 = sctx.enter_context(sbuf_u(nc, tag + "_of", [128, nblk * tc], F32))
        ps_s = sctx.enter_context(psum_u(nc, tag + "_pss", [1, tc], F32))
        ps_bc = sctx.enter_context(psum_u(nc, tag + "_psb", [128, tc], F32))
        ldc = P.sem("gh0")
        ld = P.sem("gh1")
        ca = P.sem("gc0")
        mmA = P.sem("gc1")
        rowd = P.sem("gc2")
        bcd = P.sem("gc3")
        bce = P.sem("gc4")
        app = P.sem("gc5")
        st = P.sem("gd0")
        ld_base = P.cnt(ld)
        P.dma("sync", onesc[:], ts["c_ones_col_bf"], sem=ldc)
        P.dma("sync", onesr[:], ts["c_ones_row"], sem=ldc)
        _, ldc_t = P.dma("sync", gcol[:], ts[gain_key], sem=ldc)
        for c in range(nch):
            t0 = c * tc
            for g in range(nblk):
                P.dma("sync", xc[:, g * tc:(g + 1) * tc],
                      src[g * 128:(g + 1) * 128, t0:t0 + tc], sem=ld)
            ld_tgt = ld_base + (c + 1) * nblk * 16
            for g in range(nblk):
                sl = slice(g * tc, (g + 1) * tc)
                P.emit("scalar", lambda e, a=sq[:, sl], b=xc[:, sl]:
                       e.activation(out=a, in_=b, func=AF.Square),
                       wait=[(ld, ld_tgt)] if g == 0 else (), inc=[(ca, 1)])
            for g in range(nblk):
                sl = slice(g * tc, (g + 1) * tc)
                P.emit("tensor", lambda e, r=sq[:, sl], o=ps_s[:], s=(g == 0), p=(g == nblk - 1):
                       e.matmul(out=o, lhsT=onesc[:], rhs=r, start=s, stop=p),
                       wait=[(ca, P.cnt(ca)), (ldc, ldc_t)] if g == 0 else (),
                       inc=[(mmA, 1)] if g == nblk - 1 else ())
            ms = rows[:, 0:tc]; rstd = rows[:, tc:2 * tc]
            P.emit("vector", lambda e, o=ms, i_=ps_s[:]:
                   e.tensor_scalar(out=o, in0=i_, scalar1=1.0 / ndim, scalar2=None, op0=ALU.mult),
                   wait=[(mmA, P.cnt(mmA))], inc=[(rowd, 1)])
            P.emit("scalar", lambda e, o=rstd, i_=ms: e.activation(out=o, in_=i_, func=AF.Ln, bias=eps),
                   wait=[(rowd, P.cnt(rowd))])
            P.emit("scalar", lambda e, o=rstd, i_=rstd: e.activation(out=o, in_=i_, func=AF.Exp, scale=-0.5),
                   inc=[(rowd, 1)])
            P.emit("tensor", lambda e, o=ps_bc[:], r=rstd:
                   e.matmul(out=o, lhsT=onesr[:], rhs=r, start=True, stop=True),
                   wait=[(rowd, P.cnt(rowd))], inc=[(bcd, 1)])
            P.emit("scalar", lambda e, o=bcast[:], i_=ps_bc[:]: e.copy(out=o, in_=i_),
                   wait=[(bcd, P.cnt(bcd))], inc=[(bce, 1)])
            for g in range(nblk):
                sl = slice(g * tc, (g + 1) * tc)
                P.emit("vector", lambda e, o=of32[:, sl], i_=xc[:, sl], gc=gcol[:, g:g + 1], rb=bcast[:]:
                       e.scalar_tensor_tensor(out=o, in0=i_, scalar=gc, in1=rb,
                                              op0=ALU.mult, op1=ALU.mult),
                       wait=[(bce, P.cnt(bce))] if g == 0 else (), inc=[(app, 1)])
                if out_bf is not None:
                    P.emit("scalar", lambda e, o=obf[:, sl], i_=of32[:, sl]:
                           e.copy(out=o, in_=i_),
                           wait=[(app, P.cnt(app))], inc=[(app, 1)])
            app_now = P.cnt(app)
            P.wait("gpsimd", app, app_now)
            for g in range(nblk):
                sl = slice(g * tc, (g + 1) * tc)
                if out_f32 is not None:
                    P.dma("gpsimd", out_f32[g * 128:(g + 1) * 128, t0:t0 + tc], of32[:, sl], sem=st)
                if out_bf is not None:
                    P.dma("gpsimd", out_bf[g * 128:(g + 1) * 128, t0:t0 + tc], obf[:, sl], sem=st)
            P.wait("sync", st, P.cnt(st))
        P.wait("gpsimd", st, P.cnt(st))
        P.barrier()


def emit_l0_router(P, ts, ctx, tc=512, dbg=False):
    nc = P.nc
    nk = D // 128
    nch = T // tc
    E = 32
    with ExitStack() as sctx:
        w_sb = sctx.enter_context(sbuf_u(nc, "r0_w", [128, nk * E], F32))
        rhs = sctx.enter_context(sbuf_u(nc, "r0_rhs", [128, nk * tc], F32))
        sco = sctx.enter_context(sbuf_u(nc, "r0_sco", [E, tc], F32))
        ident = sctx.enter_context(sbuf_u(nc, "r0_id", [128, 128], F32))
        gbrow = sctx.enter_context(sbuf_u(nc, "r0_gbr", [1, E], F32))
        gb_bc = sctx.enter_context(sbuf_u(nc, "r0_gbc", [128, E], F32))
        onesr = sctx.enter_context(sbuf_u(nc, "r0_or", [1, 128], F32))
        tm = sctx.enter_context(sbuf_u(nc, "r0_tm", [128, 16 * E], F32))
        cfm = sctx.enter_context(sbuf_u(nc, "r0_cfm", [E, tc], F32))
        ps_l = sctx.enter_context(psum_u(nc, "r0_psl", [E, tc], F32))
        ps_t = sctx.enter_context(psum_u(nc, "r0_pst", [128, E], F32))
        ps_c = sctx.enter_context(psum_u(nc, "r0_psc", [E, 128], F32))
        ldc = P.sem("gh0")
        lr = P.sem("gh1")
        mmL = P.sem("gc0")
        sgd = P.sem("gc1")
        trd = P.sem("gc2")
        tmd = P.sem("gc3")
        cbk = P.sem("gc4")
        st = P.sem("gd0")

        P.dma("sync", w_sb[:].rearrange("p (c i) -> p c i", c=nk),
              ts["w_l0gate"].rearrange("(c p) i -> p c i", p=128), sem=ldc)
        P.dma("sync", ident[:], ts["c_ident"], sem=ldc)
        P.dma("sync", gbrow[:], ts["b_l0gate"], sem=ldc)
        _, ldc_t = P.dma("sync", onesr[:], ts["c_ones_row"], sem=ldc)
        P.emit("tensor", lambda e: e.matmul(out=ps_t[:], lhsT=onesr[:], rhs=gbrow[:],
                                            start=True, stop=True),
               wait=[(ldc, ldc_t)], inc=[(trd, 1)])
        P.emit("scalar", lambda e: e.copy(out=gb_bc[:], in_=ps_t[:]),
               wait=[(trd, P.cnt(trd))], inc=[(sgd, 1)])
        P.barrier()

        for c in range(nch):
            t0 = c * tc
            P.wait("sync", mmL, P.cnt(mmL))
            P.dma("sync", rhs[:].rearrange("p (c t) -> p c t", c=nk),
                  ts["xn_f32"][:, t0:t0 + tc].rearrange("(c p) t -> p c t", p=128), sem=lr)
            lr_t = P.cnt(lr)
            P.wait("tensor", sgd, P.cnt(sgd))
            for k in range(nk):
                P.emit("tensor", lambda e, lt=w_sb[:, k * E:(k + 1) * E], rt=rhs[:, k * tc:(k + 1) * tc],
                       s=(k == 0), p_=(k == nk - 1):
                       e.matmul(out=ps_l[:], lhsT=lt, rhs=rt, start=s, stop=p_),
                       wait=[(lr, lr_t)] if k == 0 else (),
                       inc=[(mmL, 1)] if k == nk - 1 else ())
            P.emit("scalar", lambda e: e.activation(out=sco[:], in_=ps_l[:], func=AF.Sigmoid),
                   wait=[(mmL, P.cnt(mmL))], inc=[(sgd, 1)])
            sgd_t = P.cnt(sgd)
            for j in range(tc // 128):
                P.emit("tensor", lambda e, i_=sco[:, j * 128:(j + 1) * 128]:
                       e.transpose(out=ps_t[:], in_=i_, identity=ident[0:E, 0:E]),
                       wait=[(sgd, sgd_t), (tmd, P.cnt(tmd))], inc=[(trd, 1)])
                sc_tm = tm[:, 0:E]
                scb = tm[:, E:2 * E]
                m1 = tm[:, 2 * E:2 * E + 8]
                mask = tm[:, 3 * E:4 * E]
                grp2 = tm[:, 4 * E:5 * E]
                m2 = tm[:, 5 * E:5 * E + 8]
                gsc = tm[:, 6 * E:6 * E + 8]
                v8 = tm[:, 7 * E:7 * E + 8]
                gmask = tm[:, 8 * E:8 * E + 8]
                masked = tm[:, 9 * E:10 * E]
                v8b = tm[:, 10 * E:10 * E + 8]
                sel_ = tm[:, 11 * E:12 * E]
                tkw = tm[:, 12 * E:13 * E]
                den = tm[:, 13 * E:13 * E + 1]
                rden = tm[:, 13 * E + 1:13 * E + 2]
                compb = tm[:, 14 * E:15 * E]
                P.emit("scalar", lambda e, o=sc_tm: e.copy(out=o, in_=ps_t[:]),
                       wait=[(trd, P.cnt(trd))], inc=[(sgd, 1)])
                vg = P.sem("gc5")

                def v(fn, wait=(), inc=(), g_=True):
                    inc = list(inc)
                    if g_ and not inc:
                        inc = [(vg, 1)]
                    P.emit("vector", fn, wait=wait, inc=inc)
                    if g_:
                        sname, _ = inc[0]
                        P.wait("vector", sname, P.cnt(sname))
                v(lambda e, o=scb, a=sc_tm, b=gb_bc[:]: e.tensor_tensor(out=o, in0=a, in1=b, op=ALU.add),
                  wait=[(sgd, P.cnt(sgd))])
                v(lambda e, o=m1.rearrange("p (g o) -> p g o", o=1),
                  i_=scb.rearrange("p (g o) -> p g o", o=4):
                  e.tensor_reduce(out=o, in_=i_, axis=AX.X, op=ALU.max), g_=True)
                for o_ in range(4):
                    v(lambda e, o=mask.rearrange("p (g o) -> p g o", o=4)[:, :, o_],
                      a=scb.rearrange("p (g o) -> p g o", o=4)[:, :, o_], b=m1:
                      e.tensor_tensor(out=o, in0=a, in1=b, op=ALU.is_equal))
                v(lambda e, o=grp2, a=mask, b=scb:
                  e.scalar_tensor_tensor(out=o, in0=a, scalar=-1e30, in1=b, op0=ALU.mult, op1=ALU.add))
                v(lambda e, o=m2.rearrange("p (g o) -> p g o", o=1),
                  i_=grp2.rearrange("p (g o) -> p g o", o=4):
                  e.tensor_reduce(out=o, in_=i_, axis=AX.X, op=ALU.max), g_=True)
                v(lambda e, o=gsc, a=m1, b=m2: e.tensor_tensor(out=o, in0=a, in1=b, op=ALU.add))
                v(lambda e, o=v8, i_=gsc: e.max(out=o, in_=i_), g_=True)
                v(lambda e, o=gmask, a=gsc, th=v8[:, 3:4]:
                  e.tensor_scalar(out=o, in0=a, scalar1=th, scalar2=None, op0=ALU.is_ge))
                for o_ in range(4):
                    v(lambda e, o=masked.rearrange("p (g o) -> p g o", o=4)[:, :, o_],
                      a=scb.rearrange("p (g o) -> p g o", o=4)[:, :, o_], b=gmask:
                      e.tensor_tensor(out=o, in0=a, in1=b, op=ALU.mult))
                v(lambda e, o=v8b, i_=masked: e.max(out=o, in_=i_), g_=True)
                v(lambda e, o=sel_, a=masked, th=v8b[:, 7:8]:
                  e.tensor_scalar(out=o, in0=a, scalar1=th, scalar2=None, op0=ALU.is_ge))
                v(lambda e, o=tkw, a=sc_tm, b=sel_: e.tensor_tensor(out=o, in0=a, in1=b, op=ALU.mult))
                v(lambda e, o=den, i_=tkw.rearrange("p (g o) -> p g o", g=1):
                  e.tensor_reduce(out=o.rearrange("p (g o) -> p g o", g=1), in_=i_, axis=AX.X, op=ALU.add), g_=True)
                v(lambda e, o=den, i_=den:
                  e.tensor_scalar(out=o, in0=i_, scalar1=1e-20, scalar2=None, op0=ALU.add))
                v(lambda e, o=rden, i_=den: e.reciprocal(out=o, in_=i_), g_=True)
                v(lambda e, o=compb, a=tkw, r=rden:
                  e.tensor_scalar(out=o, in0=a, scalar1=r, scalar2=2.5, op0=ALU.mult, op1=ALU.mult),
                  inc=[(tmd, 1)])
                if dbg and c == 0 and j == 0:
                    P.wait("gpsimd", tmd, P.cnt(tmd))
                    P.dma("gpsimd", ts["dbg"][0:128, 0:16 * E], tm[:], sem=st)
                    P.wait("vector", st, P.cnt(st))
                    P.wait("tensor", st, P.cnt(st))
                    P.wait("scalar", st, P.cnt(st))
                P.emit("tensor", lambda e, i_=compb:
                       e.transpose(out=ps_c[:], in_=i_, identity=ident[:]),
                       wait=[(tmd, P.cnt(tmd)), (cbk, P.cnt(cbk))], inc=[(trd, 1)])
                P.emit("scalar", lambda e, o=cfm[:, j * 128:(j + 1) * 128]:
                       e.copy(out=o, in_=ps_c[:]),
                       wait=[(trd, P.cnt(trd))], inc=[(cbk, 1)])
            P.wait("gpsimd", cbk, P.cnt(cbk))
            P.dma("gpsimd", ts["comb"][:, t0:t0 + tc], cfm[:], sem=st)
            P.wait("sync", st, P.cnt(st))
        P.wait("gpsimd", st, P.cnt(st))
        P.barrier()


def emit_l0_moe(P, ts, ctx, tc=512, G=8):
    nc = P.nc
    nk = D // 128
    E = 32
    IT = 4
    nch = T // tc
    ngrp = E // G
    with ExitStack() as sctx:
        rhs = sctx.enter_context(sbuf_u(nc, "m0_rhs", [128, nk * tc], BF16))
        wga = [sctx.enter_context(sbuf_u(nc, f"m0_wg{b}", [128, nk * 512], BF16)) for b in range(2)]
        wua = [sctx.enter_context(sbuf_u(nc, f"m0_wu{b}", [128, nk * 512], BF16)) for b in range(2)]
        wda = [sctx.enter_context(sbuf_u(nc, f"m0_wd{b}", [128, G * 4 * 128], BF16)) for b in range(2)]
        wsh = sctx.enter_context(sbuf_u(nc, "m0_wsh", [128, nk * 1024], BF16))
        wsd = [sctx.enter_context(sbuf_u(nc, f"m0_wsd{b}", [128, 8 * 128], BF16)) for b in range(2)]
        comb_sb = sctx.enter_context(sbuf_u(nc, "m0_comb", [E, tc], F32))
        sel = sctx.enter_context(sbuf_u(nc, "m0_sel", [E, E * 128], F32))
        cbc = sctx.enter_context(sbuf_u(nc, "m0_cbc", [128, tc], F32))
        silu_sb = sctx.enter_context(sbuf_u(nc, "m0_silu", [128, tc], F32))
        t1_sb = sctx.enter_context(sbuf_u(nc, "m0_t1", [128, tc], F32))
        prod = sctx.enter_context(sbuf_u(nc, "m0_prod", [128, (G * IT + 8) * tc], BF16))
        y_sb = sctx.enter_context(sbuf_u(nc, "m0_y", [128, 8 * tc], F32))
        psg = sctx.enter_context(psum_u(nc, "m0_psg", [128, tc], F32))
        psu = sctx.enter_context(psum_u(nc, "m0_psu", [128, tc], F32))
        psb = sctx.enter_context(psum_u(nc, "m0_psb", [128, tc], F32))
        psy = [sctx.enter_context(psum_u(nc, f"m0_psy{b}", [128, tc], F32)) for b in range(2)]
        ldc = P.sem("gh0")
        lr = P.sem("gh1")
        lcomb = P.sem("gh2")
        lwg = [P.sem("gd0"), P.sem("gd1")]
        lwd = [P.sem("gd2"), P.sem("gd3")]
        lwsh = P.sem("gd4")
        mmA = P.sem("gc0")
        mmB = P.sem("gc1")
        evA = P.sem("gc2")
        pvd = P.sem("gc3")
        mmD = P.sem("gc4")
        acc = P.sem("gc5")
        st = P.sem("gd5")

        _, sel_t = P.dma("sync", sel[:], ts["c_sel32"], sem=ldc)
        wbuf_free = [P.cnt(pvd)] * 2
        wdbuf_free = [P.cnt(mmD)] * 2
        shbuf_free = P.cnt(mmD)
        psy_free = [P.cnt(acc)] * 2
        prod_free = P.cnt(mmD)
        cbc_guard = P.cnt(pvd)
        y_free = P.cnt(st)
        for c in range(nch):
            t0 = c * tc
            P.wait("sync", mmA, P.cnt(mmA))
            P.dma("sync", rhs[:].rearrange("p (c t) -> p c t", c=nk),
                  ts["xn_bf"][:, t0:t0 + tc].rearrange("(c p) t -> p c t", p=128), sem=lr)
            lr_t = P.cnt(lr)
            P.wait("sync", mmB, P.cnt(mmB))
            P.dma("sync", comb_sb[:], ts["comb"][:, t0:t0 + tc], sem=lcomb)
            lcomb_t = P.cnt(lcomb)
            for grp in range(ngrp):
                for ei in range(G):
                    e_ = grp * G + ei
                    wb = ei % 2
                    P.wait("gpsimd", pvd, wbuf_free[wb])
                    P.dma("gpsimd", wga[wb][:], ts["w_l0egp"][e_], sem=lwg[wb])
                    P.dma("gpsimd", wua[wb][:], ts["w_l0eup"][e_], sem=lwg[wb])
                    lw_t = P.cnt(lwg[wb])
                    P.emit("tensor", lambda e, lt=sel[:, e_ * 128:(e_ + 1) * 128]:
                           e.matmul(out=psb[:], lhsT=lt, rhs=comb_sb[:], start=True, stop=True),
                           wait=[(lcomb, lcomb_t), (ldc, sel_t), (evA, P.cnt(evA))], inc=[(mmB, 1)])
                    P.emit("scalar", lambda e: e.copy(out=cbc[:], in_=psb[:]),
                           wait=[(mmB, P.cnt(mmB)), (pvd, cbc_guard)], inc=[(evA, 1)])
                    for it in range(IT):
                        pslot = ei * IT + it
                        for k in range(nk):
                            P.emit("tensor", lambda e, lt=wga[wb][:, k * 512 + it * 128:k * 512 + (it + 1) * 128],
                                   rt=rhs[:, k * tc:(k + 1) * tc], s=(k == 0), p_=(k == nk - 1):
                                   e.matmul(out=psg[:], lhsT=lt, rhs=rt, start=s, stop=p_),
                                   wait=[(lwg[wb], lw_t), (lr, lr_t), (evA, P.cnt(evA))] if k == 0 else ())
                        for k in range(nk):
                            P.emit("tensor", lambda e, lt=wua[wb][:, k * 512 + it * 128:k * 512 + (it + 1) * 128],
                                   rt=rhs[:, k * tc:(k + 1) * tc], s=(k == 0), p_=(k == nk - 1):
                                   e.matmul(out=psu[:], lhsT=lt, rhs=rt, start=s, stop=p_),
                                   wait=[(lr, lr_t), (pvd, P.cnt(pvd))] if k == 0 else (),
                                   inc=[(mmA, 1)] if k == nk - 1 else ())
                        P.emit("scalar", lambda e: e.activation(out=silu_sb[:], in_=psg[:], func=AF.Silu),
                               wait=[(mmA, P.cnt(mmA)), (pvd, P.cnt(pvd))], inc=[(evA, 1)])
                        P.emit("vector", lambda e: e.tensor_tensor(out=t1_sb[:], in0=silu_sb[:], in1=cbc[:], op=ALU.mult),
                               wait=[(evA, P.cnt(evA)), (mmD, prod_free)])
                        P.emit("vector", lambda e, o=prod[:, pslot * tc:(pslot + 1) * tc]:
                               e.tensor_tensor(out=o, in0=t1_sb[:], in1=psu[:], op=ALU.mult),
                               inc=[(pvd, 1)])
                    wbuf_free[wb] = P.cnt(pvd)
                    cbc_guard = P.cnt(pvd)
                if grp == 0:
                    P.wait("gpsimd", pvd, wbuf_free[0])
                    P.wait("gpsimd", pvd, wbuf_free[1])
                    P.dma("gpsimd", wsh[:], ts["w_l0sgp"], sem=lwsh)
                    P.dma("gpsimd", wua[0][:], ts["w_l0sup0"], sem=lwsh)
                    P.dma("gpsimd", wua[1][:], ts["w_l0sup1"], sem=lwsh)
                    lwsh_t = P.cnt(lwsh)
                    for it in range(8):
                        pslot = G * IT + it
                        wu_t = wua[0] if it < 4 else wua[1]
                        ito = it % 4
                        for k in range(nk):
                            P.emit("tensor", lambda e, lt=wsh[:, k * 1024 + it * 128:k * 1024 + (it + 1) * 128],
                                   rt=rhs[:, k * tc:(k + 1) * tc], s=(k == 0), p_=(k == nk - 1):
                                   e.matmul(out=psg[:], lhsT=lt, rhs=rt, start=s, stop=p_),
                                   wait=[(lwsh, lwsh_t), (lr, lr_t), (evA, P.cnt(evA))] if k == 0 else ())
                        for k in range(nk):
                            P.emit("tensor", lambda e, lt=wu_t[:, k * 512 + ito * 128:k * 512 + (ito + 1) * 128],
                                   rt=rhs[:, k * tc:(k + 1) * tc], s=(k == 0), p_=(k == nk - 1):
                                   e.matmul(out=psu[:], lhsT=lt, rhs=rt, start=s, stop=p_),
                                   wait=[(pvd, P.cnt(pvd))] if k == 0 else (),
                                   inc=[(mmA, 1)] if k == nk - 1 else ())
                        P.emit("scalar", lambda e: e.activation(out=silu_sb[:], in_=psg[:], func=AF.Silu),
                               wait=[(mmA, P.cnt(mmA)), (pvd, P.cnt(pvd))], inc=[(evA, 1)])
                        P.emit("vector", lambda e, o=prod[:, pslot * tc:(pslot + 1) * tc]:
                               e.tensor_tensor(out=o, in0=silu_sb[:], in1=psu[:], op=ALU.mult),
                               wait=[(evA, P.cnt(evA)), (mmD, prod_free)], inc=[(pvd, 1)])
                    wbuf_free[0] = P.cnt(pvd)
                    wbuf_free[1] = P.cnt(pvd)
                pvd_t = P.cnt(pvd)
                for dt in range(8):
                    wb = dt % 2
                    pb = dt % 2
                    P.wait("gpsimd", mmD, wdbuf_free[wb])
                    P.dma("gpsimd", wda[wb][:], ts["w_l0edp"][dt, grp], sem=lwd[wb])
                    lwd_t = P.cnt(lwd[wb])
                    nsh = 0
                    if grp == 0:
                        P.dma("gpsimd", wsd[wb][:], ts["w_l0sdp"][dt], sem=lwd[wb])
                        lwd_t = P.cnt(lwd[wb])
                        nsh = 8
                    nmm = G * 4 + nsh
                    q = 0
                    P.wait("tensor", acc, psy_free[pb])
                    for ei in range(G):
                        for k in range(4):
                            P.emit("tensor", lambda e, o=psy[pb][:],
                                   lt=wda[wb][:, (ei * 4 + k) * 128:(ei * 4 + k + 1) * 128],
                                   rt=prod[:, (ei * IT + k) * tc:(ei * IT + k + 1) * tc],
                                   s=(q == 0), p_=(q == nmm - 1):
                                   e.matmul(out=o, lhsT=lt, rhs=rt, start=s, stop=p_),
                                   wait=[(lwd[wb], lwd_t), (pvd, pvd_t)] if q == 0 else (),
                                   inc=[(mmD, 1)] if q == nmm - 1 else ())
                            q += 1
                    if grp == 0:
                        for k in range(8):
                            P.emit("tensor", lambda e, o=psy[pb][:],
                                   lt=wsd[wb][:, k * 128:(k + 1) * 128],
                                   rt=prod[:, (G * IT + k) * tc:(G * IT + k + 1) * tc],
                                   s=False, p_=(q == nmm - 1):
                                   e.matmul(out=o, lhsT=lt, rhs=rt, start=s, stop=p_),
                                   inc=[(mmD, 1)] if q == nmm - 1 else ())
                            q += 1
                    ysl = y_sb[:, dt * tc:(dt + 1) * tc]
                    if grp == 0:
                        P.emit("vector", lambda e, o=ysl, p_=psy[pb][:]: e.tensor_copy(out=o, in_=p_),
                               wait=[(mmD, P.cnt(mmD)), (st, y_free)], inc=[(acc, 1)])
                    else:
                        P.emit("vector", lambda e, o=ysl, p_=psy[pb][:]: e.tensor_tensor(out=o, in0=o, in1=p_, op=ALU.add),
                               wait=[(mmD, P.cnt(mmD))], inc=[(acc, 1)])
                    psy_free[pb] = P.cnt(acc)
                    wdbuf_free[wb] = P.cnt(mmD)
                prod_free = P.cnt(mmD)
            P.wait("gpsimd", acc, P.cnt(acc))
            for dt in range(8):
                P.dma("gpsimd", ts["y0ffn"][dt * 128:(dt + 1) * 128, t0:t0 + tc],
                      y_sb[:, dt * tc:(dt + 1) * tc], sem=st)
            y_free = P.cnt(st)
        P.wait("gpsimd", st, P.cnt(st))
        P.barrier()


def emit_ln_sb(P, ts, ctx, src, scol_ap, bcol_ap, out_bf=None, out_f32=None,
               eps=1e-5, tc=512):
    nc = P.nc
    nblk = D // 128
    nch = T // tc
    with ExitStack() as sctx:
        xc = sctx.enter_context(sbuf_u(nc, "lb_xc", [128, nblk * tc], F32))
        xb = sctx.enter_context(sbuf_u(nc, "lb_xb", [128, nblk * tc], BF16))
        sq = sctx.enter_context(sbuf_u(nc, "lb_sq", [128, nblk * tc], BF16))
        rows = sctx.enter_context(sbuf_u(nc, "lb_rows", [1, 6 * tc], F32))
        bcast = sctx.enter_context(sbuf_u(nc, "lb_bc", [128, 2 * tc], F32))
        onesc = sctx.enter_context(sbuf_u(nc, "lb_ones", [128, 1], BF16))
        onesr = sctx.enter_context(sbuf_u(nc, "lb_onesr", [1, 128], F32))
        obf = sctx.enter_context(sbuf_u(nc, "lb_obf", [128, nblk * tc], BF16))
        of32 = sctx.enter_context(sbuf_u(nc, "lb_of", [128, nblk * tc], F32))
        scol = sctx.enter_context(sbuf_u(nc, "lb_scol", [128, nblk], F32))
        bcol2 = sctx.enter_context(sbuf_u(nc, "lb_bcol", [128, nblk], F32))
        ps_s = sctx.enter_context(psum_u(nc, "lb_pss", [1, tc], F32))
        ps_s2 = sctx.enter_context(psum_u(nc, "lb_pss2", [1, tc], F32))
        ps_bc = sctx.enter_context(psum_u(nc, "lb_psb", [128, 2 * tc], F32))
        ldc = P.sem("gh0")
        ld = P.sem("gh1")
        ca = P.sem("gc0")
        mmA = P.sem("gc1")
        rowd = P.sem("gc2")
        bcd = P.sem("gc3")
        bce = P.sem("gc4")
        app = P.sem("gc5")
        st = P.sem("gd0")
        ld_base = P.cnt(ld)
        P.dma("sync", onesc[:], ts["c_ones_col_bf"], sem=ldc)
        P.dma("sync", scol[:], scol_ap, sem=ldc)
        P.dma("sync", bcol2[:], bcol_ap, sem=ldc)
        _, ldc_t = P.dma("sync", onesr[:], ts["c_ones_row"], sem=ldc)
        for c in range(nch):
            t0 = c * tc
            for g in range(nblk):
                P.dma("sync", xc[:, g * tc:(g + 1) * tc],
                      src[g * 128:(g + 1) * 128, t0:t0 + tc], sem=ld)
            ld_tgt = ld_base + (c + 1) * nblk * 16
            for g in range(nblk):
                sl = slice(g * tc, (g + 1) * tc)
                P.emit("vector", lambda e, a=xb[:, sl], b=xc[:, sl]: e.tensor_copy(out=a, in_=b),
                       wait=[(ld, ld_tgt), (ldc, ldc_t)] if g == 0 else (), inc=[(ca, 1)])
                P.emit("scalar", lambda e, a=sq[:, sl], b=xc[:, sl]:
                       e.activation(out=a, in_=b, func=AF.Square),
                       wait=[(ld, ld_tgt)] if g == 0 else (), inc=[(ca, 1)])
            for g in range(nblk):
                sl = slice(g * tc, (g + 1) * tc)
                P.emit("tensor", lambda e, r=xb[:, sl], o=ps_s[:], s=(g == 0), p=(g == nblk - 1):
                       e.matmul(out=o, lhsT=onesc[:], rhs=r, start=s, stop=p),
                       wait=[(ca, P.cnt(ca)), (ldc, ldc_t)] if g == 0 else ())
            for g in range(nblk):
                sl = slice(g * tc, (g + 1) * tc)
                P.emit("tensor", lambda e, r=sq[:, sl], o=ps_s2[:], s=(g == 0), p=(g == nblk - 1):
                       e.matmul(out=o, lhsT=onesc[:], rhs=r, start=s, stop=p),
                       inc=[(mmA, 1)] if g == nblk - 1 else ())
            mrow = rows[:, 0:tc]; ex2 = rows[:, tc:2 * tc]
            var = rows[:, 2 * tc:3 * tc]; rstd = rows[:, 3 * tc:4 * tc]
            brow = rows[:, 4 * tc:5 * tc]; tmp = rows[:, 5 * tc:6 * tc]
            P.emit("vector", lambda e, o=mrow, i=ps_s[:]:
                   e.tensor_scalar(out=o, in0=i, scalar1=1.0 / D, scalar2=None, op0=ALU.mult),
                   wait=[(mmA, P.cnt(mmA))])
            P.emit("vector", lambda e, o=ex2, i=ps_s2[:]:
                   e.tensor_scalar(out=o, in0=i, scalar1=1.0 / D, scalar2=None, op0=ALU.mult))
            P.emit("vector", lambda e, o=tmp, i=mrow: e.tensor_tensor(out=o, in0=i, in1=i, op=ALU.mult))
            P.emit("vector", lambda e, o=var, i=ex2, j=tmp: e.tensor_tensor(out=o, in0=i, in1=j, op=ALU.subtract),
                   inc=[(rowd, 1)])
            P.emit("scalar", lambda e, o=tmp, i=var: e.activation(out=o, in_=i, func=AF.Ln, bias=eps),
                   wait=[(rowd, P.cnt(rowd))])
            P.emit("scalar", lambda e, o=rstd, i=tmp: e.activation(out=o, in_=i, func=AF.Exp, scale=-0.5),
                   inc=[(rowd, 1)])
            P.emit("vector", lambda e, o=brow, i=mrow, j=rstd:
                   e.scalar_tensor_tensor(out=o, in0=i, scalar=-1.0, in1=j, op0=ALU.mult, op1=ALU.mult),
                   wait=[(rowd, P.cnt(rowd))], inc=[(rowd, 1)])
            P.emit("tensor", lambda e, o=ps_bc[:, 0:tc], r=rstd:
                   e.matmul(out=o, lhsT=onesr[:], rhs=r, start=True, stop=True),
                   wait=[(rowd, P.cnt(rowd))])
            P.emit("tensor", lambda e, o=ps_bc[:, tc:2 * tc], r=brow:
                   e.matmul(out=o, lhsT=onesr[:], rhs=r, start=True, stop=True),
                   inc=[(bcd, 1)])
            P.emit("scalar", lambda e, o=bcast[:, 0:tc], i=ps_bc[:, 0:tc]: e.copy(out=o, in_=i),
                   wait=[(bcd, P.cnt(bcd))])
            P.emit("scalar", lambda e, o=bcast[:, tc:2 * tc], i=ps_bc[:, tc:2 * tc]: e.copy(out=o, in_=i),
                   inc=[(bce, 1)])
            for g in range(nblk):
                sl = slice(g * tc, (g + 1) * tc)
                P.emit("vector", lambda e, o=of32[:, sl], i=xc[:, sl], ab=bcast[:, 0:tc]:
                       e.tensor_tensor(out=o, in0=i, in1=ab, op=ALU.mult),
                       wait=[(bce, P.cnt(bce))] if g == 0 else ())
                P.emit("vector", lambda e, o=of32[:, sl], i=of32[:, sl], bb=bcast[:, tc:2 * tc]:
                       e.tensor_tensor(out=o, in0=i, in1=bb, op=ALU.add))
                P.emit("vector", lambda e, o=of32[:, sl], i=of32[:, sl],
                       sc=scol[:, g:g + 1], bc=bcol2[:, g:g + 1]:
                       e.tensor_scalar(out=o, in0=i, scalar1=sc, scalar2=bc, op0=ALU.mult, op1=ALU.add),
                       inc=[(app, 1)])
                if out_bf is not None:
                    P.emit("scalar", lambda e, o=obf[:, sl], i=of32[:, sl]:
                           e.copy(out=o, in_=i),
                           wait=[(app, P.cnt(app))], inc=[(app, 1)])
            app_now = P.cnt(app)
            P.wait("gpsimd", app, app_now)
            for g in range(nblk):
                sl = slice(g * tc, (g + 1) * tc)
                if out_f32 is not None:
                    P.dma("gpsimd", out_f32[g * 128:(g + 1) * 128, t0:t0 + tc], of32[:, sl], sem=st)
                if out_bf is not None:
                    P.dma("gpsimd", out_bf[g * 128:(g + 1) * 128, t0:t0 + tc], obf[:, sl], sem=st)
            P.wait("sync", st, P.cnt(st))
        P.wait("gpsimd", st, P.cnt(st))
        P.barrier()


def emit_combine(P, ts, ctx, x_in, s_in, y_in, x_out, tc=512):
    nc = P.nc
    nblk = D // 128
    nch = T // tc
    with ExitStack() as sctx:
        x_sb = sctx.enter_context(sbuf_u(nc, "cb_x", [128, 2 * tc], F32))
        s_sb = sctx.enter_context(sbuf_u(nc, "cb_s", [128, 2 * tc], F32))
        y_sb = sctx.enter_context(sbuf_u(nc, "cb_y", [128, 2 * tc], F32))
        sy_sb = sctx.enter_context(sbuf_u(nc, "cb_sy", [128, 2 * tc], F32))
        o_sb = sctx.enter_context(sbuf_u(nc, "cb_o", [128, 2 * tc], F32))
        ld2 = [P.sem("gh0"), P.sem("gh1")]
        ly2 = [P.sem("gd0"), P.sem("gd1")]
        sd = P.sem("gc0")
        vd = P.sem("gc1")
        st = P.sem("gd2")
        slot_free = [P.cnt(st)] * 2
        for g in range(nblk):
            for c in range(nch):
                t0 = c * tc
                sb = (g * nch + c) % 2
                sl = slice(sb * tc, (sb + 1) * tc)
                rows = slice(g * 128, (g + 1) * 128)
                P.wait("sync", st, slot_free[sb])
                P.dma("sync", x_sb[:, sl], x_in[rows, t0:t0 + tc], sem=ld2[sb])
                P.dma("sync", s_sb[:, sl], s_in[rows, t0:t0 + tc], sem=ld2[sb])
                P.wait("gpsimd", st, slot_free[sb])
                P.dma("gpsimd", y_sb[:, sl], y_in[rows, t0:t0 + tc], sem=ly2[sb])
                P.emit("scalar", lambda e, o=sy_sb[:, sl], i_=y_sb[:, sl]:
                       e.activation(out=o, in_=i_, func=AF.Silu),
                       wait=[(ly2[sb], P.cnt(ly2[sb]))], inc=[(sd, 1)])
                P.emit("vector", lambda e, o=sy_sb[:, sl], a=s_sb[:, sl]:
                       e.tensor_tensor(out=o, in0=o, in1=a, op=ALU.mult),
                       wait=[(sd, P.cnt(sd)), (ld2[sb], P.cnt(ld2[sb]))])
                P.emit("vector", lambda e, o=o_sb[:, sl], a=x_sb[:, sl], b=sy_sb[:, sl]:
                       e.tensor_tensor(out=o, in0=a, in1=b, op=ALU.add),
                       inc=[(vd, 1)])
                P.wait("gpsimd", vd, P.cnt(vd))
                P.dma("gpsimd", x_out[rows, t0:t0 + tc], o_sb[:, sl], sem=st)
                slot_free[sb] = P.cnt(st)
        P.wait("gpsimd", st, P.cnt(st))
        P.barrier()


def emit_l1_attn_layer(P, ts, ctx, l, tc=512):
    nc = P.nc
    H = 16
    DK = 64
    nblk = D // 128
    nch = T // tc
    ntile = T // 128
    with ExitStack() as sctx:
        wk_sb = sctx.enter_context(sbuf_u(nc, "la_wk", [128, nblk * D], BF16))
        wv_sb = sctx.enter_context(sbuf_u(nc, "la_wv", [128, nblk * D], BF16))
        wo_sb = sctx.enter_context(sbuf_u(nc, "la_wo", [128, nblk * D], BF16))
        bk_bc = sctx.enter_context(sbuf_u(nc, "la_bkbc", [128, D], F32))
        bv_bc = sctx.enter_context(sbuf_u(nc, "la_bvbc", [128, D], F32))
        brow = sctx.enter_context(sbuf_u(nc, "la_brow", [1, 2 * D], F32))
        onesr = sctx.enter_context(sbuf_u(nc, "la_onesr", [1, 128], F32))
        onescb = sctx.enter_context(sbuf_u(nc, "la_onescb", [128, 1], BF16))
        bocol = sctx.enter_context(sbuf_u(nc, "la_bo", [128, nblk], F32))
        zsel = sctx.enter_context(sbuf_u(nc, "la_zsel", [16, 8 * 128], F32))
        htile = sctx.enter_context(sbuf_u(nc, "la_h", [128, 2 * nblk * 128], BF16))
        ktmp = sctx.enter_context(sbuf_u(nc, "la_ktmp", [128, 512], F32))
        ktmp2 = sctx.enter_context(sbuf_u(nc, "la_ktmp2", [128, D], BF16))
        etmp = sctx.enter_context(sbuf_u(nc, "la_etmp", [128, 512], F32))
        kp_tm = sctx.enter_context(sbuf_u(nc, "la_kp", [128, D], BF16))
        v_tm = sctx.enter_context(sbuf_u(nc, "la_v", [128, D], BF16))
        kv_bf = sctx.enter_context(sbuf_u(nc, "la_kv", [128, 512], BF16))
        ks_col = sctx.enter_context(sbuf_u(nc, "la_ks", [128, 8], F32))
        ks_blk = sctx.enter_context(sbuf_u(nc, "la_ksb", [128, 8 * 16], BF16))
        qch = sctx.enter_context(sbuf_u(nc, "la_q", [128, nblk * tc], BF16))
        den_sb = sctx.enter_context(sbuf_u(nc, "la_den", [16, tc], F32))
        z_sb = sctx.enter_context(sbuf_u(nc, "la_z", [16, tc], F32))
        zbc_sb = sctx.enter_context(sbuf_u(nc, "la_zbc", [128, 2 * tc], F32))
        attn_sb = sctx.enter_context(sbuf_u(nc, "la_attn", [128, nblk * tc], BF16))
        xer = sctx.enter_context(sbuf_u(nc, "la_xer", [128, 2 * tc], F32))
        oo = sctx.enter_context(sbuf_u(nc, "la_oo", [128, 2 * tc], F32))
        psk = sctx.enter_context(psum_u(nc, "la_psk", [128, 512], F32))
        psv = sctx.enter_context(psum_u(nc, "la_psv", [128, 512], F32))
        psks = sctx.enter_context(psum_u(nc, "la_psks", [128, 8], F32))

        ldc = P.sem("gh0")
        ldcg = P.sem("gd0")
        lh2 = [P.sem("gh1"), P.sem("gh2")]
        lq = P.sem("gh3")
        lx2 = [P.sem("gh4"), P.sem("gh5")]
        mmk = P.sem("gc0")
        evk = P.sem("gc1")
        mkv = P.sem("gc2")
        bev = P.sem("gc3")
        mden = P.sem("gc4")
        mzb = P.sem("gc5")
        myy = P.sem("gc6")
        evy = P.sem("gc7")
        mo = P.sem("gc8")
        evo = P.sem("gc9")
        st = P.sem("gd1")

        P.dma("gpsimd", wk_sb[:], ts["w_Wk_p"][l], sem=ldcg)
        P.dma("gpsimd", wv_sb[:], ts["w_Wv_p"][l], sem=ldcg)
        _, ldcg_t = P.dma("gpsimd", wo_sb[:], ts["w_Wo_p"][l], sem=ldcg)
        P.dma("sync", brow[:, 0:D], ts["p_bk"][l:l + 1, :], sem=ldc)
        P.dma("sync", brow[:, D:2 * D], ts["p_bv"][l:l + 1, :], sem=ldc)
        P.dma("sync", onesr[:], ts["c_ones_row"], sem=ldc)
        P.dma("sync", onescb[:], ts["c_ones_col_bf"], sem=ldc)
        P.dma("sync", bocol[:], ts["bo_cols"][:, l * nblk:(l + 1) * nblk], sem=ldc)
        _, ldc_t = P.dma("sync", zsel[:], ts["c_zsel"], sem=ldc)
        with ExitStack() as bctx:
            psboot = bctx.enter_context(psum_u(nc, "la_psboot", [128, 512], F32))
            for half in range(2):
                for which in range(2):
                    P.emit("tensor", lambda e, o=psboot[:], r=brow[:, which * D + half * 512: which * D + (half + 1) * 512]:
                           e.matmul(out=o, lhsT=onesr[:], rhs=r, start=True, stop=True),
                           wait=[(ldc, ldc_t), (bev, P.cnt(bev))], inc=[(mzb, 1)])
                    dst = bk_bc if which == 0 else bv_bc
                    P.emit("scalar", lambda e, o=dst[:, half * 512:(half + 1) * 512], i_=psboot[:]:
                           e.copy(out=o, in_=i_),
                           wait=[(mzb, P.cnt(mzb))], inc=[(bev, 1)])
            bev_boot = P.cnt(bev)
            P.barrier()

        with ExitStack() as pctx:
            psk = pctx.enter_context(psum_u(nc, "la_psk", [128, 512], F32))
            psv = pctx.enter_context(psum_u(nc, "la_psv", [128, 512], F32))
            psks = pctx.enter_context(psum_u(nc, "la_psks", [128, 8], F32))
            h_free = [P.cnt(mmk)] * 2
            kv_st = P.cnt(st)
            for j in range(ntile):
                hb = j % 2
                hsl = slice(hb * nblk * 128, (hb + 1) * nblk * 128)
                P.wait("sync", mmk, h_free[hb])
                P.dma("sync", htile[:, hsl].rearrange("p (c t) -> p c t", c=nblk),
                      ts["h1l_bf"][:, j * 128:(j + 1) * 128].rearrange("(c p) t -> p c t", p=128),
                      sem=lh2[hb])
                lh_t = P.cnt(lh2[hb])
                for half in range(2):
                    csl = slice(half * 512, (half + 1) * 512)
                    for kb in range(nblk):
                        P.emit("tensor", lambda e, o=psk[:], lt=htile[:, hsl][:, kb * 128:(kb + 1) * 128],
                               rt=wk_sb[:, kb * D:kb * D + D][:, csl], s=(kb == 0), p_=(kb == nblk - 1):
                               e.matmul(out=o, lhsT=lt, rhs=rt, start=s, stop=p_),
                               wait=[(lh2[hb], lh_t), (ldc, ldc_t), (ldcg, ldcg_t), (evk, P.cnt(evk))] if kb == 0 else ())
                    for kb in range(nblk):
                        P.emit("tensor", lambda e, o=psv[:], lt=htile[:, hsl][:, kb * 128:(kb + 1) * 128],
                               rt=wv_sb[:, kb * D:kb * D + D][:, csl], s=(kb == 0), p_=(kb == nblk - 1):
                               e.matmul(out=o, lhsT=lt, rhs=rt, start=s, stop=p_),
                               inc=[(mmk, 1)] if kb == nblk - 1 else ())
                    P.emit("vector", lambda e, o=ktmp[:], i_=psk[:], b=bk_bc[:, csl]:
                           e.tensor_tensor(out=o, in0=i_, in1=b, op=ALU.add),
                           wait=[(mmk, P.cnt(mmk)), (bev, bev_boot), (st, kv_st)])
                    P.emit("vector", lambda e, o=etmp[:], i_=ktmp[:]:
                           e.tensor_scalar(out=o, in0=i_, scalar1=0.0, scalar2=None, op0=ALU.min),
                           inc=[(evk, 1)])
                    P.emit("scalar", lambda e, o=etmp[:], i_=etmp[:]:
                           e.activation(out=o, in_=i_, func=AF.Exp),
                           wait=[(evk, P.cnt(evk))], inc=[(evk, 1)])
                    P.emit("vector", lambda e, o=kp_tm[:, csl], i_=ktmp[:], m=etmp[:]:
                           e.scalar_tensor_tensor(out=o, in0=i_, scalar=0.0, in1=m,
                                                  op0=ALU.max, op1=ALU.add),
                           wait=[(evk, P.cnt(evk))])
                    P.emit("vector", lambda e, o=v_tm[:, csl], i_=psv[:], b=bv_bc[:, csl]:
                           e.tensor_tensor(out=o, in0=i_, in1=b, op=ALU.add),
                           inc=[(evk, 1)])
                evk_t = P.cnt(evk)
                for s_ in range(nblk):
                    P.emit("tensor", lambda e, o=psks[:, s_:s_ + 1],
                           lt=kp_tm[:, s_ * 128:(s_ + 1) * 128], rt=onescb[:],
                           s=(j == 0 and s_ == 0), p_=(j == ntile - 1 and s_ == nblk - 1):
                           e.matmul(out=o, lhsT=lt, rhs=rt, start=s, stop=p_),
                           wait=[(evk, evk_t)] if s_ == 0 else (),
                           inc=[(mkv, 1)] if s_ == nblk - 1 else ())
                h_free[hb] = P.cnt(mmk)
                P.wait("gpsimd", mkv, P.cnt(mkv))
                P.dma("gpsimd", ts["k_tm"][j * 128:(j + 1) * 128, :], kp_tm[:], sem=st)
                P.dma("gpsimd", ts["v_tm"][j * 128:(j + 1) * 128, :], v_tm[:], sem=st)
                kv_st = P.cnt(st)
            P.emit("scalar", lambda e: e.copy(out=ks_col[:], in_=psks[:]),
                   wait=[(mkv, P.cnt(mkv))], inc=[(bev, 1)])
            P.emit("vector", lambda e: e.memset(ks_blk[:], 0.0), wait=[(bev, P.cnt(bev))],
                   inc=[(bev, 1)])
            P.wait("vector", bev, P.cnt(bev))
            for h_ in range(H):
                s_ = h_ // 2
                base = (h_ % 2) * 64
                P.emit("vector", lambda e, o=ks_blk[base:base + 64, s_ * 16 + h_:s_ * 16 + h_ + 1],
                       i_=ks_col[base:base + 64, s_:s_ + 1]:
                       e.tensor_copy(out=o, in_=i_),
                       inc=[(bev, 1)] if h_ == H - 1 else ())
            P.wait("vector", bev, P.cnt(bev))
            P.barrier()
        with ExitStack() as pctx:
            pspair_full = [pctx.enter_context(psum_u(nc, f"la_pp{b}", [128, 512], F32))
                           for b in range(4)]
            pspair = [t[:, 0:128] for t in pspair_full]
            kvld = [P.sem("gh6"), P.sem("gh7")]
            for half2 in range(2):
                kvt_free = [P.cnt(mkv)] * 2
                for j in range(ntile):
                    hb = j % 2
                    hsl = slice(hb * nblk * 128, (hb + 1) * nblk * 128)
                    P.wait("sync", mkv, kvt_free[hb])
                    P.dma("sync", htile[:, hsl][:, 0:D], ts["k_tm"][j * 128:(j + 1) * 128, :], sem=kvld[hb])
                    P.dma("sync", v_tm[:] if hb == 0 else ktmp2[:],
                          ts["v_tm"][j * 128:(j + 1) * 128, :], sem=kvld[hb])
                    lkv_t = P.cnt(kvld[hb])
                    vsrc = v_tm if hb == 0 else ktmp2
                    for sp in range(4):
                        s_ = half2 * 4 + sp
                        P.emit("tensor", lambda e, o=pspair[sp],
                               lt=htile[:, hsl][:, 2 * s_ * 64:(2 * s_ + 2) * 64],
                               rt=vsrc[:, 2 * s_ * 64:(2 * s_ + 2) * 64],
                               s=(j == 0), p_=(j == ntile - 1):
                               e.matmul(out=o, lhsT=lt, rhs=rt, start=s, stop=p_),
                               wait=[(kvld[hb], lkv_t)] if sp == 0 else (),
                               inc=[(mkv, 1)] if sp == 3 else ())
                    kvt_free[hb] = P.cnt(mkv)
                mkv_t2 = P.cnt(mkv)
                for sp in range(4):
                    s_ = half2 * 4 + sp
                    flat0 = s_ * 64 if s_ < 4 else 256 + (s_ - 4) * 64
                    dst0 = kv_bf[0:64, flat0:flat0 + 64]
                    dst1 = kv_bf[64:128, flat0:flat0 + 64]
                    P.emit("scalar", lambda e, o=dst0, i_=pspair_full[sp][0:64, 0:64]: e.copy(out=o, in_=i_),
                           wait=[(mkv, mkv_t2)] if sp == 0 else ())
                    P.emit("scalar", lambda e, o=dst1, i_=pspair_full[sp][64:128, 64:128]: e.copy(out=o, in_=i_),
                           inc=[(bev, 1)] if sp == 3 else ())
                P.wait("scalar", bev, P.cnt(bev))
                P.wait("tensor", bev, P.cnt(bev))
            kvrdy = P.cnt(bev)
            P.barrier()
        pctxC = ExitStack()
        psden = pctxC.enter_context(psum_u(nc, "la_psden", [16, tc], F32))
        pszb = pctxC.enter_context(psum_u(nc, "la_pszb", [128, tc], F32))
        psy = [pctxC.enter_context(psum_u(nc, f"la_psy{b}", [128, tc], F32)) for b in range(2)]
        q_guard = (myy, 0)
        psy_guard = [(evy, 0), (evy, 0)]
        zbc_guard = [(evy, 0), (evy, 0)]
        xer_guard = [(evo, 0), (evo, 0)]
        oo_guard = [(st, 0), (st, 0)]
        attn_guard = (mo, 0)
        den_ps_guard = (bev, kvrdy)
        den_sb_guard = (bev, kvrdy)
        for c in range(nch):
            t0 = c * tc
            P.wait("sync", q_guard[0], q_guard[1])
            P.dma("sync", qch[:].rearrange("p (c t) -> p c t", c=nblk),
                  ts["q1_bf"][:, t0:t0 + tc].rearrange("(c p) t -> p c t", p=128), sem=lq)
            lq_t = P.cnt(lq)
            for s_ in range(nblk):
                P.emit("tensor", lambda e, o=psden[:], lt=ks_blk[:, s_ * 16:(s_ + 1) * 16],
                       rt=qch[:, s_ * tc:(s_ + 1) * tc], s=(s_ == 0), p_=(s_ == nblk - 1):
                       e.matmul(out=o, lhsT=lt, rhs=rt, start=s, stop=p_),
                       wait=[(lq, lq_t), (bev, kvrdy), den_ps_guard] if s_ == 0 else (),
                       inc=[(mden, 1)] if s_ == nblk - 1 else ())
            P.emit("scalar", lambda e: e.copy(out=den_sb[:], in_=psden[:]),
                   wait=[(mden, P.cnt(mden)), den_sb_guard], inc=[(bev, 1)])
            den_ps_guard = (bev, P.cnt(bev))
            P.emit("vector", lambda e: e.tensor_scalar(out=den_sb[:], in0=den_sb[:],
                                                       scalar1=1e-6, scalar2=None, op0=ALU.add),
                   wait=[(bev, P.cnt(bev))])
            P.emit("vector", lambda e: e.reciprocal(out=z_sb[:], in_=den_sb[:]), inc=[(bev, 1)])
            P.wait("vector", bev, P.cnt(bev))
            zrdy = P.cnt(bev)
            for s_ in range(nblk):
                zb = s_ % 2
                zsl = zbc_sb[:, zb * tc:(zb + 1) * tc]
                P.emit("tensor", lambda e, o=pszb[:], lt=zsel[:, s_ * 128:(s_ + 1) * 128], r=z_sb[:]:
                       e.matmul(out=o, lhsT=lt, rhs=r, start=True, stop=True),
                       wait=[(bev, zrdy), zbc_guard[zb], (bev, P.cnt(bev))], inc=[(mzb, 1)])
                P.emit("scalar", lambda e, o=zsl, i_=pszb[:]: e.copy(out=o, in_=i_),
                       wait=[(mzb, P.cnt(mzb))], inc=[(bev, 1)])
                zbc_rdy = P.cnt(bev)
                pb = s_ % 2
                for hh in range(2):
                    h_ = 2 * s_ + hh
                    base = (h_ % 2) * 64
                    kvs = kv_bf[:, 0:256] if h_ < 8 else kv_bf[:, 256:512]
                    col = ((h_ // 2) % 4) * 64
                    P.emit("tensor", lambda e, o=psy[pb][base:base + 64, :],
                           lt=kvs[base:base + 64, col:col + 64],
                           rt=qch[:, s_ * tc:(s_ + 1) * tc][base:base + 64, :]:
                           e.matmul(out=o, lhsT=lt, rhs=rt, start=True, stop=True),
                           wait=[psy_guard[pb]] if hh == 0 else (),
                           inc=[(myy, 1)] if hh == 1 else ())
                P.emit("vector", lambda e, o=attn_sb[:, s_ * tc:(s_ + 1) * tc], a=psy[pb][:], b=zsl:
                       e.tensor_tensor(out=o, in0=a, in1=b, op=ALU.mult),
                       wait=[(myy, P.cnt(myy)), (bev, zbc_rdy), attn_guard],
                       inc=[(evy, 1)])
                psy_guard[pb] = (evy, P.cnt(evy))
                zbc_guard[zb] = (evy, P.cnt(evy))
            evy_t = P.cnt(evy)
            q_guard = (myy, P.cnt(myy))
            for it in range(nblk):
                xb_ = it % 2
                P.wait("sync", xer_guard[xb_][0], xer_guard[xb_][1])
                P.dma("sync", xer[:, xb_ * tc:(xb_ + 1) * tc],
                      ts["xe"][it * 128:(it + 1) * 128, t0:t0 + tc], sem=lx2[xb_])
                lx_t = P.cnt(lx2[xb_])
                pb = it % 2
                for kb in range(nblk):
                    P.emit("tensor", lambda e, o=psy[pb][:], lt=wo_sb[:, kb * D + it * 128:kb * D + (it + 1) * 128],
                           rt=attn_sb[:, kb * tc:(kb + 1) * tc], s=(kb == 0), p_=(kb == nblk - 1):
                           e.matmul(out=o, lhsT=lt, rhs=rt, start=s, stop=p_),
                           wait=[(evy, evy_t), psy_guard[pb]] if kb == 0 else (),
                           inc=[(mo, 1)] if kb == nblk - 1 else ())
                P.emit("vector", lambda e, o=oo[:, (it % 2) * tc:(it % 2 + 1) * tc],
                       i_=psy[pb][:], b=bocol[:, it:it + 1], r=xer[:, xb_ * tc:(xb_ + 1) * tc]:
                       e.scalar_tensor_tensor(out=o, in0=i_, scalar=b, in1=r,
                                              op0=ALU.add, op1=ALU.add),
                       wait=[(mo, P.cnt(mo)), (lx2[xb_], lx_t), oo_guard[it % 2]],
                       inc=[(evo, 1)])
                psy_guard[pb] = (evo, P.cnt(evo))
                xer_guard[xb_] = (evo, P.cnt(evo))
                P.wait("gpsimd", evo, P.cnt(evo))
                P.dma("gpsimd", ts["xe2"][it * 128:(it + 1) * 128, t0:t0 + tc],
                      oo[:, (it % 2) * tc:(it % 2 + 1) * tc], sem=st)
                oo_guard[it % 2] = (st, P.cnt(st))
            attn_guard = (mo, P.cnt(mo))
            den_sb_guard = (mzb, P.cnt(mzb))
        P.wait("gpsimd", st, P.cnt(st))
        P.barrier()
        pctxC.close()


def emit_l1_router(P, ts, ctx, tc=512):
    nc = P.nc
    nk = D // 128
    E = 8
    nch = T // tc
    with ExitStack() as sctx:
        w_sb = sctx.enter_context(sbuf_u(nc, "r1_w", [128, nk * E], F32))
        rhs = sctx.enter_context(sbuf_u(nc, "r1_rhs", [128, nk * tc], F32))
        gfm = sctx.enter_context(sbuf_u(nc, "r1_gfm", [E, tc], F32))
        ident = sctx.enter_context(sbuf_u(nc, "r1_id", [128, 128], F32))
        rbrow = sctx.enter_context(sbuf_u(nc, "r1_rbr", [1, E], F32))
        rb_bc = sctx.enter_context(sbuf_u(nc, "r1_rbc", [128, E], F32))
        onesr = sctx.enter_context(sbuf_u(nc, "r1_or", [1, 128], F32))
        tm = sctx.enter_context(sbuf_u(nc, "r1_tm", [128, 8 * E], F32))
        wfm = sctx.enter_context(sbuf_u(nc, "r1_wfm", [E, tc], F32))
        ps_l = sctx.enter_context(psum_u(nc, "r1_psl", [E, tc], F32))
        ps_t = sctx.enter_context(psum_u(nc, "r1_pst", [128, E], F32))
        ps_c = sctx.enter_context(psum_u(nc, "r1_psc", [E, 128], F32))
        ldc = P.sem("gh0")
        lr = P.sem("gd0")
        mmL = P.sem("gc0")
        sgd = P.sem("gc1")
        trd = P.sem("gc2")
        tmd = P.sem("gc3")
        cbk = P.sem("gc4")
        st = P.sem("gd1")
        vg = P.sem("gc5")

        P.dma("sync", w_sb[:].rearrange("p (c i) -> p c i", c=nk),
              ts["w_l1r"].rearrange("(c p) i -> p c i", p=128), sem=ldc)
        P.dma("sync", ident[:], ts["c_ident"], sem=ldc)
        _, ldc_t0 = P.dma("sync", rbrow[:], ts["b_l1r"], sem=ldc)
        _, ldc_t = P.dma("sync", onesr[:], ts["c_ones_row"], sem=ldc)
        P.emit("tensor", lambda e: e.matmul(out=ps_t[:], lhsT=onesr[:], rhs=rbrow[:],
                                            start=True, stop=True),
               wait=[(ldc, ldc_t)], inc=[(trd, 1)])
        P.emit("scalar", lambda e: e.copy(out=rb_bc[:], in_=ps_t[:]),
               wait=[(trd, P.cnt(trd))], inc=[(sgd, 1)])
        P.barrier()

        for c in range(nch):
            t0 = c * tc
            P.wait("gpsimd", mmL, P.cnt(mmL))
            P.dma("gpsimd", rhs[:].rearrange("p (c t) -> p c t", c=nk),
                  ts["xn1_f32"][:, t0:t0 + tc].rearrange("(c p) t -> p c t", p=128), sem=lr)
            lr_t = P.cnt(lr)
            P.wait("tensor", sgd, P.cnt(sgd))
            for k in range(nk):
                P.emit("tensor", lambda e, lt=w_sb[:, k * E:(k + 1) * E], rt=rhs[:, k * tc:(k + 1) * tc],
                       s=(k == 0), p_=(k == nk - 1):
                       e.matmul(out=ps_l[:], lhsT=lt, rhs=rt, start=s, stop=p_),
                       wait=[(lr, lr_t)] if k == 0 else (),
                       inc=[(mmL, 1)] if k == nk - 1 else ())
            P.emit("scalar", lambda e: e.copy(out=gfm[:], in_=ps_l[:]),
                   wait=[(mmL, P.cnt(mmL))], inc=[(sgd, 1)])
            sgd_t = P.cnt(sgd)
            for j in range(tc // 128):
                P.emit("tensor", lambda e, i_=gfm[:, j * 128:(j + 1) * 128]:
                       e.transpose(out=ps_t[:], in_=i_, identity=ident[0:E, 0:E]),
                       wait=[(sgd, sgd_t), (tmd, P.cnt(tmd))], inc=[(trd, 1)])
                g_tm = tm[:, 0:E]
                gb = tm[:, E:2 * E]
                v8 = tm[:, 2 * E:3 * E]
                erow = tm[:, 3 * E:3 * E + 1]
                w1c = tm[:, 3 * E + 1:3 * E + 2]
                w2c = tm[:, 3 * E + 2:3 * E + 3]
                m1 = tm[:, 4 * E:5 * E]
                m2 = tm[:, 5 * E:6 * E]
                wtm = tm[:, 6 * E:7 * E]
                P.emit("scalar", lambda e, o=g_tm: e.copy(out=o, in_=ps_t[:]),
                       wait=[(trd, P.cnt(trd))], inc=[(sgd, 1)])

                def v(fn, wait=(), inc=(), g_=True):
                    inc = list(inc)
                    if g_ and not inc:
                        inc = [(vg, 1)]
                    P.emit("vector", fn, wait=wait, inc=inc)
                    if g_:
                        sname, _ = inc[0]
                        P.wait("vector", sname, P.cnt(sname))
                v(lambda e, o=gb, a=g_tm, b=rb_bc[:]: e.tensor_tensor(out=o, in0=a, in1=b, op=ALU.add),
                  wait=[(sgd, P.cnt(sgd))])
                v(lambda e, o=v8, i_=gb: e.max(out=o, in_=i_))
                v(lambda e, o=erow, a=v8[:, 1:2], b=v8[:, 0:1]:
                  e.tensor_tensor(out=o, in0=a, in1=b, op=ALU.subtract))
                P.emit("scalar", lambda e, o=erow, i_=erow: e.activation(out=o, in_=i_, func=AF.Exp),
                       wait=[(vg, P.cnt(vg))], inc=[(vg, 1)])
                P.wait("vector", vg, P.cnt(vg))
                v(lambda e, o=w1c, i_=erow:
                  e.tensor_scalar(out=o, in0=i_, scalar1=1.0, scalar2=None, op0=ALU.add))
                v(lambda e, o=w1c, i_=w1c: e.reciprocal(out=o, in_=i_))
                v(lambda e, o=w2c, a=erow, b=w1c: e.tensor_tensor(out=o, in0=a, in1=b, op=ALU.mult))
                v(lambda e, o=m1, a=gb, th=v8[:, 0:1]:
                  e.tensor_scalar(out=o, in0=a, scalar1=th, scalar2=None, op0=ALU.is_equal))
                v(lambda e, o=m2, a=gb, th=v8[:, 1:2]:
                  e.tensor_scalar(out=o, in0=a, scalar1=th, scalar2=None, op0=ALU.is_equal))
                v(lambda e, o=wtm, a=m1, w=w1c:
                  e.tensor_scalar(out=o, in0=a, scalar1=w, scalar2=None, op0=ALU.mult))
                v(lambda e, o=wtm, a=m2, w=w2c, b=wtm:
                  e.scalar_tensor_tensor(out=o, in0=a, scalar=w, in1=b, op0=ALU.mult, op1=ALU.add),
                  inc=[(tmd, 1)])
                P.emit("tensor", lambda e, i_=wtm:
                       e.transpose(out=ps_c[:], in_=i_, identity=ident[:]),
                       wait=[(tmd, P.cnt(tmd)), (cbk, P.cnt(cbk))], inc=[(trd, 1)])
                P.emit("scalar", lambda e, o=wfm[:, j * 128:(j + 1) * 128]:
                       e.copy(out=o, in_=ps_c[:]),
                       wait=[(trd, P.cnt(trd))], inc=[(cbk, 1)])
            P.wait("gpsimd", cbk, P.cnt(cbk))
            P.dma("gpsimd", ts["l1w"][:, t0:t0 + tc], wfm[:], sem=st)
            P.wait("sync", st, P.cnt(st))
        P.wait("gpsimd", st, P.cnt(st))
        P.barrier()


CP = 1280
CPCH = [(0, 512), (512, 512), (1024, 256)]


def emit_l1_sparse_prep(P, ts, ctx):
    nc = P.nc
    with ExitStack() as sctx:
        t_w8 = sctx.enter_context(sbuf_u(nc, "sp_w8", [8, T], F32))
        t_m = sctx.enter_context(sbuf_u(nc, "sp_m", [8, T], F32))
        t_a = sctx.enter_context(sbuf_u(nc, "sp_a", [8, T], F32))
        t_b = sctx.enter_context(sbuf_u(nc, "sp_b", [8, T], F32))
        t_c = sctx.enter_context(sbuf_u(nc, "sp_c", [8, T], F32))
        t_d = sctx.enter_context(sbuf_u(nc, "sp_d", [8, T], F32))
        t_st = sctx.enter_context(sbuf_u(nc, "sp_st", [8, T], F32))
        t_v = sctx.enter_context(sbuf_u(nc, "sp_v", [8, T], F32))
        Qa = sctx.enter_context(sbuf_u(nc, "sp_Qa", [128, T], F32))
        Qb = sctx.enter_context(sbuf_u(nc, "sp_Qb", [128, T], F32))
        w8 = t_w8[:]
        m = t_m[:]
        ones8 = t_a[:]
        sel = t_a[:]
        cs = t_b[:]
        rk = t_b[:]
        slot = t_c[:]
        tmp = t_d[:]
        st2 = t_st[:]
        vmask = t_v[:]
        dump8 = Qb[0:8, :]
        metaA = sctx.enter_context(sbuf_u(nc, "sp_mA", [2, T], F32))
        metaB = sctx.enter_context(sbuf_u(nc, "sp_mB", [2, T], F32))
        zini = sctx.enter_context(sbuf_u(nc, "sp_z", [8, 1], F32))
        ecol = sctx.enter_context(sbuf_u(nc, "sp_ec", [8, 1], F32))
        lst = sctx.enter_context(sbuf_u(nc, "sp_lst", [8, 8], F32))
        grp2 = sctx.enter_context(sbuf_u(nc, "sp_g2", [128, 2], F32))
        idsb = sctx.enter_context(sbuf_u(nc, "sp_id", [128, 128], F32))
        tm4 = [sctx.enter_context(sbuf_u(nc, f"sp_t4{b}", [128, 4], F32)) for b in range(2)]
        tm8 = [sctx.enter_context(sbuf_u(nc, f"sp_t8{b}", [128, 8], F32)) for b in range(2)]
        tmu2 = [sctx.enter_context(sbuf_u(nc, f"sp_u2{b}", [128, 2], U32)) for b in range(2)]
        tmu8 = [sctx.enter_context(sbuf_u(nc, f"sp_u8{b}", [128, 8], U32)) for b in range(2)]
        pm = sctx.enter_context(psum_u(nc, "sp_pm", [8, 512], F32))
        p2a = sctx.enter_context(psum_u(nc, "sp_p2a", [2, 512], F32))
        p2b = sctx.enter_context(psum_u(nc, "sp_p2b", [2, 512], F32))
        pt = [sctx.enter_context(psum_u(nc, f"sp_pt{b}", [128, 128], F32)) for b in range(2)]

        ld = P.sem("gh0")
        vd = P.sem("gc0")
        mmd = P.sem("gc1")
        evd = P.sem("gc2")
        trd = P.sem("gc3")
        cvd = P.sem("gc4")
        std = P.sem("gd0")

        P.dma("sync", w8, ts["l1w"], sem=ld)
        P.dma("sync", ecol[:], ts["c_ecol8"], sem=ld)
        P.dma("sync", lst[:], ts["c_lst8"], sem=ld)
        P.dma("sync", grp2[:], ts["c_grp2"], sem=ld)
        _, ld_t = P.dma("sync", idsb[:], ts["c_ident"], sem=ld)
        P.emit("vector", lambda e: e.memset(ones8, 1.0))
        P.emit("vector", lambda e: e.memset(zini[:], 0.0))
        P.emit("vector", lambda e: e.memset(Qa[:], 0.0))
        P.emit("vector", lambda e: e.memset(Qb[:], 0.0), inc=[(vd, 1)])
        P.wait("sync", vd, P.cnt(vd))
        _, lddp_t = P.dma("sync", dump8, ts["c_dump8"], sem=ld)
        P.emit("vector", lambda e: e.tensor_scalar(out=m, in0=w8, scalar1=0.0,
                                                   scalar2=None, op0=ALU.is_gt),
               wait=[(ld, ld_t), (ld, lddp_t)])
        P.emit("vector", lambda e: e.tensor_tensor_scan(out=cs, data0=ones8, data1=m,
                                                        initial=zini[:], op0=ALU.mult, op1=ALU.add))
        P.emit("vector", lambda e: e.tensor_scalar(out=slot, in0=cs, scalar1=ecol[:, 0:1],
                                                   scalar2=-1.0, op0=ALU.add, op1=ALU.add))
        P.emit("vector", lambda e: e.tensor_scalar(out=tmp, in0=cs, scalar1=float(CP) + 0.5,
                                                   scalar2=None, op0=ALU.is_lt))
        P.emit("vector", lambda e: e.tensor_tensor(out=vmask, in0=tmp, in1=m, op=ALU.mult))
        P.emit("vector", lambda e: e.tensor_tensor(out=slot, in0=slot, in1=vmask, op=ALU.mult))
        P.emit("vector", lambda e: e.tensor_scalar(out=tmp, in0=vmask, scalar1=-1.0,
                                                   scalar2=1.0, op0=ALU.mult, op1=ALU.add))
        P.emit("vector", lambda e: e.tensor_tensor(out=tmp, in0=tmp, in1=dump8, op=ALU.mult))
        P.emit("vector", lambda e: e.tensor_tensor(out=st2, in0=slot, in1=tmp, op=ALU.add),
               inc=[(vd, 1)])
        vd_m = P.cnt(vd)
        for c in range(8):
            sl = slice(c * 512, (c + 1) * 512)
            P.emit("tensor", lambda e, o=pm[:], r=m[:, sl]:
                   e.matmul(out=o, lhsT=lst[0:8, 0:8], rhs=r, start=True, stop=True),
                   wait=[(vd, vd_m), (evd, P.cnt(evd))], inc=[(mmd, 1)])
            P.emit("scalar", lambda e, o=rk[:, sl], i_=pm[:]:
                   e.copy(out=o, in_=i_),
                   wait=[(mmd, P.cnt(mmd))], inc=[(evd, 1)])
        P.wait("vector", evd, P.cnt(evd))
        for r in range(2):
            P.emit("vector", lambda e, rr=float(r): e.tensor_scalar(out=sel, in0=rk,
                   scalar1=rr, scalar2=None, op0=ALU.is_equal))
            P.emit("vector", lambda e: e.tensor_tensor(out=sel, in0=sel, in1=vmask, op=ALU.mult))
            P.emit("vector", lambda e, b=r * 64: e.tensor_tensor(out=Qa[b:b + 8, :], in0=sel,
                   in1=st2, op=ALU.mult))
            P.emit("vector", lambda e, b=r * 64: e.tensor_tensor(out=Qb[b:b + 8, :], in0=sel,
                   in1=w8, op=ALU.mult), inc=[(vd, 1)])
        vd_q = P.cnt(vd)
        for c in range(8):
            sl = slice(c * 512, (c + 1) * 512)
            P.emit("tensor", lambda e, o=p2a[:], r=Qa[:, sl]:
                   e.matmul(out=o, lhsT=grp2[:], rhs=r, start=True, stop=True),
                   wait=[(vd, vd_q), (evd, P.cnt(evd))], inc=[(mmd, 1)])
            P.emit("scalar", lambda e, o=metaA[:, sl], i_=p2a[:]:
                   e.copy(out=o, in_=i_),
                   wait=[(mmd, P.cnt(mmd))], inc=[(evd, 1)])
            P.emit("tensor", lambda e, o=p2b[:], r=Qb[:, sl]:
                   e.matmul(out=o, lhsT=grp2[:], rhs=r, start=True, stop=True),
                   wait=[(evd, P.cnt(evd))], inc=[(mmd, 1)])
            P.emit("scalar", lambda e, o=metaB[:, sl], i_=p2b[:]:
                   e.copy(out=o, in_=i_),
                   wait=[(mmd, P.cnt(mmd))], inc=[(evd, 1)])
        evd_meta = P.cnt(evd)
        for jt in range(32):
            b = jt % 2
            tsl = slice(jt * 128, (jt + 1) * 128)
            P.emit("tensor", lambda e, o=pt[b][0:128, 0:2], i_=metaA[0:2, tsl]:
                   e.transpose(out=o, in_=i_, identity=idsb[0:2, 0:2]),
                   wait=[(evd, evd_meta), (cvd, P.cnt(cvd)), (std, P.cnt(std))], inc=[(trd, 1)])
            P.emit("scalar", lambda e, o=tm4[b][:, 0:2], i_=pt[b][0:128, 0:2]:
                   e.copy(out=o, in_=i_), wait=[(trd, P.cnt(trd))], inc=[(evd, 1)])
            P.emit("tensor", lambda e, o=pt[b][0:128, 0:2], i_=metaB[0:2, tsl]:
                   e.transpose(out=o, in_=i_, identity=idsb[0:2, 0:2]),
                   wait=[(evd, P.cnt(evd))], inc=[(trd, 1)])
            P.emit("scalar", lambda e, o=tm4[b][:, 2:4], i_=pt[b][0:128, 0:2]:
                   e.copy(out=o, in_=i_), wait=[(trd, P.cnt(trd))], inc=[(evd, 1)])
            P.emit("tensor", lambda e, o=pt[b][0:128, 0:8], i_=st2[:, tsl]:
                   e.transpose(out=o, in_=i_, identity=idsb[0:8, 0:8]),
                   wait=[(evd, P.cnt(evd))], inc=[(trd, 1)])
            P.emit("scalar", lambda e, o=tm8[b][:], i_=pt[b][0:128, 0:8]:
                   e.copy(out=o, in_=i_), wait=[(trd, P.cnt(trd))], inc=[(evd, 1)])
            P.emit("vector", lambda e, o=tmu2[b][:], i_=tm4[b][:, 0:2]:
                   e.tensor_copy(out=o, in_=i_), wait=[(evd, P.cnt(evd))])
            P.emit("vector", lambda e, o=tmu8[b][:], i_=tm8[b][:]:
                   e.tensor_copy(out=o, in_=i_), inc=[(cvd, 1)])
            cv_t = P.cnt(cvd)
            P.wait("gpsimd", cvd, cv_t)
            P.dma("gpsimd", ts["gidx1"][tsl, :], tmu2[b][:], sem=std)
            P.dma("gpsimd", ts["wsel1"][tsl, :], tm4[b][:, 2:4], sem=std)
            P.dma("gpsimd", ts["stu1"][tsl, :], tmu8[b][:], sem=std)
            P.wait("gpsimd", std, P.cnt(std))
        P.wait("gpsimd", std, P.cnt(std))
        P.barrier()


def emit_l1_input_scatter(P, ts, ctx):
    nc = P.nc
    with ExitStack() as sctx:
        xt = [sctx.enter_context(sbuf_u(nc, f"is_x{b}", [128, D], BF16)) for b in range(2)]
        su = [sctx.enter_context(sbuf_u(nc, f"is_s{b}", [128, 8], U32)) for b in range(2)]
        lt2 = [P.sem("gh0"), P.sem("gh1")]
        sc = P.sem("gd0")
        buf_free = [0, 0]
        for jt in range(32):
            b = jt % 2
            tsl = slice(jt * 128, (jt + 1) * 128)
            P.wait("sync", sc, buf_free[b])
            P.emit("sync", lambda e, o=xt[b][:], i_=ts["xn1_bf"][:, tsl]:
                   e.dma_start_transpose(out=o, in_=i_), inc=[(lt2[b], 16)])
            P.dma("sync", su[b][:], ts["stu1"][tsl, :], sem=lt2[b])
            lt_t = P.cnt(lt2[b])
            for e_ in range(8):
                P.emit("gpsimd", lambda e, b_=b, ee=e_:
                       e.indirect_dma_start(out=ts["xg1"],
                                            out_offset=IndirectOffsetOnAxis(su[b_][:, ee:ee + 1], 0),
                                            in_=xt[b_][:], in_offset=None),
                       wait=[(lt2[b], lt_t)] if e_ == 0 else (), inc=[(sc, 16)])
            buf_free[b] = P.cnt(sc)
        P.wait("gpsimd", sc, P.cnt(sc))
        P.barrier()


def emit_l1_experts(P, ts, ctx):
    nc = P.nc
    with ExitStack() as octx:
        b1c = octx.enter_context(sbuf_u(nc, "ex_b1", [128, 10 * 32], F32))
        b2e = octx.enter_context(sbuf_u(nc, "ex_b2", [128, 64], F32))
        idsb = octx.enter_context(sbuf_u(nc, "ex_id", [128, 128], F32))
        ldc = P.sem("gh0")
        P.dma("sync", b1c[:], ts["l1_b1_cols"], sem=ldc)
        P.dma("sync", b2e[:], ts["c_b2e"], sem=ldc)
        _, ldc_t = P.dma("sync", idsb[:], ts["c_ident"], sem=ldc)
        P.barrier()

        for e_ in range(8):
            with ExitStack() as sctx:
                prod = sctx.enter_context(sbuf_u(nc, "ex_pr", [128, 32 * CP], BF16))
                pu = [sctx.enter_context(psum_u(nc, f"ex_pu{b}", [128, 512], F32)) for b in range(2)]
                pd = [sctx.enter_context(psum_u(nc, f"ex_pd{b}", [128, 512], F32)) for b in range(2)]
                ptb = [sctx.enter_context(psum_u(nc, f"ex_pt{b}", [128, 128], F32)) for b in range(2)]
                upctx = ExitStack()
                w1 = upctx.enter_context(sbuf_u(nc, "ex_w1", [128, 8 * 4096], BF16))
                rhs = upctx.enter_context(sbuf_u(nc, "ex_rhs", [128, 8 * CP], BF16))
                lw = P.sem("gd0")
                lw2 = [P.sem("gd1"), P.sem("gd2")]
                lr = P.sem("gh1")
                mm = P.sem("gc0")
                ev = P.sem("gc1")
                trd = P.sem("gc2")
                tev = P.sem("gc3")
                std = P.sem("gd3")

                _, lw_t = P.dma("gpsimd", w1[:].rearrange("p (c i) -> p c i", c=8),
                                ts["w_l1e1"][e_].rearrange("(c p) i -> p c i", p=128), sem=lw)
                for (off, sz) in CPCH:
                    for ds in range(8):
                        P.emit("sync", lambda e, o=rhs[:, ds * CP + off:ds * CP + off + sz],
                               i_=ts["xg1"][e_ * CP + off:e_ * CP + off + sz, ds * 128:(ds + 1) * 128]:
                               e.dma_start_transpose(out=o, in_=i_), inc=[(lr, 16)])
                lr_t = P.cnt(lr)
                ps_free = [P.cnt(ev)] * 2
                q = 0
                for it in range(32):
                    for (off, sz) in CPCH:
                        pb = q % 2
                        q += 1
                        P.wait("tensor", ev, ps_free[pb])
                        for k in range(8):
                            P.emit("tensor", lambda e, o=pu[pb][0:128, 0:sz],
                                   lt=w1[:, k * 4096 + it * 128:k * 4096 + (it + 1) * 128],
                                   rt=rhs[:, k * CP + off:k * CP + off + sz],
                                   s=(k == 0), p_=(k == 7):
                                   e.matmul(out=o, lhsT=lt, rhs=rt, start=s, stop=p_),
                                   wait=[(lw, lw_t), (lr, lr_t), (ldc, ldc_t)] if k == 0 else (),
                                   inc=[(mm, 1)] if k == 7 else ())
                        P.emit("scalar", lambda e, o=prod[:, it * CP + off:it * CP + off + sz],
                               i_=pu[pb][0:128, 0:sz], bc=b1c[:, e_ * 32 + it:e_ * 32 + it + 1]:
                               e.activation(out=o, in_=i_, func=AF.Gelu, bias=bc),
                               wait=[(mm, P.cnt(mm))], inc=[(ev, 1)])
                        ps_free[pb] = P.cnt(ev)
                ev_up = P.cnt(ev)
                upctx.close()
                dnctx = ExitStack()
                w2b = [dnctx.enter_context(sbuf_u(nc, f"ex_w2{b}", [128, 32 * 128], BF16))
                       for b in range(2)]
                yfm = dnctx.enter_context(sbuf_u(nc, "ex_y", [128, 8 * CP], F32))
                ytm = dnctx.enter_context(sbuf_u(nc, "ex_yt", [128, D], BF16))
                pd_free = [P.cnt(ev)] * 2
                w2_free = [P.cnt(mm)] * 2
                for dt in range(8):
                    wb = dt % 2
                    P.wait("gpsimd", mm, w2_free[wb])
                    _, lw2_t = P.dma("gpsimd", w2b[wb][:].rearrange("p (c i) -> p c i", c=32),
                                     ts["w_l1e2"][e_][:, dt * 128:(dt + 1) * 128]
                                     .rearrange("(c p) i -> p c i", p=128), sem=lw2[wb])
                    for ci, (off, sz) in enumerate(CPCH):
                        pb = (dt * 3 + ci) % 2
                        P.wait("tensor", ev, pd_free[pb])
                        for k in range(32):
                            P.emit("tensor", lambda e, o=pd[pb][0:128, 0:sz],
                                   lt=w2b[wb][:, k * 128:(k + 1) * 128],
                                   rt=prod[:, k * CP + off:k * CP + off + sz],
                                   s=(k == 0), p_=(k == 31):
                                   e.matmul(out=o, lhsT=lt, rhs=rt, start=s, stop=p_),
                                   wait=[(lw2[wb], lw2_t), (ev, ev_up)] if k == 0 else (),
                                   inc=[(mm, 1)] if k == 31 else ())
                        P.emit("vector", lambda e, o=yfm[:, dt * CP + off:dt * CP + off + sz],
                               i_=pd[pb][0:128, 0:sz], bc=b2e[:, e_ * 8 + dt:e_ * 8 + dt + 1]:
                               e.tensor_scalar(out=o, in0=i_, scalar1=bc, scalar2=None, op0=ALU.add),
                               wait=[(mm, P.cnt(mm))], inc=[(ev, 1)])
                        pd_free[pb] = P.cnt(ev)
                    w2_free[wb] = P.cnt(mm)
                ev_dn = P.cnt(ev)
                ptb_free = [P.cnt(tev)] * 2
                for tt in range(10):
                    st_prev = P.cnt(std)
                    for dt in range(8):
                        pb = dt % 2
                        w_ = [(ev, ev_dn), (tev, ptb_free[pb])]
                        if dt == 0:
                            w_.append((std, st_prev))
                        P.emit("tensor", lambda e, o=ptb[pb][:],
                               i_=yfm[:, dt * CP + tt * 128:dt * CP + (tt + 1) * 128]:
                               e.transpose(out=o, in_=i_, identity=idsb[:]),
                               wait=w_, inc=[(trd, 1)])
                        P.emit("scalar", lambda e, o=ytm[:, dt * 128:(dt + 1) * 128], i_=ptb[pb][:]:
                               e.copy(out=o, in_=i_),
                               wait=[(trd, P.cnt(trd)), (std, st_prev)] if dt == 0 else
                                    [(trd, P.cnt(trd))],
                               inc=[(tev, 1)])
                        ptb_free[pb] = P.cnt(tev)
                    P.wait("gpsimd", tev, P.cnt(tev))
                    P.dma("gpsimd", ts["yg1"][e_ * CP + tt * 128:e_ * CP + (tt + 1) * 128, :],
                          ytm[:], sem=std)
                P.wait("gpsimd", std, P.cnt(std))
                P.barrier()
                dnctx.close()


def emit_l1_sparse_combine(P, ts, ctx):
    nc = P.nc
    with ExitStack() as sctx:
        gy = sctx.enter_context(sbuf_u(nc, "cbm_g", [128, 4 * D], BF16))
        gu = [sctx.enter_context(sbuf_u(nc, f"cbm_u{b}", [128, 2], U32)) for b in range(2)]
        wv = [sctx.enter_context(sbuf_u(nc, f"cbm_w{b}", [128, 2], F32)) for b in range(2)]
        acc = [sctx.enter_context(sbuf_u(nc, f"cbm_a{b}", [128, D], F32)) for b in range(2)]
        ymt = [sctx.enter_context(sbuf_u(nc, f"cbm_y{b}", [128, 128], F32)) for b in range(2)]
        idsb = sctx.enter_context(sbuf_u(nc, "cbm_id", [128, 128], F32))
        pt = [sctx.enter_context(psum_u(nc, f"cbm_p{b}", [128, 128], F32)) for b in range(2)]
        ldc = P.sem("gh0")
        lg2 = [P.sem("gh1"), P.sem("gh2")]
        gth = P.sem("gd0")
        vd = P.sem("gc0")
        trd = P.sem("gc1")
        tev = P.sem("gc2")
        std = P.sem("gd1")
        _, ldc_t = P.dma("sync", idsb[:], ts["c_ident"], sem=ldc)
        buf_free = [0, 0]
        ymt_free = [0, 0]
        ymt_store = [0, 0]
        acc_free = [0, 0]
        for jt in range(32):
            b = jt % 2
            tsl = slice(jt * 128, (jt + 1) * 128)
            P.wait("sync", vd, buf_free[b])
            P.dma("sync", gu[b][:], ts["gidx1"][tsl, :], sem=lg2[b])
            P.dma("sync", wv[b][:], ts["wsel1"][tsl, :], sem=lg2[b])
            lg_t = P.cnt(lg2[b])
            P.wait("gpsimd", vd, buf_free[b])
            for r in range(2):
                P.emit("gpsimd", lambda e, b_=b, rr=r:
                       e.indirect_dma_start(out=gy[:, (2 * b_ + rr) * D:(2 * b_ + rr + 1) * D],
                                            out_offset=None, in_=ts["yg1"],
                                            in_offset=IndirectOffsetOnAxis(gu[b_][:, rr:rr + 1], 0)),
                       wait=[(lg2[b], lg_t)] if r == 0 else (), inc=[(gth, 16)])
            gth_t = P.cnt(gth)
            P.emit("vector", lambda e, b_=b: e.tensor_scalar(
                   out=acc[b_][:], in0=gy[:, (2 * b_) * D:(2 * b_ + 1) * D],
                   scalar1=wv[b_][:, 0:1], scalar2=None, op0=ALU.mult),
                   wait=[(gth, gth_t), (trd, acc_free[b])])
            P.emit("vector", lambda e, b_=b: e.scalar_tensor_tensor(
                   out=acc[b_][:], in0=gy[:, (2 * b_ + 1) * D:(2 * b_ + 2) * D],
                   scalar=wv[b_][:, 1:2], in1=acc[b_][:], op0=ALU.mult, op1=ALU.add),
                   inc=[(vd, 1)])
            vd_t = P.cnt(vd)
            for ds in range(8):
                pb = ds % 2
                P.emit("tensor", lambda e, o=pt[pb][:], i_=acc[b][:, ds * 128:(ds + 1) * 128]:
                       e.transpose(out=o, in_=i_, identity=idsb[:]),
                       wait=[(vd, vd_t), (ldc, ldc_t), (tev, ymt_free[pb])], inc=[(trd, 1)])
                P.emit("scalar", lambda e, o=ymt[pb][:], i_=pt[pb][:]:
                       e.copy(out=o, in_=i_),
                       wait=[(trd, P.cnt(trd)), (std, ymt_store[pb])],
                       inc=[(tev, 1)])
                tev_t = P.cnt(tev)
                ymt_free[pb] = tev_t
                P.wait("gpsimd", tev, tev_t)
                P.dma("gpsimd", ts["y1moe"][ds * 128:(ds + 1) * 128, tsl], ymt[pb][:], sem=std)
                ymt_store[pb] = P.cnt(std)
            buf_free[b] = P.cnt(vd)
            acc_free[b] = P.cnt(trd)
        P.wait("gpsimd", std, P.cnt(std))
        P.barrier()


def emit_combine2(P, ts, ctx, x_in, s_in, y1_in, y2_in, alpha, x_out, tc=512):
    nc = P.nc
    nblk = D // 128
    nch = T // tc
    with ExitStack() as sctx:
        x_sb = sctx.enter_context(sbuf_u(nc, "c2_x", [128, 2 * tc], F32))
        s_sb = sctx.enter_context(sbuf_u(nc, "c2_s", [128, 2 * tc], F32))
        y1_sb = sctx.enter_context(sbuf_u(nc, "c2_y1", [128, 2 * tc], F32))
        y2_sb = sctx.enter_context(sbuf_u(nc, "c2_y2", [128, 2 * tc], F32))
        sy_sb = sctx.enter_context(sbuf_u(nc, "c2_sy", [128, 2 * tc], F32))
        o_sb = sctx.enter_context(sbuf_u(nc, "c2_o", [128, 2 * tc], F32))
        ld2 = [P.sem("gh0"), P.sem("gh1")]
        ly2 = [P.sem("gd0"), P.sem("gd1")]
        sd = P.sem("gc0")
        vd = P.sem("gc1")
        st = P.sem("gd2")
        slot_free = [P.cnt(st)] * 2
        for g in range(nblk):
            for c in range(nch):
                t0 = c * tc
                sb = (g * nch + c) % 2
                sl = slice(sb * tc, (sb + 1) * tc)
                rows = slice(g * 128, (g + 1) * 128)
                P.wait("sync", st, slot_free[sb])
                P.dma("sync", x_sb[:, sl], x_in[rows, t0:t0 + tc], sem=ld2[sb])
                P.dma("sync", s_sb[:, sl], s_in[rows, t0:t0 + tc], sem=ld2[sb])
                P.wait("gpsimd", st, slot_free[sb])
                P.dma("gpsimd", y1_sb[:, sl], y1_in[rows, t0:t0 + tc], sem=ly2[sb])
                P.dma("gpsimd", y2_sb[:, sl], y2_in[rows, t0:t0 + tc], sem=ly2[sb])
                P.emit("vector", lambda e, o=sy_sb[:, sl], a=y1_sb[:, sl], b=y2_sb[:, sl], al=alpha:
                       e.scalar_tensor_tensor(out=o, in0=a, scalar=al, in1=b,
                                              op0=ALU.mult, op1=ALU.add),
                       wait=[(ly2[sb], P.cnt(ly2[sb]))], inc=[(vd, 1)])
                P.emit("scalar", lambda e, o=sy_sb[:, sl], i_=sy_sb[:, sl]:
                       e.activation(out=o, in_=i_, func=AF.Silu),
                       wait=[(vd, P.cnt(vd))], inc=[(sd, 1)])
                P.emit("vector", lambda e, o=sy_sb[:, sl], a=s_sb[:, sl]:
                       e.tensor_tensor(out=o, in0=o, in1=a, op=ALU.mult),
                       wait=[(sd, P.cnt(sd)), (ld2[sb], P.cnt(ld2[sb]))])
                P.emit("vector", lambda e, o=o_sb[:, sl], a=x_sb[:, sl], b=sy_sb[:, sl]:
                       e.tensor_tensor(out=o, in0=a, in1=b, op=ALU.add),
                       inc=[(vd, 1)])
                P.wait("gpsimd", vd, P.cnt(vd))
                P.dma("gpsimd", x_out[rows, t0:t0 + tc], o_sb[:, sl], sem=st)
                slot_free[sb] = P.cnt(st)
        P.wait("gpsimd", st, P.cnt(st))
        P.barrier()


def emit_l1_moe(P, ts, ctx, tc=512):
    nc = P.nc
    nk = D // 128
    HID = 4096
    NIT = HID // 128
    nch = T // tc
    with ExitStack() as sctx:
        rhs = sctx.enter_context(sbuf_u(nc, "m1_rhs", [128, nk * tc], BF16))
        w1t = [sctx.enter_context(sbuf_u(nc, f"m1_w1{b}", [128, nk * 128], BF16)) for b in range(2)]
        w2t = [sctx.enter_context(sbuf_u(nc, f"m1_w2{b}", [128, NIT * 128], BF16)) for b in range(2)]
        b1c = sctx.enter_context(sbuf_u(nc, "m1_b1", [128, 10 * NIT], F32))
        b2bf = sctx.enter_context(sbuf_u(nc, "m1_b2", [8, D], BF16))
        sb2bf = sctx.enter_context(sbuf_u(nc, "m1_sb2", [2, D], BF16))
        wch = sctx.enter_context(sbuf_u(nc, "m1_wch", [8, tc], F32))
        wbf = sctx.enter_context(sbuf_u(nc, "m1_wbf", [8, tc], BF16))
        halfrow = sctx.enter_context(sbuf_u(nc, "m1_half", [2, tc], BF16))
        sel8 = sctx.enter_context(sbuf_u(nc, "m1_sel", [8, 8 * 128], F32))
        wbc = sctx.enter_context(sbuf_u(nc, "m1_wbc", [128, tc], F32))
        he_f = sctx.enter_context(sbuf_u(nc, "m1_hef", [128, tc], F32))
        prod = sctx.enter_context(sbuf_u(nc, "m1_prod", [128, NIT * tc], BF16))
        y_sb = sctx.enter_context(sbuf_u(nc, "m1_y", [128, nk * tc], F32))
        psu = [sctx.enter_context(psum_u(nc, f"m1_psu{b}", [128, tc], F32)) for b in range(2)]
        psy = [sctx.enter_context(psum_u(nc, f"m1_psy{b}", [128, tc], F32)) for b in range(2)]
        psb = sctx.enter_context(psum_u(nc, "m1_psb", [128, tc], F32))
        ldc = P.sem("gh0")
        lw1 = [P.sem("gd0"), P.sem("gd1")]
        lw2s = [P.sem("gd2"), P.sem("gd3")]
        lr = P.sem("gd4")
        lwc = P.sem("gh1")
        mmU = P.sem("gc0")
        mmB = P.sem("gc1")
        evU = P.sem("gc2")
        pvd = P.sem("gc3")
        mmD = P.sem("gc4")
        acc = P.sem("gc5")
        st = P.sem("gd5")

        P.dma("sync", sel8[:], ts["c_sel8"], sem=ldc)
        P.dma("sync", b1c[:], ts["l1_b1_cols"], sem=ldc)
        P.dma("gpsimd", b2bf[:], ts["b_l1e2"], sem=lw1[0])
        _, lb2_t = P.dma("gpsimd", sb2bf[:], ts["b_l1s2"], sem=lw1[0])
        ldc_t = P.cnt(ldc)
        P.emit("vector", lambda e: e.memset(halfrow[:], 0.5), inc=[(evU, 1)])
        P.barrier()

        wbuf_free = [P.cnt(pvd)] * 2
        w2buf_free = [P.cnt(mmD)] * 2
        psu_free = [P.cnt(pvd)] * 2
        psy_free = [P.cnt(acc)] * 2
        prod_free = P.cnt(mmD)
        wbc_guard = P.cnt(pvd)
        y_store_free = P.cnt(st)
        for c in range(nch):
            t0 = c * tc
            P.wait("gpsimd", mmU, P.cnt(mmU))
            P.dma("gpsimd", rhs[:].rearrange("p (c t) -> p c t", c=nk),
                  ts["xn1_bf"][:, t0:t0 + tc].rearrange("(c p) t -> p c t", p=128), sem=lr)
            lr_t = P.cnt(lr)
            P.wait("sync", mmB, P.cnt(mmB))
            P.dma("sync", wch[:], ts["l1w"][:, t0:t0 + tc], sem=lwc)
            lwc_t = P.cnt(lwc)
            P.emit("vector", lambda e: e.tensor_copy(out=wbf[:], in_=wch[:]),
                   wait=[(lwc, lwc_t), (mmD, P.cnt(mmD))], inc=[(evU, 1)])
            for ei in range(10):
                routed = ei < 8
                if routed:
                    P.emit("tensor", lambda e, lt=sel8[:, ei * 128:(ei + 1) * 128]:
                           e.matmul(out=psb[:], lhsT=lt, rhs=wch[:], start=True, stop=True),
                           wait=[(lwc, lwc_t), (ldc, P.cnt(ldc)), (evU, P.cnt(evU))], inc=[(mmB, 1)])
                    P.emit("scalar", lambda e: e.copy(out=wbc[:], in_=psb[:]),
                           wait=[(mmB, P.cnt(mmB)), (pvd, wbc_guard)], inc=[(evU, 1)])
                for it in range(NIT):
                    wb = it % 2
                    pb = it % 2
                    P.wait("gpsimd", pvd, wbuf_free[wb])
                    wsrc = ts["w_l1e1t"][ei] if routed else ts["w_l1s1t"][ei - 8]
                    P.dma("gpsimd", w1t[wb][:], wsrc[it], sem=lw1[wb])
                    lw_t = P.cnt(lw1[wb])
                    P.wait("tensor", pvd, psu_free[pb])
                    for k in range(nk):
                        P.emit("tensor", lambda e, o=psu[pb][:], lt=w1t[wb][:, k * 128:(k + 1) * 128],
                               rt=rhs[:, k * tc:(k + 1) * tc], s=(k == 0), p_=(k == nk - 1):
                               e.matmul(out=o, lhsT=lt, rhs=rt, start=s, stop=p_),
                               wait=[(lw1[wb], lw_t), (lr, lr_t)] if k == 0 else (),
                               inc=[(mmU, 1)] if k == nk - 1 else ())
                    bslot = (ei * NIT + it) if routed else ((8 + (ei - 8)) * NIT + it)
                    P.emit("scalar", lambda e, o=he_f[:], i_=psu[pb][:], b=b1c[:, bslot:bslot + 1]:
                           e.activation(out=o, in_=i_, func=AF.Gelu, bias=b),
                           wait=[(mmU, P.cnt(mmU)), (pvd, P.cnt(pvd))], inc=[(evU, 1)])
                    if routed:
                        P.emit("vector", lambda e, o=prod[:, it * tc:(it + 1) * tc], a=he_f[:], b=wbc[:]:
                               e.tensor_tensor(out=o, in0=a, in1=b, op=ALU.mult),
                               wait=[(evU, P.cnt(evU)), (mmD, prod_free)], inc=[(pvd, 1)])
                    else:
                        P.emit("vector", lambda e, o=prod[:, it * tc:(it + 1) * tc], a=he_f[:]:
                               e.tensor_scalar(out=o, in0=a, scalar1=0.5, scalar2=None, op0=ALU.mult),
                               wait=[(evU, P.cnt(evU)), (mmD, prod_free)], inc=[(pvd, 1)])
                    psu_free[pb] = P.cnt(pvd)
                    wbuf_free[wb] = P.cnt(pvd)
                wbc_guard = P.cnt(pvd)
                pvd_t = P.cnt(pvd)
                for dt in range(nk):
                    wb = dt % 2
                    pb = dt % 2
                    P.wait("gpsimd", mmD, w2buf_free[wb])
                    w2src = ts["w_l1e2t"][ei] if routed else ts["w_l1s2t"][ei - 8]
                    P.dma("gpsimd", w2t[wb][:], w2src[dt], sem=lw2s[wb])
                    lw2_t = P.cnt(lw2s[wb])
                    has_bias = routed or ei == 9
                    P.wait("tensor", acc, psy_free[pb])
                    for k in range(NIT):
                        P.emit("tensor", lambda e, o=psy[pb][:], lt=w2t[wb][:, k * 128:(k + 1) * 128],
                               rt=prod[:, k * tc:(k + 1) * tc], s=(k == 0),
                               p_=(not has_bias and k == NIT - 1):
                               e.matmul(out=o, lhsT=lt, rhs=rt, start=s, stop=p_),
                               wait=[(lw2s[wb], lw2_t), (pvd, pvd_t)] if k == 0 else (),
                               inc=[(mmD, 1)] if (not has_bias and k == NIT - 1) else ())
                    if routed:
                        P.emit("tensor", lambda e, o=psy[pb][:], lt=b2bf[:, dt * 128:(dt + 1) * 128],
                               rt=wbf[:]:
                               e.matmul(out=o, lhsT=lt, rhs=rt, start=False, stop=True),
                               wait=[(lw1[0], lb2_t), (evU, P.cnt(evU))],
                               inc=[(mmD, 1)])
                    elif ei == 9:
                        P.emit("tensor", lambda e, o=psy[pb][:], lt=sb2bf[:, dt * 128:(dt + 1) * 128],
                               rt=halfrow[:]:
                               e.matmul(out=o, lhsT=lt, rhs=rt, start=False, stop=True),
                               inc=[(mmD, 1)])
                    ysl = y_sb[:, dt * tc:(dt + 1) * tc]
                    if ei == 0:
                        P.emit("vector", lambda e, o=ysl, p_=psy[pb][:]: e.tensor_copy(out=o, in_=p_),
                               wait=[(mmD, P.cnt(mmD)), (st, y_store_free)], inc=[(acc, 1)])
                    else:
                        P.emit("vector", lambda e, o=ysl, p_=psy[pb][:]:
                               e.tensor_tensor(out=o, in0=o, in1=p_, op=ALU.add),
                               wait=[(mmD, P.cnt(mmD))], inc=[(acc, 1)])
                    psy_free[pb] = P.cnt(acc)
                    w2buf_free[wb] = P.cnt(mmD)
                prod_free = P.cnt(mmD)
            P.wait("gpsimd", acc, P.cnt(acc))
            for dt in range(nk):
                P.dma("gpsimd", ts["y1ffn"][dt * 128:(dt + 1) * 128, t0:t0 + tc],
                      y_sb[:, dt * tc:(dt + 1) * tc], sem=st)
            y_store_free = P.cnt(st)
        P.wait("gpsimd", st, P.cnt(st))
        P.barrier()


def build_dev(shared, phases, dump=()):
    nc = bass.Bass("TRN2", target_bir_lowering=False)
    ctx = ExitStack()
    ts = Tensors(nc, shared, dump=dump)
    ts.scratch("xn_bf", [D, T], BF16)
    ts.scratch("xn_f32", [D, T], F32)
    ts.scratch("u_pre", [DI, T], F32)
    ts.scratch("z_f32", [DI, T], F32)
    ts.scratch("u_f32", [DI, T], F32)
    ts.scratch("u_bf", [DI, T], BF16)
    ts.scratch("dtbc", [96, T], F32)
    ts.scratch("delta", [DI, T], F32)
    ts.scratch("du", [DI, T], F32)
    ts.scratch("y_scan", [DI, T], F32)
    ts.scratch("yg_bf", [DI, T], BF16)
    ts.scratch("mam_res", [D, T], F32)
    ts.scratch("h1_bf", [D, T], BF16)
    ts.scratch("h1_f32", [D, T], F32)
    ts.scratch("ff_bf", [4 * D, T], BF16)
    ts.scratch("s0pre", [D, T], F32)
    ts.scratch("s0", [D, T], F32)
    ts.scratch("comb", [32, T], F32)
    ts.scratch("y0ffn", [D, T], F32)
    ts.scratch("x1", [D, T], F32)
    ts.scratch("xn1_bf", [D, T], BF16)
    ts.scratch("xn1_f32", [D, T], F32)
    ts.scratch("xe", [D, T], F32)
    ts.scratch("h1l_bf", [D, T], BF16)
    ts.scratch("q1_bf", [D, T], BF16)
    ts.scratch("xe2", [D, T], F32)
    ts.scratch("h2l_bf", [D, T], BF16)
    ts.scratch("k_tm", [T, D], BF16)
    ts.scratch("v_tm", [T, D], BF16)
    ts.scratch("ffl_bf", [4 * D, T], BF16)
    ts.scratch("s1", [D, T], F32)
    ts.scratch("l1w", [8, T], F32)
    ts.scratch("y1ffn", [D, T], F32)
    ts.scratch("xg1", [8 * CP + 128, D], BF16)
    ts.scratch("yg1", [8 * CP + 128, D], BF16)
    ts.scratch("stu1", [T, 8], U32)
    ts.scratch("gidx1", [T, 2], U32)
    ts.scratch("wsel1", [T, 2], F32)
    ts.scratch("ysh0", [D, T], F32)
    ts.scratch("ysh", [D, T], F32)
    ts.scratch("y1moe", [D, T], F32)
    ts.scratch("dbg", [16 * 128, 512], F32)
    with ctx:
        P = Prog(nc, ctx)
        boot(P, ts, ctx)
        if "ln0" in phases:
            emit_ln_fm(P, ts, ctx, ts["x0"], out_bf=ts["xn_bf"], out_f32=ts["xn_f32"])
        if "l0m_inproj" in phases:
            emit_proj2(P, ctx, w=ts["w_ipu_p"], Kdim=D, Idim=DI,
                       rhs=ts["xn_bf"], outs=[(ts["u_pre"], F32)], tag="ipu")
            emit_proj2(P, ctx, w=ts["w_ipz_p"], Kdim=D, Idim=DI,
                       rhs=ts["xn_bf"], outs=[(ts["z_f32"], F32)], tag="ipz")
        if "l0m_conv" in phases:
            emit_l0_conv(P, ts, ctx)
        if "l0m_xproj" in phases:
            emit_proj2(P, ctx, w=ts["w_xproj_p"], Kdim=DI, Idim=96,
                       rhs=ts["u_bf"], outs=[(ts["dtbc"], F32)], tag="xp")
        if "l0m_delta" in phases:
            emit_l0_delta(P, ts, ctx)
        if "l0m_scan" in phases:
            emit_l0_scan(P, ts, ctx, dbg=("dbg" in dump))
        if "l0m_gate" in phases:
            emit_l0_gate(P, ts, ctx)
        if "l0m_out" in phases:
            emit_proj2(P, ctx, w=ts["w_outproj_p"], Kdim=DI, Idim=D,
                       rhs=ts["yg_bf"], res=ts["xn_f32"], outs=[(ts["mam_res"], F32)], tag="op")
            emit_rms_fm(P, ts, ctx, ts["mam_res"], "g_norm", 1e-12,
                        out_bf=ts["h1_bf"], out_f32=ts["h1_f32"], tag="rm1")
        if "l0m_ffn" in phases:
            emit_proj2(P, ctx, w=ts["w_l0ff1_p"], Kdim=D, Idim=4 * D,
                       rhs=ts["h1_bf"], bias=ts["b_l0ff1"], func=AF.Gelu,
                       outs=[(ts["ff_bf"], BF16)], tag="f1")
            emit_proj2(P, ctx, w=ts["w_l0ff2_p"], Kdim=4 * D, Idim=D,
                       rhs=ts["ff_bf"], bias=ts["b_l0ff2"], res=ts["h1_f32"],
                       outs=[(ts["s0pre"], F32)], tag="f2")
            emit_rms_fm(P, ts, ctx, ts["s0pre"], "g_ffnorm", 1e-12,
                        out_f32=ts["s0"], tag="rm2")
        if "l0f_router" in phases:
            emit_l0_router(P, ts, ctx, dbg=("dbg" in dump))
        if "l0f_moe" in phases:
            emit_l0_moe(P, ts, ctx)
        if "l0_combine" in phases:
            emit_combine(P, ts, ctx, ts["x0"], ts["s0"], ts["y0ffn"], ts["x1"])
        if "ln1" in phases:
            emit_ln_fm(P, ts, ctx, ts["x1"], out_bf=ts["xn1_bf"], out_f32=ts["xn1_f32"],
                       extra=("axpb", "xe", 32.0, "c_pe"))
        for l in range(4):
            if f"l1a_{l}" in phases:
                emit_ln_sb(P, ts, ctx, ts["xe"],
                           ts["ln1_s_cols"][:, l * 8:(l + 1) * 8],
                           ts["ln1_b_cols"][:, l * 8:(l + 1) * 8],
                           out_bf=ts["h1l_bf"])
                emit_proj2(P, ctx, w=ts["w_Wq_p"][l], Kdim=D, Idim=D, rhs=ts["h1l_bf"],
                           bias=ts["bq_cols"][:, l * 8:(l + 1) * 8], func="elu1",
                           outs=[(ts["q1_bf"], BF16)], tag=f"q{l}")
                emit_l1_attn_layer(P, ts, ctx, l)
                emit_ln_sb(P, ts, ctx, ts["xe2"],
                           ts["ln2_s_cols"][:, l * 8:(l + 1) * 8],
                           ts["ln2_b_cols"][:, l * 8:(l + 1) * 8],
                           out_bf=ts["h2l_bf"])
                emit_proj2(P, ctx, w=ts["w_W1_p"][l], Kdim=D, Idim=4 * D, rhs=ts["h2l_bf"],
                           bias=ts["b1_cols"][:, l * 32:(l + 1) * 32], func=AF.Relu,
                           outs=[(ts["ffl_bf"], BF16)], tag=f"e1_{l}")
                emit_proj2(P, ctx, w=ts["w_W2_p"][l], Kdim=4 * D, Idim=D, rhs=ts["ffl_bf"],
                           bias=ts["b2_cols"][:, l * 8:(l + 1) * 8], res=ts["xe2"],
                           outs=[(ts["xe"], F32)], tag=f"e2_{l}")
        if "l1a_final" in phases:
            emit_ln_sb(P, ts, ctx, ts["xe"], ts["p_lnf_s"], ts["p_lnf_b"],
                       out_f32=ts["s1"])
        if "l1f_router" in phases:
            emit_l1_router(P, ts, ctx)
        if "l1f_moe" in phases:
            emit_l1_moe(P, ts, ctx)
        if "l1_combine" in phases:
            emit_combine(P, ts, ctx, ts["x1"], ts["s1"], ts["y1ffn"], ts["out"])
        if "l1_combine" not in phases:
            P.dma("sync", ts["out"][0:128, 0:128], ts["x0"][0:128, 0:128], sem="dma_sync")
            P.wait("sync", "dma_sync", P.cnt("dma_sync"))
        P.finalize()
    return nc


ALL_PHASES = ["ln0", "l0m_inproj", "l0m_conv", "l0m_xproj", "l0m_delta", "l0m_scan",
              "l0m_gate", "l0m_out", "l0m_ffn", "l0f_router", "l0f_moe",
              "l0_combine", "ln1", "l1a_0", "l1a_1", "l1a_2", "l1a_3", "l1a_final",
              "l1f_router", "l1f_moe", "l1_combine"]


def kernel(**inputs):
    shared, per_core = host_prep(inputs)
    nc = build_dev(shared, ALL_PHASES, dump=())
    in_maps = []
    for c in range(NCORES):
        m = dict(shared)
        m.update(per_core[c])
        in_maps.append(m)
    res = run_bass_kernel_spmd(nc, in_maps, core_ids=list(range(NCORES)))
    out = np.stack([np.ascontiguousarray(res.results[c]["out"].T) for c in range(NCORES)])
    return out.astype(np.float32)


if __name__ == "__main__":
    pass


# version 04 (final)
# speedup vs baseline: 1.0241x; 1.0241x over previous
import math
from contextlib import ExitStack

import numpy as np
import ml_dtypes

import concourse.bass as bass
import concourse.mybir as mybir
from concourse.bass_utils import run_bass_kernel_spmd
from concourse.alu_op_type import AluOpType as ALU
from concourse.bass_core_rust import IndirectOffsetOnAxis

F32 = mybir.dt.float32
BF16 = mybir.dt.bfloat16
U32 = mybir.dt.uint32
AF = mybir.ActivationFunctionType
AX = mybir.AxisListType

T = 4096
D = 1024
DI = 2048
DS = 16
DTR = 64
NCORES = 8

ENGINES = ("sync", "vector", "scalar", "tensor", "gpsimd")

ACT_CONSTS = [1e-5, 1e-12, 1e-6, 1e-20, -0.5, -1.0, 0.5, 1.0, 2.5, 32.0,
              1.0 / D, 1.0 / DI]




_NAME_UID = [0]


def sbuf_u(nc, name, shape, dtype):
    _NAME_UID[0] += 1
    return nc.sbuf_tensor(f"{name}_{_NAME_UID[0]}", shape, dtype)


def psum_u(nc, name, shape, dtype):
    _NAME_UID[0] += 1
    return nc.psum_tensor(f"{name}_{_NAME_UID[0]}", shape, dtype)

class Prog:

    def __init__(self, nc, ctx):
        self.nc = nc
        self.ctx = ctx
        self.q = {e: [] for e in ENGINES}
        self._sems = {}
        self._cnt = {}
        self._dirty = {e: False for e in ENGINES}
        self._last_inst = {e: None for e in ENGINES}
        self.ph = self.sem("ph")
        self.esem = {e: self.sem("fin_" + e) for e in ENGINES}

    def sem(self, name):
        if name not in self._cnt:
            h = self.ctx.enter_context(self.nc.semaphore(name))
            self._cnt[name] = 0
            self._sems[name] = h
        return name

    def cnt(self, sem):
        return self._cnt[sem]

    def _bump(self, sem, n):
        self._cnt[sem] += n
        return self._cnt[sem]

    def emit(self, eng, fn, wait=(), inc=(), is_dma=False, dirty=True):
        sems = self._sems
        targets = [self._bump(s, n) for (s, n) in inc]
        inc2 = list(inc)
        wait2 = list(wait)

        def thunk(e):
            for s, v in wait2:
                e.wait_ge(sems[s], v)
            inst = fn(e)
            for s, n in inc2:
                inst = inst.then_inc(sems[s], n)
            return inst

        self.q[eng].append(thunk)
        if dirty:
            self._dirty[eng] = True
        return targets

    def wait(self, eng, sem, value):
        sems = self._sems
        self.q[eng].append(lambda e: e.wait_ge(sems[sem], value))

    def dma(self, eng, out_ap, in_ap, sem=None, **kw):
        if sem is None:
            sem = "dma_" + eng
        if sem not in self._cnt:
            h = self.ctx.enter_context(self.nc.semaphore(sem))
            self._cnt[sem] = 0
            self._sems[sem] = h
        tgt = self.emit(eng, lambda e: e.dma_start(out=out_ap, in_=in_ap, **kw),
                        inc=[(sem, 16)])
        return sem, tgt[0]

    def barrier(self):
        sems = self._sems
        for e in ENGINES:
            if self._dirty[e]:
                if e in ("vector", "scalar", "tensor", "gpsimd"):
                    t = self.emit(e, lambda en: en.engine_nop() if hasattr(en, "engine_nop") else en.nop(),
                                  inc=[(self.esem[e], 1)], dirty=False)[0]
                    self.wait(e, self.esem[e], t)
                tgt = self._bump(self.ph, 1)
                self.q[e].append(lambda en, s=sems[self.ph]: en.nop().then_inc(s, 1))
        total = self._cnt[self.ph]
        for e in ENGINES:
            self.wait(e, self.ph, total)
            self._dirty[e] = False
        return total

    def finalize(self):
        nc = self.nc
        with nc.Block() as block:
            @block.sync
            def _(e):
                for th in self.q["sync"]:
                    th(e)

            @block.vector
            def _(e):
                for th in self.q["vector"]:
                    th(e)

            @block.scalar
            def _(e):
                for th in self.q["scalar"]:
                    th(e)

            @block.tensor
            def _(e):
                for th in self.q["tensor"]:
                    th(e)

            @block.gpsimd
            def _(e):
                for th in self.q["gpsimd"]:
                    th(e)


def _cols(v):
    n = v.shape[0] // 128
    return np.ascontiguousarray(v.reshape(n, 128).T)


def _prew(w):
    K, I = w.shape
    nk = K // 128
    return np.ascontiguousarray(w.reshape(nk, 128, I).transpose(1, 0, 2).reshape(128, nk * I))


def host_prep(inputs):
    f = np.float32
    sh = {}
    a = {k: np.asarray(v) for k, v in inputs.items()}

    sh["c_actc"] = np.tile(np.asarray(ACT_CONSTS, f)[None, :], (128, 1))
    sh["c_ident"] = np.eye(128, dtype=f)
    sh["c_ident_bf"] = np.eye(128, dtype=ml_dtypes.bfloat16)
    sh["c_ones_col"] = np.ones((128, 1), f)
    sh["c_ones_col_bf"] = np.ones((128, 1), ml_dtypes.bfloat16)
    sh["c_ones_row"] = np.ones((1, 128), f)
    for K in (16, 32):
        s = np.zeros((K, K * 128), f)
        for n in range(K):
            s[n, n * 128:(n + 1) * 128] = 1.0
        sh[f"c_sel{K}"] = s
    half = D // 2
    freq = np.exp(np.arange(half, dtype=np.float64) * (-math.log(10000.0) / (half - 1)))
    pos = (np.arange(T, dtype=np.float64) + 1.0)[:, None] * freq[None, :]
    pe = np.concatenate([np.sin(pos), np.cos(pos)], axis=-1).astype(f)
    sh["c_pe"] = np.ascontiguousarray(pe.T)

    w_inproj = np.ascontiguousarray(a["L0_attn_in_proj_w"].T)
    sh["w_ipu_p"] = _prew(w_inproj[:, 0:2048])
    sh["w_ipz_p"] = _prew(w_inproj[:, 2048:4096])
    sh["w_conv"] = _cols(np.ascontiguousarray(a["L0_attn_conv_w"].T).reshape(-1))
    for j in range(4):
        sh[f"w_conv{j}"] = _cols(np.ascontiguousarray(a["L0_attn_conv_w"][:, j]))
    sh["b_conv"] = _cols(a["L0_attn_conv_b"])
    sh["w_xproj_p"] = _prew(np.ascontiguousarray(a["L0_attn_x_proj_w"].T))
    sh["w_dtproj"] = np.ascontiguousarray(a["L0_attn_dt_proj_w"].T)
    sh["b_dtproj"] = _cols(a["L0_attn_dt_proj_b"])
    sh["a_log"] = np.ascontiguousarray(
        a["L0_attn_A_log"].reshape(16, 128, DS).transpose(1, 0, 2).reshape(128, 16 * DS))
    sh["d_vec"] = _cols(a["L0_attn_D"])
    sh["w_outproj_p"] = _prew(np.ascontiguousarray(a["L0_attn_out_proj_w"].T))
    sh["g_norm"] = _cols(a["L0_attn_norm_w"])
    sh["w_l0ff1_p"] = _prew(np.ascontiguousarray(a["L0_attn_ffn_w1"].T))
    sh["b_l0ff1"] = _cols(a["L0_attn_ffn_b1"])
    sh["w_l0ff2_p"] = _prew(np.ascontiguousarray(a["L0_attn_ffn_w2"].T))
    sh["b_l0ff2"] = _cols(a["L0_attn_ffn_b2"])
    sh["g_ffnorm"] = _cols(a["L0_attn_ffn_norm_w"])

    sh["w_l0gate"] = np.ascontiguousarray(a["L0_ffn_gate_w"].T)
    sh["b_l0gate"] = np.ascontiguousarray(a["L0_ffn_gate_bias"].reshape(1, 32))
    eg = a["L0_ffn_w_gate"].transpose(0, 2, 1)
    eu = a["L0_ffn_w_up"].transpose(0, 2, 1)
    sh["w_l0egp"] = np.ascontiguousarray(
        eg.reshape(32, 8, 128, 512).transpose(0, 2, 1, 3).reshape(32, 128, 8 * 512))
    sh["w_l0eup"] = np.ascontiguousarray(
        eu.reshape(32, 8, 128, 512).transpose(0, 2, 1, 3).reshape(32, 128, 8 * 512))
    wd = a["L0_ffn_w_down"].transpose(0, 2, 1)
    sh["w_l0edp"] = np.ascontiguousarray(
        wd.reshape(4, 8, 4, 128, 8, 128).transpose(4, 0, 3, 1, 2, 5)
        .reshape(8, 4, 128, 8 * 4 * 128))
    sg_ = np.ascontiguousarray(a["L0_ffn_sh_gate"].T)
    su_ = np.ascontiguousarray(a["L0_ffn_sh_up"].T)
    sh["w_l0sgp"] = _prew(sg_)
    sh["w_l0sup0"] = _prew(np.ascontiguousarray(su_[:, 0:512]))
    sh["w_l0sup1"] = _prew(np.ascontiguousarray(su_[:, 512:1024]))
    sd_ = a["L0_ffn_sh_down"].T
    sdr = sd_.reshape(1024, 8, 128).transpose(1, 0, 2)
    sh["w_l0sdp"] = np.ascontiguousarray(
        sdr.reshape(8, 8, 128, 128).transpose(0, 2, 1, 3).reshape(8, 128, 8 * 128))

    for nm in ("Wq", "Wk", "Wv", "Wo", "W1", "W2"):
        wl = a["L1_attn_" + nm]
        sh[f"w_{nm}_p"] = np.ascontiguousarray(np.stack([_prew(wl[l]) for l in range(4)]))
    for nm in ("bq", "bk", "bv", "bo", "b1", "b2", "ln1_s", "ln1_b", "ln2_s", "ln2_b"):
        v = a["L1_attn_" + nm]
        sh["p_" + nm] = np.ascontiguousarray(v)
    sh["p_lnf_s"] = _cols(a["L1_attn_lnf_s"])
    sh["p_lnf_b"] = _cols(a["L1_attn_lnf_b"])
    for nm, tgt in (("bq", "bq_cols"), ("bo", "bo_cols"), ("b2", "b2_cols"),
                    ("ln1_s", "ln1_s_cols"), ("ln1_b", "ln1_b_cols"),
                    ("ln2_s", "ln2_s_cols"), ("ln2_b", "ln2_b_cols")):
        sh[tgt] = np.concatenate([_cols(a["L1_attn_" + nm][l]) for l in range(4)], axis=1)
    sh["b1_cols"] = np.concatenate([_cols(a["L1_attn_b1"][l]) for l in range(4)], axis=1)
    s8 = np.zeros((8, 8 * 128), f)
    for n in range(8):
        s8[n, n * 128:(n + 1) * 128] = 1.0
    sh["c_sel8"] = s8
    sh["l1_b1_cols"] = np.concatenate(
        [_cols(a["L1_ffn_b1"][e]) for e in range(8)] +
        [_cols(a["L1_ffn_sb1"][s_]) for s_ in range(2)], axis=1)
    zsel = np.zeros((16, 8 * 128), f)
    for s_ in range(8):
        zsel[2 * s_, s_ * 128:s_ * 128 + 64] = 1.0
        zsel[2 * s_ + 1, s_ * 128 + 64:(s_ + 1) * 128] = 1.0
    sh["c_zsel"] = zsel

    sh["w_l1r"] = np.ascontiguousarray(a["L1_ffn_rW"])
    sh["b_l1r"] = np.ascontiguousarray(a["L1_ffn_rb"].reshape(1, 8))
    sh["w_l1e1t"] = np.ascontiguousarray(
        a["L1_ffn_W1"].reshape(8, 8, 128, 32, 128).transpose(0, 3, 2, 1, 4)
        .reshape(8, 32, 128, 8 * 128))
    sh["b_l1e1"] = np.ascontiguousarray(a["L1_ffn_b1"])
    sh["w_l1e2t"] = np.ascontiguousarray(
        a["L1_ffn_W2"].reshape(8, 32, 128, 8, 128).transpose(0, 3, 2, 1, 4)
        .reshape(8, 8, 128, 32 * 128))
    sh["b_l1e2"] = np.ascontiguousarray(a["L1_ffn_b2"])
    sh["w_l1s1t"] = np.ascontiguousarray(
        a["L1_ffn_sW1"].reshape(2, 8, 128, 32, 128).transpose(0, 3, 2, 1, 4)
        .reshape(2, 32, 128, 8 * 128))
    sh["b_l1s1"] = np.ascontiguousarray(a["L1_ffn_sb1"])
    sh["w_l1s2t"] = np.ascontiguousarray(
        a["L1_ffn_sW2"].reshape(2, 32, 128, 8, 128).transpose(0, 3, 2, 1, 4)
        .reshape(2, 8, 128, 32 * 128))
    sh["b_l1s2"] = np.ascontiguousarray(a["L1_ffn_sb2"])
    CPv = 1280
    lst8 = np.zeros((8, 8), f)
    for e in range(8):
        lst8[:e, e] = 1.0
    sh["c_lst8"] = lst8
    sh["c_ecol8"] = (np.arange(8, dtype=f) * CPv).reshape(8, 1)
    g2 = np.zeros((128, 2), f)
    g2[0:8, 0] = 1.0
    g2[64:72, 1] = 1.0
    sh["c_grp2"] = g2
    sh["c_dump8"] = np.tile((8 * CPv + (np.arange(4096) % 128)).astype(f)[None, :], (8, 1))
    sh["c_b2e"] = np.concatenate([_cols(a["L1_ffn_b2"][e]) for e in range(8)], axis=1)
    sh["c_sb2"] = np.concatenate([_cols(a["L1_ffn_sb2"][s_]) for s_ in range(2)], axis=1)

    x = a["L0_attn_input_tensor"]
    per_core = []
    for c in range(NCORES):
        d = {"x0": np.ascontiguousarray(x[c].T)}
        per_core.append(d)
    return sh, per_core


class Tensors:

    def __init__(self, nc, shared, dump=()):
        self.nc = nc
        self.dump = set(dump)
        self.t = {}
        for k, v in shared.items():
            dt = BF16 if v.dtype == ml_dtypes.bfloat16 else F32
            self.t[k] = nc.dram_tensor(k, list(v.shape), dt, kind="ExternalInput").ap()
        self.t["x0"] = nc.dram_tensor("x0", [D, T], F32, kind="ExternalInput").ap()
        self.t["out"] = nc.dram_tensor("out", [D, T], F32, kind="ExternalOutput").ap()

    def scratch(self, name, shape, dtype):
        kind = "ExternalOutput" if name in self.dump else "Internal"
        self.t[name] = self.nc.dram_tensor(name, list(shape), dtype, kind=kind).ap()
        return self.t[name]

    def __getitem__(self, k):
        return self.t[k]


def boot(P, ts, ctx):
    nc = P.nc
    n = len(ACT_CONSTS)
    actc = ctx.enter_context(sbuf_u(nc, "actc", [128, n], F32))
    P.dma("sync", actc[:], ts["c_actc"])
    for i, v in enumerate(ACT_CONSTS):
        nc.const_aps.register(F32, v, actc[:, i:i + 1])
    P.barrier()


PRECAST = [
    ("w_l1e1t", "b_w_l1e1t", 8 * 32, 1024),
    ("w_l1e2t", "b_w_l1e2t", 8 * 8, 4096),
    ("w_l1s1t", "b_w_l1s1t", 2 * 32, 1024),
    ("w_l1s2t", "b_w_l1s2t", 2 * 8, 4096),
    ("w_l0egp", "b_w_l0egp", 32, 4096),
    ("w_l0eup", "b_w_l0eup", 32, 4096),
    ("w_l0edp", "b_w_l0edp", 8 * 4, 4096),
    ("w_l0sgp", "b_w_l0sgp", 1, 8192),
    ("w_l0sup0", "b_w_l0sup0", 1, 4096),
    ("w_l0sup1", "b_w_l0sup1", 1, 4096),
    ("w_l0sdp", "b_w_l0sdp", 8, 1024),
]


def emit_precast(P, ts, ctx):
    nc = P.nc
    with ExitStack() as sctx:
        buf = [sctx.enter_context(sbuf_u(nc, f"pc_b{b}", [128, 8192], BF16)) for b in range(2)]
        ld2 = [P.sem("gh0"), P.sem("gh1")]
        st = P.sem("gd0")
        st_free = [0, 0]
        q = 0
        for (sk, dk, n, cols) in PRECAST:
            src = ts[sk]
            dst = ts[dk]
            for j in range(n):
                b = q % 2
                q += 1
                sap = src
                dap = dst
                if len(src.shape) == 4:
                    o0 = j // (src.shape[1]); o1 = j % (src.shape[1])
                    sap = src[o0, o1]; dap = dst[o0, o1]
                elif len(src.shape) == 3:
                    sap = src[j]; dap = dst[j]
                P.wait("gpsimd", st, st_free[b])
                P.dma("gpsimd", buf[b][:, 0:cols], sap, sem=ld2[b])
                P.wait("sync", ld2[b], P.cnt(ld2[b]))
                P.dma("sync", dap, buf[b][:, 0:cols], sem=st)
                st_free[b] = P.cnt(st)
        P.wait("sync", st, P.cnt(st))
        P.barrier()


def emit_ln_fm(P, ts, ctx, src, out_bf=None, out_f32=None, eps=1e-5,
               extra=None, tc=512):
    nc = P.nc
    nblk = D // 128
    nch = T // tc
    with ExitStack() as sctx:
        xc = sctx.enter_context(sbuf_u(nc, "ln_xc", [128, nblk * tc], F32))
        xb = sctx.enter_context(sbuf_u(nc, "ln_xb", [128, nblk * tc], BF16))
        sq = sctx.enter_context(sbuf_u(nc, "ln_sq", [128, nblk * tc], BF16))
        rows = sctx.enter_context(sbuf_u(nc, "ln_rows", [1, 6 * tc], F32))
        bcast = sctx.enter_context(sbuf_u(nc, "ln_bc", [128, 2 * tc], F32))
        onesc = sctx.enter_context(sbuf_u(nc, "ln_ones", [128, 1], BF16))
        onesr = sctx.enter_context(sbuf_u(nc, "ln_onesr", [1, 128], F32))
        obf = sctx.enter_context(sbuf_u(nc, "ln_obf", [128, nblk * tc], BF16))
        of32 = sctx.enter_context(sbuf_u(nc, "ln_of32", [128, nblk * tc], F32))
        pec = sctx.enter_context(sbuf_u(nc, "ln_pec", [128, nblk * tc], F32)) \
            if extra else None
        ps_s = sctx.enter_context(psum_u(nc, "ln_ps_s", [1, tc], F32))
        ps_s2 = sctx.enter_context(psum_u(nc, "ln_ps_s2", [1, tc], F32))
        ps_bc = sctx.enter_context(psum_u(nc, "ln_ps_bc", [128, 2 * tc], F32))

        ld = P.sem("gh0")
        pe_ld = P.sem("gd0")
        ld_base = P.cnt(ld)
        peld_base = P.cnt(pe_ld)
        ca = P.sem("gc0")
        mmA = P.sem("gc1")
        rowd = P.sem("gc2")
        bcd = P.sem("gc3")
        bce = P.sem("gc4")
        app = P.sem("gc5")
        cst = P.sem("gc6")
        st = P.sem("gd1")

        P.dma("sync", onesc[:], ts["c_ones_col_bf"], sem=ld)
        P.dma("sync", onesr[:], ts["c_ones_row"], sem=ld)

        for c in range(nch):
            t0 = c * tc
            for g in range(nblk):
                P.dma("sync", xc[:, g * tc:(g + 1) * tc],
                      src[g * 128:(g + 1) * 128, t0:t0 + tc], sem=ld)
            if extra:
                for g in range(nblk):
                    P.dma("gpsimd", pec[:, g * tc:(g + 1) * tc],
                          ts[extra[3]][g * 128:(g + 1) * 128, t0:t0 + tc], sem=pe_ld)
            ld_tgt = ld_base + 32 + (c + 1) * nblk * 16
            for g in range(nblk):
                sl = slice(g * tc, (g + 1) * tc)
                P.emit("vector", lambda e, a=xb[:, sl], b=xc[:, sl]: e.tensor_copy(out=a, in_=b),
                       wait=[(ld, ld_tgt)] if g == 0 else (), inc=[(ca, 1)])
                P.emit("scalar", lambda e, a=sq[:, sl], b=xc[:, sl]:
                       e.activation(out=a, in_=b, func=AF.Square),
                       wait=[(ld, ld_tgt)] if g == 0 else (), inc=[(ca, 1)])
            for g in range(nblk):
                sl = slice(g * tc, (g + 1) * tc)
                P.emit("tensor", lambda e, r=xb[:, sl], o=ps_s[:], s=(g == 0), p=(g == nblk - 1):
                       e.matmul(out=o, lhsT=onesc[:], rhs=r, start=s, stop=p),
                       wait=[(ca, P.cnt(ca))] if g == 0 else ())
            for g in range(nblk):
                sl = slice(g * tc, (g + 1) * tc)
                P.emit("tensor", lambda e, r=sq[:, sl], o=ps_s2[:], s=(g == 0), p=(g == nblk - 1):
                       e.matmul(out=o, lhsT=onesc[:], rhs=r, start=s, stop=p),
                       inc=[(mmA, 1)] if g == nblk - 1 else ())
            mrow = rows[:, 0 * tc:1 * tc]; ex2 = rows[:, 1 * tc:2 * tc]
            var = rows[:, 2 * tc:3 * tc]; rstd = rows[:, 3 * tc:4 * tc]
            brow = rows[:, 4 * tc:5 * tc]; tmp = rows[:, 5 * tc:6 * tc]
            P.emit("vector", lambda e, o=mrow, i=ps_s[:]:
                   e.tensor_scalar(out=o, in0=i, scalar1=1.0 / D, scalar2=None, op0=ALU.mult),
                   wait=[(mmA, P.cnt(mmA))])
            P.emit("vector", lambda e, o=ex2, i=ps_s2[:]:
                   e.tensor_scalar(out=o, in0=i, scalar1=1.0 / D, scalar2=None, op0=ALU.mult))
            P.emit("vector", lambda e, o=tmp, i=mrow: e.tensor_tensor(out=o, in0=i, in1=i, op=ALU.mult))
            P.emit("vector", lambda e, o=var, i=ex2, j=tmp: e.tensor_tensor(out=o, in0=i, in1=j, op=ALU.subtract),
                   inc=[(rowd, 1)])
            P.emit("scalar", lambda e, o=tmp, i=var: e.activation(out=o, in_=i, func=AF.Ln, bias=eps),
                   wait=[(rowd, P.cnt(rowd))])
            P.emit("scalar", lambda e, o=rstd, i=tmp: e.activation(out=o, in_=i, func=AF.Exp, scale=-0.5),
                   inc=[(rowd, 1)])
            P.emit("vector", lambda e, o=brow, i=mrow, j=rstd:
                   e.scalar_tensor_tensor(out=o, in0=i, scalar=-1.0, in1=j, op0=ALU.mult, op1=ALU.mult),
                   wait=[(rowd, P.cnt(rowd))], inc=[(rowd, 1)])
            P.emit("tensor", lambda e, o=ps_bc[:, 0:tc], r=rstd:
                   e.matmul(out=o, lhsT=onesr[:], rhs=r, start=True, stop=True),
                   wait=[(rowd, P.cnt(rowd))])
            P.emit("tensor", lambda e, o=ps_bc[:, tc:2 * tc], r=brow:
                   e.matmul(out=o, lhsT=onesr[:], rhs=r, start=True, stop=True),
                   inc=[(bcd, 1)])
            P.emit("scalar", lambda e, o=bcast[:, 0:tc], i=ps_bc[:, 0:tc]: e.copy(out=o, in_=i),
                   wait=[(bcd, P.cnt(bcd))])
            P.emit("scalar", lambda e, o=bcast[:, tc:2 * tc], i=ps_bc[:, tc:2 * tc]: e.copy(out=o, in_=i),
                   inc=[(bce, 1)])
            for g in range(nblk):
                sl = slice(g * tc, (g + 1) * tc)
                P.emit("vector", lambda e, o=of32[:, sl], i=xc[:, sl], ab=bcast[:, 0:tc]:
                       e.tensor_tensor(out=o, in0=i, in1=ab, op=ALU.mult),
                       wait=[(bce, P.cnt(bce))] if g == 0 else ())
                P.emit("vector", lambda e, o=of32[:, sl], i=of32[:, sl], bb=bcast[:, tc:2 * tc]:
                       e.tensor_tensor(out=o, in0=i, in1=bb, op=ALU.add),
                       inc=[(app, 1)])
                if out_bf is not None:
                    P.emit("scalar", lambda e, o=obf[:, sl], i=of32[:, sl], w=P.cnt(app):
                           e.copy(out=o, in_=i),
                           wait=[(app, P.cnt(app))], inc=[(cst, 1)])
            if extra:
                _, out_name, alpha, _ = extra
                P.wait("vector", app, P.cnt(app))
                for g in range(nblk):
                    sl = slice(g * tc, (g + 1) * tc)
                    P.emit("vector", lambda e, o=xc[:, sl], i=of32[:, sl], p=pec[:, sl], al=alpha:
                           e.scalar_tensor_tensor(out=o, in0=i, scalar=al, in1=p, op0=ALU.mult, op1=ALU.add),
                           wait=[(pe_ld, peld_base + (c + 1) * nblk * 16)] if g == 0 else (),
                           inc=[(app, 1)])
            app_now = P.cnt(app)
            cst_now = P.cnt(cst)
            for g in range(nblk):
                sl = slice(g * tc, (g + 1) * tc)
                if out_f32 is not None:
                    P.wait("gpsimd", app, app_now)
                    P.dma("gpsimd", out_f32[g * 128:(g + 1) * 128, t0:t0 + tc], of32[:, sl], sem=st)
                if out_bf is not None:
                    P.wait("gpsimd", cst, cst_now)
                    P.dma("gpsimd", out_bf[g * 128:(g + 1) * 128, t0:t0 + tc], obf[:, sl], sem=st)
                if extra:
                    P.wait("gpsimd", app, app_now)
                    P.dma("gpsimd", ts[extra[1]][g * 128:(g + 1) * 128, t0:t0 + tc], xc[:, sl], sem=st)
            P.wait("sync", st, P.cnt(st))
            if extra:
                P.wait("gpsimd", st, P.cnt(st))
        P.wait("sync", st, P.cnt(st))
        P.barrier()


def emit_proj(P, ctx, *, w, Kdim, Idim, rhs, outs, bias=None, func=None,
              res=None, tc=512, rhs_is_bf=True, tag="pj"):
    nc = P.nc
    nk = (Kdim + 127) // 128
    ni = (Idim + 127) // 128
    nch = T // tc
    with ExitStack() as sctx:
        rhs_sb = sctx.enter_context(sbuf_u(nc, tag + "_rhs", [128, nk * tc], BF16))
        w_sb = [sctx.enter_context(sbuf_u(nc, f"{tag}_w{b}", [128, nk * 128], BF16))
                for b in range(2)]
        o_sb = sctx.enter_context(sbuf_u(nc, tag + "_o", [128, 2 * tc], F32))
        ob_sb = sctx.enter_context(sbuf_u(nc, tag + "_ob", [128, 2 * tc], BF16))
        r_sb = sctx.enter_context(sbuf_u(nc, tag + "_r", [128, 2 * tc], F32)) \
            if res is not None else None
        e1_sb = sctx.enter_context(sbuf_u(nc, tag + "_e1", [128, 2 * tc], F32)) \
            if func == "elu1" else None
        b_sb = sctx.enter_context(sbuf_u(nc, tag + "_b", [128, max(ni, 1)], F32)) \
            if bias is not None else None
        ps = [sctx.enter_context(psum_u(nc, f"{tag}_ps{b}", [128, tc], F32))
              for b in range(2)]

        lw2 = [P.sem("gd0"), P.sem("gd1")]
        pbs = P.sem("gh0")
        lr = P.sem("gd3")
        lres2 = [P.sem("gh1"), P.sem("gh2")]
        mm = P.sem("gc1")
        ev = P.sem("gc2")
        st = P.sem("gd2")

        pbs_t = 0
        if bias is not None:
            _, pbs_t = P.dma("sync", b_sb[:, 0:ni], bias[:, 0:ni], sem=pbs)

        psum_free = [P.cnt(ev)] * 2
        wbuf_free = [P.cnt(ev)] * 2
        osb_free = [P.cnt(st)] * 2
        res_free = [P.cnt(ev)] * 2
        last_mm_chunk = P.cnt(mm)
        first_evac = True
        for c in range(nch):
            t0 = c * tc
            if c > 0:
                P.wait("gpsimd", mm, last_mm_chunk)
            if Kdim >= 128:
                P.dma("gpsimd", rhs_sb[:].rearrange("p (c t) -> p c t", c=nk),
                      rhs[:, t0:t0 + tc].rearrange("(c p) t -> p c t", p=128), sem=lr)
            else:
                P.dma("gpsimd", rhs_sb[0:Kdim, 0:tc], rhs[:, t0:t0 + tc], sem=lr)
            lr_t = P.cnt(lr)
            for i in range(ni):
                isz = min(128, Idim - i * 128)
                wb = i % 2
                P.wait("gpsimd", ev, wbuf_free[wb])
                if Kdim >= 128:
                    P.dma("gpsimd",
                          w_sb[wb][:].rearrange("p (c i) -> p c i", c=nk)[:, :, 0:isz],
                          w[:, i * 128:i * 128 + isz].rearrange("(c p) i -> p c i", p=128),
                          sem=lw2[wb])
                else:
                    P.dma("gpsimd", w_sb[wb][0:Kdim, 0:isz],
                          w[:, i * 128:i * 128 + isz], sem=lw2[wb])
                lw_s, lw_t = lw2[wb], P.cnt(lw2[wb])
                if res is not None:
                    P.wait("sync", ev, res_free[i % 2])
                    P.dma("sync", r_sb[:, (i % 2) * tc:(i % 2) * tc + tc][0:isz, :],
                          res[i * 128:i * 128 + isz, t0:t0 + tc], sem=lres2[i % 2])
                pb = i % 2
                P.wait("tensor", ev, psum_free[pb])
                for k in range(nk):
                    ksz = min(128, Kdim - k * 128)
                    P.emit("tensor", lambda e, o=ps[pb][0:isz, :],
                           lt=w_sb[wb][0:ksz, k * 128:k * 128 + isz],
                           rt=rhs_sb[0:ksz, k * tc:(k + 1) * tc],
                           s=(k == 0), p_=(k == nk - 1):
                           e.matmul(out=o, lhsT=lt, rhs=rt, start=s, stop=p_),
                           wait=[(lw_s, lw_t), (lr, lr_t)] if k == 0 else (),
                           inc=[(mm, 1)] if k == nk - 1 else ())
                last_mm_chunk = P.cnt(mm)
                bcol = b_sb[0:isz, i:i + 1] if bias is not None else None
                osl = o_sb[:, pb * tc:(pb + 1) * tc][0:isz, :]
                obsl = ob_sb[:, pb * tc:(pb + 1) * tc][0:isz, :]
                wait_ev = [(mm, P.cnt(mm)), (st, osb_free[pb])]
                if first_evac and bias is not None:
                    wait_ev.append((pbs, pbs_t))
                first_evac = False
                if res is not None:
                    wait_ev.append((lres2[i % 2], P.cnt(lres2[i % 2])))
                want_f32 = any(dt == F32 for (_, dt) in outs)
                want_bf = any(dt == BF16 for (_, dt) in outs)
                ffunc = func if func is not None else AF.Identity
                first_out = osl if want_f32 else obsl
                if func == "elu1":
                    vs = osl
                    es = e1_sb[:, pb * tc:(pb + 1) * tc][0:isz, :]
                    if bcol is not None:
                        P.emit("vector", lambda e, o=vs, i_=ps[pb][0:isz, :], b=bcol:
                               e.tensor_scalar(out=o, in0=i_, scalar1=b, scalar2=None, op0=ALU.add),
                               wait=wait_ev)
                    else:
                        P.emit("vector", lambda e, o=vs, i_=ps[pb][0:isz, :]:
                               e.tensor_copy(out=o, in_=i_), wait=wait_ev)
                    P.emit("vector", lambda e, o=es, i_=vs:
                           e.tensor_scalar(out=o, in0=i_, scalar1=0.0, scalar2=None, op0=ALU.min),
                           inc=[(ev, 1)])
                    P.emit("scalar", lambda e, o=es, i_=es:
                           e.activation(out=o, in_=i_, func=AF.Exp),
                           wait=[(ev, P.cnt(ev))], inc=[(ev, 1)])
                    P.emit("vector", lambda e, o=obsl, i_=vs, l=es:
                           e.scalar_tensor_tensor(out=o, in0=i_, scalar=0.0, in1=l,
                                                  op0=ALU.max, op1=ALU.add),
                           wait=[(ev, P.cnt(ev))], inc=[(ev, 1)])
                elif res is None:
                    if bcol is not None:
                        P.emit("scalar", lambda e, o=first_out, i_=ps[pb][0:isz, :], bc=bcol, fn=ffunc:
                               e.activation(out=o, in_=i_, func=fn, bias=bc),
                               wait=wait_ev, inc=[(ev, 1)])
                    else:
                        P.emit("scalar", lambda e, o=first_out, i_=ps[pb][0:isz, :], fn=ffunc:
                               e.activation(out=o, in_=i_, func=fn),
                               wait=wait_ev, inc=[(ev, 1)])
                else:
                    rsl = r_sb[:, (i % 2) * tc:(i % 2) * tc + tc][0:isz, :]
                    if bcol is not None:
                        P.emit("vector", lambda e, o=first_out, i_=ps[pb][0:isz, :], bc=bcol, rr=rsl:
                               e.scalar_tensor_tensor(out=o, in0=i_, scalar=bc, in1=rr,
                                                      op0=ALU.add, op1=ALU.add),
                               wait=wait_ev, inc=[(ev, 1)])
                    else:
                        P.emit("vector", lambda e, o=first_out, i_=ps[pb][0:isz, :], rr=rsl:
                               e.tensor_tensor(out=o, in0=i_, in1=rr, op=ALU.add),
                               wait=wait_ev, inc=[(ev, 1)])
                if want_f32 and want_bf:
                    P.emit("scalar", lambda e, o=obsl, i_=osl:
                           e.copy(out=o, in_=i_),
                           wait=[(ev, P.cnt(ev))], inc=[(ev, 1)])
                psum_free[pb] = P.cnt(ev)
                wbuf_free[wb] = P.cnt(ev)
                res_free[i % 2] = P.cnt(ev)
                ev_t = P.cnt(ev)
                P.wait("gpsimd", ev, ev_t)
                for (ap, dt_) in outs:
                    src = osl if dt_ == F32 else obsl
                    P.dma("gpsimd", ap[i * 128:i * 128 + isz, t0:t0 + tc], src, sem=st)
                osb_free[pb] = P.cnt(st)
        P.wait("sync", st, P.cnt(st))
        P.wait("gpsimd", st, P.cnt(st))
        P.barrier()


def emit_proj2(P, ctx, *, w, Kdim, Idim, rhs, outs, bias=None, func=None,
               res=None, tc=512, tag="pk"):
    nc = P.nc
    nk = (Kdim + 127) // 128
    ni = (Idim + 127) // 128
    nch = T // tc
    rhs_res = Kdim <= 1024
    with ExitStack() as sctx:
        w_sb = sctx.enter_context(sbuf_u(nc, tag + "_w", [128, nk * Idim], BF16))
        if rhs_res:
            rhs_sb = [sctx.enter_context(sbuf_u(nc, tag + "_rhsr", [128, nk * T], BF16))]
        else:
            rhs_sb = [sctx.enter_context(sbuf_u(nc, f"{tag}_rhs{b}", [128, nk * tc], BF16))
                      for b in range(2)]
        o_sb = sctx.enter_context(sbuf_u(nc, tag + "_o", [128, 2 * tc], F32))
        ob_sb = sctx.enter_context(sbuf_u(nc, tag + "_ob", [128, 2 * tc], BF16))
        r_sb = sctx.enter_context(sbuf_u(nc, tag + "_r", [128, 2 * tc], F32)) \
            if res is not None else None
        e1_sb = sctx.enter_context(sbuf_u(nc, tag + "_e1", [128, 2 * tc], F32)) \
            if func == "elu1" else None
        b_sb = sctx.enter_context(sbuf_u(nc, tag + "_b", [128, max(ni, 1)], F32)) \
            if bias is not None else None
        ps = [sctx.enter_context(psum_u(nc, f"{tag}_ps{b}", [128, tc], F32))
              for b in range(2)]

        lw = P.sem("gd0")
        lr2 = [P.sem("gh3"), P.sem("gh4")]
        pbs = P.sem("gh0")
        lres2 = [P.sem("gh1"), P.sem("gh2")]
        mm = P.sem("gc1")
        ev = P.sem("gc2")
        st = P.sem("gd2")

        _, lw_t = P.dma("gpsimd", w_sb[:], w, sem=lw)
        pbs_t = 0
        if bias is not None:
            _, pbs_t = P.dma("sync", b_sb[:, 0:ni], bias[:, 0:ni], sem=pbs)

        psum_free = [P.cnt(ev)] * 2
        osb_free = [P.cnt(st)] * 2
        res_free = [P.cnt(ev)] * 2
        rhs_free = [P.cnt(mm)] * 2
        lr_t = [0, 0]
        first_evac = True

        def load_rhs(c):
            cb = c % 2
            P.wait("sync", mm, rhs_free[cb])
            P.dma("sync", rhs_sb[cb][:].rearrange("p (c t) -> p c t", c=nk),
                  rhs[:, c * tc:(c + 1) * tc].rearrange("(c p) t -> p c t", p=128),
                  sem=lr2[cb])
            lr_t[cb] = P.cnt(lr2[cb])

        if rhs_res:
            P.dma("sync", rhs_sb[0][:].rearrange("p (c t) -> p c t", c=nk),
                  rhs.rearrange("(c p) t -> p c t", p=128), sem=lr2[0])
            lr_t[0] = P.cnt(lr2[0])
        else:
            load_rhs(0)
        for c in range(nch):
            cb = 0 if rhs_res else c % 2
            t0 = c * tc
            if not rhs_res and c + 1 < nch:
                load_rhs(c + 1)
            for i in range(ni):
                isz = min(128, Idim - i * 128)
                if res is not None:
                    P.wait("sync", ev, res_free[i % 2])
                    P.dma("sync", r_sb[:, (i % 2) * tc:(i % 2) * tc + tc][0:isz, :],
                          res[i * 128:i * 128 + isz, t0:t0 + tc], sem=lres2[i % 2])
                pb = i % 2
                P.wait("tensor", ev, psum_free[pb])
                for k in range(nk):
                    ksz = min(128, Kdim - k * 128)
                    if rhs_res:
                        rt_ = rhs_sb[0][0:ksz, k * T + t0:k * T + t0 + tc]
                    else:
                        rt_ = rhs_sb[cb][0:ksz, k * tc:(k + 1) * tc]
                    P.emit("tensor", lambda e, o=ps[pb][0:isz, :],
                           lt=w_sb[0:ksz, k * Idim + i * 128:k * Idim + i * 128 + isz],
                           rt=rt_,
                           s=(k == 0), p_=(k == nk - 1):
                           e.matmul(out=o, lhsT=lt, rhs=rt, start=s, stop=p_),
                           wait=[(lw, lw_t), (lr2[cb], lr_t[cb])] if k == 0 else (),
                           inc=[(mm, 1)] if k == nk - 1 else ())
                rhs_free[cb] = P.cnt(mm)
                bcol = b_sb[0:isz, i:i + 1] if bias is not None else None
                osl = o_sb[:, pb * tc:(pb + 1) * tc][0:isz, :]
                obsl = ob_sb[:, pb * tc:(pb + 1) * tc][0:isz, :]
                wait_ev = [(mm, P.cnt(mm)), (st, osb_free[pb])]
                if first_evac and bias is not None:
                    wait_ev.append((pbs, pbs_t))
                first_evac = False
                if res is not None:
                    wait_ev.append((lres2[i % 2], P.cnt(lres2[i % 2])))
                want_f32 = any(dt == F32 for (_, dt) in outs)
                want_bf = any(dt == BF16 for (_, dt) in outs)
                ffunc = func if func is not None else AF.Identity
                first_out = osl if want_f32 else obsl
                if func == "elu1":
                    vs = osl
                    es = e1_sb[:, pb * tc:(pb + 1) * tc][0:isz, :]
                    if bcol is not None:
                        P.emit("vector", lambda e, o=vs, i_=ps[pb][0:isz, :], b=bcol:
                               e.tensor_scalar(out=o, in0=i_, scalar1=b, scalar2=None, op0=ALU.add),
                               wait=wait_ev)
                    else:
                        P.emit("vector", lambda e, o=vs, i_=ps[pb][0:isz, :]:
                               e.tensor_copy(out=o, in_=i_), wait=wait_ev)
                    P.emit("vector", lambda e, o=es, i_=vs:
                           e.tensor_scalar(out=o, in0=i_, scalar1=0.0, scalar2=None, op0=ALU.min),
                           inc=[(ev, 1)])
                    P.emit("scalar", lambda e, o=es, i_=es:
                           e.activation(out=o, in_=i_, func=AF.Exp),
                           wait=[(ev, P.cnt(ev))], inc=[(ev, 1)])
                    P.emit("vector", lambda e, o=obsl, i_=vs, l=es:
                           e.scalar_tensor_tensor(out=o, in0=i_, scalar=0.0, in1=l,
                                                  op0=ALU.max, op1=ALU.add),
                           wait=[(ev, P.cnt(ev))], inc=[(ev, 1)])
                elif res is None:
                    if bcol is not None:
                        P.emit("scalar", lambda e, o=first_out, i_=ps[pb][0:isz, :], bc=bcol, fn=ffunc:
                               e.activation(out=o, in_=i_, func=fn, bias=bc),
                               wait=wait_ev, inc=[(ev, 1)])
                    else:
                        P.emit("scalar", lambda e, o=first_out, i_=ps[pb][0:isz, :], fn=ffunc:
                               e.activation(out=o, in_=i_, func=fn),
                               wait=wait_ev, inc=[(ev, 1)])
                else:
                    rsl = r_sb[:, (i % 2) * tc:(i % 2) * tc + tc][0:isz, :]
                    if bcol is not None:
                        P.emit("vector", lambda e, o=first_out, i_=ps[pb][0:isz, :], bc=bcol, rr=rsl:
                               e.scalar_tensor_tensor(out=o, in0=i_, scalar=bc, in1=rr,
                                                      op0=ALU.add, op1=ALU.add),
                               wait=wait_ev, inc=[(ev, 1)])
                    else:
                        P.emit("vector", lambda e, o=first_out, i_=ps[pb][0:isz, :], rr=rsl:
                               e.tensor_tensor(out=o, in0=i_, in1=rr, op=ALU.add),
                               wait=wait_ev, inc=[(ev, 1)])
                if want_f32 and want_bf:
                    P.emit("scalar", lambda e, o=obsl, i_=osl:
                           e.copy(out=o, in_=i_),
                           wait=[(ev, P.cnt(ev))], inc=[(ev, 1)])
                psum_free[pb] = P.cnt(ev)
                res_free[i % 2] = P.cnt(ev)
                ev_t = P.cnt(ev)
                P.wait("gpsimd", ev, ev_t)
                for (ap, dt_) in outs:
                    src = osl if dt_ == F32 else obsl
                    P.dma("gpsimd", ap[i * 128:i * 128 + isz, t0:t0 + tc], src, sem=st)
                osb_free[pb] = P.cnt(st)
        P.wait("sync", st, P.cnt(st))
        P.wait("gpsimd", st, P.cnt(st))
        P.barrier()


def emit_l0_conv(P, ts, ctx, tc=512):
    nc = P.nc
    nblk = DI // 128
    nch = T // tc
    with ExitStack() as sctx:
        bufz = sctx.enter_context(sbuf_u(nc, "cv_bz", [128, tc + 3], F32))
        bufn = [sctx.enter_context(sbuf_u(nc, f"cv_bn{b}", [128, tc + 3], F32))
                for b in range(2)]
        acc = sctx.enter_context(sbuf_u(nc, "cv_acc", [128, 2 * tc], F32))
        uo = sctx.enter_context(sbuf_u(nc, "cv_uo", [128, 2 * tc], F32))
        ub = sctx.enter_context(sbuf_u(nc, "cv_ub", [128, 2 * tc], BF16))
        wc = sctx.enter_context(sbuf_u(nc, "cv_w", [128, 5 * (DI // 128)], F32))
        ldc = P.sem("gh0")
        ldb = {}
        dn = P.sem("gc0")
        sl_ = P.sem("gc1")
        st = P.sem("gd0")

        for j in range(4):
            P.dma("sync", wc[:, j * nblk:(j + 1) * nblk], ts[f"w_conv{j}"], sem=ldc)
        _, ldc_t = P.dma("sync", wc[:, 4 * nblk:5 * nblk], ts["b_conv"], sem=ldc)
        for i_, b_ in enumerate((id(bufz), id(bufn[0]), id(bufn[1]))):
            ldb[b_] = P.sem(f"cv_lb{i_}")
        P.emit("vector", lambda e: e.memset(bufz[:, 0:3], 0.0), inc=[(dn, 1)])
        P.barrier()

        buf_free = {}
        out_free = [0, 0]
        out_st = [0, 0]
        for g in range(nblk):
            for c in range(nch):
                t0 = c * tc
                if c == 0:
                    buf = bufz
                    dst = bufz[:, 3:3 + tc]
                    srcap = ts["u_pre"][g * 128:(g + 1) * 128, 0:tc]
                else:
                    buf = bufn[c % 2]
                    dst = buf[:]
                    srcap = ts["u_pre"][g * 128:(g + 1) * 128, t0 - 3:t0 + tc]
                bid = id(buf)
                P.wait("sync", dn, buf_free.get(bid, 0))
                P.dma("sync", dst, srcap, sem=ldb[bid])
                ld_s, ld_t = ldb[bid], P.cnt(ldb[bid])
                ob = c % 2
                asl = acc[:, ob * tc:(ob + 1) * tc]
                P.emit("vector", lambda e, o=asl, i_=buf[:, 0:tc], w0=wc[:, 0 * nblk + g:0 * nblk + g + 1]:
                       e.tensor_scalar(out=o, in0=i_, scalar1=w0, scalar2=None, op0=ALU.mult),
                       wait=[(ld_s, ld_t), (ldc, ldc_t), (sl_, out_free[ob])])
                for j in range(1, 4):
                    P.emit("vector", lambda e, o=asl, i_=buf[:, j:j + tc],
                           wj=wc[:, j * nblk + g:j * nblk + g + 1], a=asl:
                           e.scalar_tensor_tensor(out=o, in0=i_, scalar=wj, in1=a,
                                                  op0=ALU.mult, op1=ALU.add),
                           inc=[(dn, 1)] if j == 3 else ())
                buf_free[bid] = P.cnt(dn)
                usl = uo[:, ob * tc:(ob + 1) * tc]
                ubsl = ub[:, ob * tc:(ob + 1) * tc]
                P.emit("scalar", lambda e, o=usl, i_=asl, b=wc[:, 4 * nblk + g:4 * nblk + g + 1]:
                       e.activation(out=o, in_=i_, func=AF.Silu, bias=b),
                       wait=[(dn, P.cnt(dn)), (st, out_st[ob])])
                P.emit("scalar", lambda e, o=ubsl, i_=usl: e.copy(out=o, in_=i_),
                       inc=[(sl_, 1)])
                out_free[ob] = P.cnt(sl_)
                sl_t = P.cnt(sl_)
                P.wait("gpsimd", sl_, sl_t)
                P.dma("gpsimd", ts["u_f32"][g * 128:(g + 1) * 128, t0:t0 + tc], usl, sem=st)
                P.dma("gpsimd", ts["u_bf"][g * 128:(g + 1) * 128, t0:t0 + tc], ubsl, sem=st)
                out_st[ob] = P.cnt(st)
        P.wait("gpsimd", st, P.cnt(st))
        P.barrier()


def emit_l0_delta(P, ts, ctx, tc=512):
    nc = P.nc
    nblk = DI // 128
    nch = T // tc
    with ExitStack() as sctx:
        rhs_sb = sctx.enter_context(sbuf_u(nc, "dl_rhs", [DTR, tc], BF16))
        w_sb = sctx.enter_context(sbuf_u(nc, "dl_w", [DTR, DI], BF16))
        bcol = sctx.enter_context(sbuf_u(nc, "dl_b", [128, nblk], F32))
        v_sb = sctx.enter_context(sbuf_u(nc, "dl_v", [128, 2 * tc], F32))
        av_sb = sctx.enter_context(sbuf_u(nc, "dl_av", [128, 2 * tc], F32))
        e_sb = sctx.enter_context(sbuf_u(nc, "dl_e", [128, 2 * tc], F32))
        sp_sb = sctx.enter_context(sbuf_u(nc, "dl_sp", [128, 2 * tc], F32))
        du_sb = sctx.enter_context(sbuf_u(nc, "dl_du", [128, 2 * tc], F32))
        u_sb = sctx.enter_context(sbuf_u(nc, "dl_u", [128, 2 * tc], F32))
        ps = [sctx.enter_context(psum_u(nc, f"dl_ps{b}", [128, tc], F32)) for b in range(2)]
        ld = P.sem("gd0")
        lu2 = [P.sem("gh0"), P.sem("gh1")]
        mm = P.sem("gc0")
        ev = P.sem("gc1")
        sc = P.sem("gc2")
        fin = P.sem("gc3")
        st = P.sem("gd1")

        P.dma("gpsimd", w_sb[:], ts["w_dtproj"], sem=ld)
        P.dma("gpsimd", bcol[:], ts["b_dtproj"], sem=ld)
        ps_free = [0, 0]
        buf_free = [0, 0]
        u_free = [0, 0]
        for c in range(nch):
            t0 = c * tc
            P.wait("gpsimd", mm, P.cnt(mm))
            P.dma("gpsimd", rhs_sb[:], ts["dtbc"][0:DTR, t0:t0 + tc], sem=ld)
            ld_t = P.cnt(ld)
            for g in range(nblk):
                pb = g % 2
                P.wait("tensor", ev, ps_free[pb])
                P.emit("tensor", lambda e, o=ps[pb][:], lt=w_sb[:, g * 128:(g + 1) * 128], rt=rhs_sb[:]:
                       e.matmul(out=o, lhsT=lt, rhs=rt, start=True, stop=True),
                       wait=[(ld, ld_t)], inc=[(mm, 1)])
                P.wait("sync", fin, u_free[pb])
                P.dma("sync", u_sb[:, pb * tc:(pb + 1) * tc],
                      ts["u_f32"][g * 128:(g + 1) * 128, t0:t0 + tc], sem=lu2[pb])
                bc = bcol[:, g:g + 1]
                vs = v_sb[:, pb * tc:(pb + 1) * tc]
                avs = av_sb[:, pb * tc:(pb + 1) * tc]
                es = e_sb[:, pb * tc:(pb + 1) * tc]
                sps = sp_sb[:, pb * tc:(pb + 1) * tc]
                dus = du_sb[:, pb * tc:(pb + 1) * tc]
                us = u_sb[:, pb * tc:(pb + 1) * tc]
                mm_t = P.cnt(mm)
                st_guard = buf_free[pb]
                P.emit("vector", lambda e, o=vs, i_=ps[pb][:], b=bc:
                       e.tensor_scalar(out=o, in0=i_, scalar1=b, scalar2=None, op0=ALU.add),
                       wait=[(mm, mm_t), (st, st_guard)])
                P.emit("vector", lambda e, o=avs, i_=vs:
                       e.scalar_tensor_tensor(out=o, in0=i_, scalar=-1.0, in1=i_,
                                              op0=ALU.mult, op1=ALU.max),
                       inc=[(ev, 1)])
                ps_free[pb] = P.cnt(ev)
                P.emit("scalar", lambda e, o=es, i_=avs:
                       e.activation(out=o, in_=i_, func=AF.Exp, scale=-1.0),
                       wait=[(ev, P.cnt(ev))])
                P.emit("scalar", lambda e, o=es, i_=es:
                       e.activation(out=o, in_=i_, func=AF.Ln, bias=1.0),
                       inc=[(sc, 1)])
                P.emit("vector", lambda e, o=sps, i_=vs, l=es:
                       e.scalar_tensor_tensor(out=o, in0=i_, scalar=0.0, in1=l,
                                              op0=ALU.max, op1=ALU.add),
                       wait=[(sc, P.cnt(sc))])
                P.emit("vector", lambda e, o=dus, i_=sps, u_=us:
                       e.tensor_tensor(out=o, in0=i_, in1=u_, op=ALU.mult),
                       wait=[(lu2[pb], P.cnt(lu2[pb]))], inc=[(fin, 1)])
                u_free[pb] = P.cnt(fin)
                fin_t = P.cnt(fin)
                P.wait("gpsimd", fin, fin_t)
                P.dma("gpsimd", ts["delta"][g * 128:(g + 1) * 128, t0:t0 + tc], sps, sem=st)
                P.dma("gpsimd", ts["du"][g * 128:(g + 1) * 128, t0:t0 + tc], dus, sem=st)
                buf_free[pb] = P.cnt(st)
        P.wait("gpsimd", st, P.cnt(st))
        P.barrier()


def emit_l0_scan(P, ts, ctx, tc=512, dbg=False):
    nc = P.nc
    nblk = DI // 128
    nch = T // tc
    with ExitStack() as sctx:
        A_sb = sctx.enter_context(sbuf_u(nc, "sc_A", [128, nblk * DS], F32))
        hcol = sctx.enter_context(sbuf_u(nc, "sc_h0", [128, nblk * DS], F32))
        sel = sctx.enter_context(sbuf_u(nc, "sc_sel", [32, 32 * 128], F32))
        bcB = sctx.enter_context(sbuf_u(nc, "sc_bcB", [128, DS * tc], F32))
        bcC = sctx.enter_context(sbuf_u(nc, "sc_bcC", [128, DS * tc], F32))
        BC = sctx.enter_context(sbuf_u(nc, "sc_BC", [32, tc], F32))
        dts = sctx.enter_context(sbuf_u(nc, "sc_dt", [128, 2 * tc], F32))
        dus = sctx.enter_context(sbuf_u(nc, "sc_du", [128, 2 * tc], F32))
        a_sb = sctx.enter_context(sbuf_u(nc, "sc_a", [128, 2 * tc], F32))
        b_sb = sctx.enter_context(sbuf_u(nc, "sc_b", [128, 2 * tc], F32))
        h_sb = sctx.enter_context(sbuf_u(nc, "sc_hh", [128, 2 * tc], F32))
        hc_sb = sctx.enter_context(sbuf_u(nc, "sc_hc", [128, tc], F32))
        y_sb = sctx.enter_context(sbuf_u(nc, "sc_y", [128, 2 * tc], F32))
        ps = [sctx.enter_context(psum_u(nc, f"sc_ps{b}", [128, tc], F32)) for b in range(2)]
        ldboot = P.sem("gh0")
        lbc = P.sem("gh1")
        ldg2 = [P.sem("gd0"), P.sem("gd1")]
        bca = P.sem("gc0")
        bcm = P.sem("gc1")
        adn = P.sem("gc2")
        sdn = P.sem("gc3")
        ydn = P.sem("gc4")
        st = P.sem("gd2")

        P.dma("sync", A_sb[:], ts["a_log"], sem=ldboot)
        P.dma("sync", sel[:], ts["c_sel32"], sem=ldboot)
        P.wait("scalar", ldboot, P.cnt(ldboot))
        P.wait("vector", ldboot, P.cnt(ldboot))
        P.emit("scalar", lambda e: e.activation(out=A_sb[:], in_=A_sb[:], func=AF.Exp))
        P.emit("vector", lambda e: e.memset(hcol[:], 0.0))
        P.barrier()
        P.emit("vector", lambda e: e.tensor_scalar(out=A_sb[:], in0=A_sb[:], scalar1=-1.0,
                                                   scalar2=None, op0=ALU.mult))
        P.barrier()

        dt_free = [0, 0]
        ab_free = [0, 0]
        y_free = [0, 0]
        for c in range(nch):
            t0 = c * tc
            P.wait("sync", bcm, P.cnt(bcm))
            P.dma("sync", BC[:], ts["dtbc"][DTR:DTR + 32, t0:t0 + tc], sem=lbc)
            ld_bc = P.cnt(lbc)
            if c > 0:
                P.wait("scalar", ydn, P.cnt(ydn))
            bca_base = P.cnt(bca)
            for q in range(2 * DS):
                n, isC = q // 2, q % 2
                which = (DS + n) if isC else n
                dst = bcC if isC else bcB
                pb = q % 2
                wt = [(lbc, ld_bc)] if q < 2 else [(bca, bca_base + q - 1)]
                P.emit("tensor", lambda e, o=ps[pb][:], lt=sel[:, which * 128:(which + 1) * 128], r=BC[:]:
                       e.matmul(out=o, lhsT=lt, rhs=r, start=True, stop=True),
                       wait=wt, inc=[(bcm, 1)])
                P.emit("scalar", lambda e, o=dst[:, n * tc:(n + 1) * tc], i_=ps[pb][:]:
                       e.copy(out=o, in_=i_),
                       wait=[(bcm, P.cnt(bcm))], inc=[(bca, 1)])
            bca_t = P.cnt(bca)
            for g in range(nblk):
                gb = g % 2
                P.wait("gpsimd", sdn, dt_free[gb])
                P.dma("gpsimd", dts[:, gb * tc:(gb + 1) * tc],
                      ts["delta"][g * 128:(g + 1) * 128, t0:t0 + tc], sem=ldg2[gb])
                P.dma("gpsimd", dus[:, gb * tc:(gb + 1) * tc],
                      ts["du"][g * 128:(g + 1) * 128, t0:t0 + tc], sem=ldg2[gb])
                ld_s, ld_t = ldg2[gb], P.cnt(ldg2[gb])
                for n in range(DS):
                    p = g * DS + n
                    ab = (g * DS + n) % 2
                    asl = a_sb[:, ab * tc:(ab + 1) * tc]
                    bsl = b_sb[:, ab * tc:(ab + 1) * tc]
                    hsl = h_sb[:, ab * tc:(ab + 1) * tc]
                    P.emit("scalar", lambda e, o=asl, i_=dts[:, gb * tc:(gb + 1) * tc],
                           s=A_sb[:, p:p + 1]:
                           e.activation(out=o, in_=i_, func=AF.Exp, scale=s),
                           wait=[(ld_s, ld_t), (sdn, ab_free[ab])], inc=[(adn, 1)])
                    P.emit("vector", lambda e, o=bsl, i_=dus[:, gb * tc:(gb + 1) * tc],
                           bb=bcB[:, n * tc:(n + 1) * tc]:
                           e.tensor_tensor(out=o, in0=i_, in1=bb, op=ALU.mult),
                           wait=[(bca, bca_t), (ld_s, ld_t), (sdn, ab_free[ab])])
                    P.emit("vector", lambda e, o=hsl, a=asl, b=bsl, init=hcol[:, p:p + 1]:
                           e.tensor_tensor_scan(out=o, data0=a, data1=b, initial=init,
                                                op0=ALU.mult, op1=ALU.add),
                           wait=[(adn, P.cnt(adn))], inc=[(sdn, 1)])
                    P.wait("vector", sdn, P.cnt(sdn))
                    ab_free[ab] = P.cnt(sdn)
                    if dbg and c == 0 and g == 1 and n < 4:
                        for slot, src in ((4 * n + 0, asl), (4 * n + 1, bsl), (4 * n + 2, hsl),
                                          (4 * n + 3, bcB[:, n * tc:(n + 1) * tc])):
                            P.dma("gpsimd", ts["dbg"][slot * 128:slot * 128 + 128, 0:tc], src, sem=st)
                        P.wait("vector", st, P.cnt(st))
                        P.wait("scalar", st, P.cnt(st))
                    P.emit("vector", lambda e, o=hcol[:, p:p + 1], i_=hsl:
                           e.tensor_copy(out=o, in_=i_[:, tc - 1:tc]))
                    yb = g % 2
                    ysl = y_sb[:, yb * tc:(yb + 1) * tc]
                    P.emit("vector", lambda e, o=hc_sb[:], i_=hsl, cc=bcC[:, n * tc:(n + 1) * tc]:
                           e.tensor_tensor(out=o, in0=i_, in1=cc, op=ALU.mult),
                           wait=[(st, y_free[yb])] if n == 0 else ())
                    if n == 0:
                        P.emit("vector", lambda e, o=ysl, i_=hc_sb[:]:
                               e.tensor_copy(out=o, in_=i_))
                    else:
                        P.emit("vector", lambda e, o=ysl, i_=hc_sb[:]:
                               e.tensor_tensor(out=o, in0=o, in1=i_, op=ALU.add),
                               inc=[(ydn, 1)] if n == DS - 1 else ())
                yb = g % 2
                P.wait("gpsimd", ydn, P.cnt(ydn))
                P.dma("gpsimd", ts["y_scan"][g * 128:(g + 1) * 128, t0:t0 + tc],
                      y_sb[:, yb * tc:(yb + 1) * tc], sem=st)
                y_free[yb] = P.cnt(st)
                dt_free[gb] = P.cnt(sdn)
        P.wait("gpsimd", st, P.cnt(st))
        P.barrier()


def emit_l0_gate(P, ts, ctx, tc=512):
    nc = P.nc
    nblk = DI // 128
    nch = T // tc
    with ExitStack() as sctx:
        y_sb = sctx.enter_context(sbuf_u(nc, "gt_y", [128, 2 * tc], F32))
        u_sb = sctx.enter_context(sbuf_u(nc, "gt_u", [128, 2 * tc], F32))
        z_sb = sctx.enter_context(sbuf_u(nc, "gt_z", [128, 2 * tc], F32))
        sz_sb = sctx.enter_context(sbuf_u(nc, "gt_sz", [128, 2 * tc], F32))
        t_sb = sctx.enter_context(sbuf_u(nc, "gt_t", [128, 2 * tc], F32))
        o_sb = sctx.enter_context(sbuf_u(nc, "gt_o", [128, 2 * tc], BF16))
        dcol = sctx.enter_context(sbuf_u(nc, "gt_d", [128, nblk], F32))
        ld2 = [P.sem("gh0"), P.sem("gh1")]
        ldz2 = [P.sem("gd0"), P.sem("gd1")]
        ldc = P.sem("gh2")
        vd = P.sem("gc0")
        sd = P.sem("gc1")
        st = P.sem("gd2")
        _, gldc_t = P.dma("sync", dcol[:], ts["d_vec"], sem=ldc)
        slot_free = [0, 0]
        for g in range(nblk):
            for c in range(nch):
                t0 = c * tc
                sb = (g * nch + c) % 2
                sl = slice(sb * tc, (sb + 1) * tc)
                rows = slice(g * 128, (g + 1) * 128)
                P.wait("sync", st, slot_free[sb])
                P.dma("sync", y_sb[:, sl], ts["y_scan"][rows, t0:t0 + tc], sem=ld2[sb])
                P.dma("sync", u_sb[:, sl], ts["u_f32"][rows, t0:t0 + tc], sem=ld2[sb])
                P.wait("gpsimd", st, slot_free[sb])
                P.dma("gpsimd", z_sb[:, sl], ts["z_f32"][rows, t0:t0 + tc], sem=ldz2[sb])
                P.emit("scalar", lambda e, o=sz_sb[:, sl], i_=z_sb[:, sl]:
                       e.activation(out=o, in_=i_, func=AF.Silu),
                       wait=[(ldz2[sb], P.cnt(ldz2[sb]))], inc=[(sd, 1)])
                P.emit("vector", lambda e, o=t_sb[:, sl], u_=u_sb[:, sl], d=dcol[:, g:g + 1], y=y_sb[:, sl]:
                       e.scalar_tensor_tensor(out=o, in0=u_, scalar=d, in1=y, op0=ALU.mult, op1=ALU.add),
                       wait=[(ld2[sb], P.cnt(ld2[sb])), (ldc, gldc_t)])
                P.emit("vector", lambda e, o=o_sb[:, sl], a=t_sb[:, sl], b=sz_sb[:, sl]:
                       e.tensor_tensor(out=o, in0=a, in1=b, op=ALU.mult),
                       wait=[(sd, P.cnt(sd))], inc=[(vd, 1)])
                P.wait("gpsimd", vd, P.cnt(vd))
                P.dma("gpsimd", ts["yg_bf"][rows, t0:t0 + tc], o_sb[:, sl], sem=st)
                slot_free[sb] = P.cnt(st)
        P.wait("gpsimd", st, P.cnt(st))
        P.barrier()


def emit_rms_fm(P, ts, ctx, src, gain_key, eps, out_bf=None, out_f32=None,
                ndim=D, tc=512, tag="rms"):
    nc = P.nc
    nblk = ndim // 128
    nch = T // tc
    with ExitStack() as sctx:
        xc = sctx.enter_context(sbuf_u(nc, tag + "_xc", [128, nblk * tc], F32))
        sq = sctx.enter_context(sbuf_u(nc, tag + "_sq", [128, nblk * tc], BF16))
        rows = sctx.enter_context(sbuf_u(nc, tag + "_rows", [1, 2 * tc], F32))
        bcast = sctx.enter_context(sbuf_u(nc, tag + "_bc", [128, tc], F32))
        onesc = sctx.enter_context(sbuf_u(nc, tag + "_ones", [128, 1], BF16))
        onesr = sctx.enter_context(sbuf_u(nc, tag + "_onesr", [1, 128], F32))
        gcol = sctx.enter_context(sbuf_u(nc, tag + "_g", [128, nblk], F32))
        obf = sctx.enter_context(sbuf_u(nc, tag + "_obf", [128, nblk * tc], BF16))
        of32 = sctx.enter_context(sbuf_u(nc, tag + "_of", [128, nblk * tc], F32))
        ps_s = sctx.enter_context(psum_u(nc, tag + "_pss", [1, tc], F32))
        ps_bc = sctx.enter_context(psum_u(nc, tag + "_psb", [128, tc], F32))
        ldc = P.sem("gh0")
        ld = P.sem("gh1")
        ca = P.sem("gc0")
        mmA = P.sem("gc1")
        rowd = P.sem("gc2")
        bcd = P.sem("gc3")
        bce = P.sem("gc4")
        app = P.sem("gc5")
        st = P.sem("gd0")
        ld_base = P.cnt(ld)
        P.dma("sync", onesc[:], ts["c_ones_col_bf"], sem=ldc)
        P.dma("sync", onesr[:], ts["c_ones_row"], sem=ldc)
        _, ldc_t = P.dma("sync", gcol[:], ts[gain_key], sem=ldc)
        for c in range(nch):
            t0 = c * tc
            for g in range(nblk):
                P.dma("sync", xc[:, g * tc:(g + 1) * tc],
                      src[g * 128:(g + 1) * 128, t0:t0 + tc], sem=ld)
            ld_tgt = ld_base + (c + 1) * nblk * 16
            for g in range(nblk):
                sl = slice(g * tc, (g + 1) * tc)
                P.emit("scalar", lambda e, a=sq[:, sl], b=xc[:, sl]:
                       e.activation(out=a, in_=b, func=AF.Square),
                       wait=[(ld, ld_tgt)] if g == 0 else (), inc=[(ca, 1)])
            for g in range(nblk):
                sl = slice(g * tc, (g + 1) * tc)
                P.emit("tensor", lambda e, r=sq[:, sl], o=ps_s[:], s=(g == 0), p=(g == nblk - 1):
                       e.matmul(out=o, lhsT=onesc[:], rhs=r, start=s, stop=p),
                       wait=[(ca, P.cnt(ca)), (ldc, ldc_t)] if g == 0 else (),
                       inc=[(mmA, 1)] if g == nblk - 1 else ())
            ms = rows[:, 0:tc]; rstd = rows[:, tc:2 * tc]
            P.emit("vector", lambda e, o=ms, i_=ps_s[:]:
                   e.tensor_scalar(out=o, in0=i_, scalar1=1.0 / ndim, scalar2=None, op0=ALU.mult),
                   wait=[(mmA, P.cnt(mmA))], inc=[(rowd, 1)])
            P.emit("scalar", lambda e, o=rstd, i_=ms: e.activation(out=o, in_=i_, func=AF.Ln, bias=eps),
                   wait=[(rowd, P.cnt(rowd))])
            P.emit("scalar", lambda e, o=rstd, i_=rstd: e.activation(out=o, in_=i_, func=AF.Exp, scale=-0.5),
                   inc=[(rowd, 1)])
            P.emit("tensor", lambda e, o=ps_bc[:], r=rstd:
                   e.matmul(out=o, lhsT=onesr[:], rhs=r, start=True, stop=True),
                   wait=[(rowd, P.cnt(rowd))], inc=[(bcd, 1)])
            P.emit("scalar", lambda e, o=bcast[:], i_=ps_bc[:]: e.copy(out=o, in_=i_),
                   wait=[(bcd, P.cnt(bcd))], inc=[(bce, 1)])
            for g in range(nblk):
                sl = slice(g * tc, (g + 1) * tc)
                P.emit("vector", lambda e, o=of32[:, sl], i_=xc[:, sl], gc=gcol[:, g:g + 1], rb=bcast[:]:
                       e.scalar_tensor_tensor(out=o, in0=i_, scalar=gc, in1=rb,
                                              op0=ALU.mult, op1=ALU.mult),
                       wait=[(bce, P.cnt(bce))] if g == 0 else (), inc=[(app, 1)])
                if out_bf is not None:
                    P.emit("scalar", lambda e, o=obf[:, sl], i_=of32[:, sl]:
                           e.copy(out=o, in_=i_),
                           wait=[(app, P.cnt(app))], inc=[(app, 1)])
            app_now = P.cnt(app)
            P.wait("gpsimd", app, app_now)
            for g in range(nblk):
                sl = slice(g * tc, (g + 1) * tc)
                if out_f32 is not None:
                    P.dma("gpsimd", out_f32[g * 128:(g + 1) * 128, t0:t0 + tc], of32[:, sl], sem=st)
                if out_bf is not None:
                    P.dma("gpsimd", out_bf[g * 128:(g + 1) * 128, t0:t0 + tc], obf[:, sl], sem=st)
            P.wait("sync", st, P.cnt(st))
        P.wait("gpsimd", st, P.cnt(st))
        P.barrier()


def emit_l0_router(P, ts, ctx, tc=512, dbg=False):
    nc = P.nc
    nk = D // 128
    nch = T // tc
    E = 32
    with ExitStack() as sctx:
        w_sb = sctx.enter_context(sbuf_u(nc, "r0_w", [128, nk * E], F32))
        rhs = sctx.enter_context(sbuf_u(nc, "r0_rhs", [128, nk * tc], F32))
        sco = sctx.enter_context(sbuf_u(nc, "r0_sco", [E, tc], F32))
        ident = sctx.enter_context(sbuf_u(nc, "r0_id", [128, 128], F32))
        gbrow = sctx.enter_context(sbuf_u(nc, "r0_gbr", [1, E], F32))
        gb_bc = sctx.enter_context(sbuf_u(nc, "r0_gbc", [128, E], F32))
        onesr = sctx.enter_context(sbuf_u(nc, "r0_or", [1, 128], F32))
        tm = sctx.enter_context(sbuf_u(nc, "r0_tm", [128, 16 * E], F32))
        cfm = sctx.enter_context(sbuf_u(nc, "r0_cfm", [E, tc], F32))
        ps_l = sctx.enter_context(psum_u(nc, "r0_psl", [E, tc], F32))
        ps_t = sctx.enter_context(psum_u(nc, "r0_pst", [128, E], F32))
        ps_c = sctx.enter_context(psum_u(nc, "r0_psc", [E, 128], F32))
        ldc = P.sem("gh0")
        lr = P.sem("gh1")
        mmL = P.sem("gc0")
        sgd = P.sem("gc1")
        trd = P.sem("gc2")
        tmd = P.sem("gc3")
        cbk = P.sem("gc4")
        st = P.sem("gd0")

        P.dma("sync", w_sb[:].rearrange("p (c i) -> p c i", c=nk),
              ts["w_l0gate"].rearrange("(c p) i -> p c i", p=128), sem=ldc)
        P.dma("sync", ident[:], ts["c_ident"], sem=ldc)
        P.dma("sync", gbrow[:], ts["b_l0gate"], sem=ldc)
        _, ldc_t = P.dma("sync", onesr[:], ts["c_ones_row"], sem=ldc)
        P.emit("tensor", lambda e: e.matmul(out=ps_t[:], lhsT=onesr[:], rhs=gbrow[:],
                                            start=True, stop=True),
               wait=[(ldc, ldc_t)], inc=[(trd, 1)])
        P.emit("scalar", lambda e: e.copy(out=gb_bc[:], in_=ps_t[:]),
               wait=[(trd, P.cnt(trd))], inc=[(sgd, 1)])
        P.barrier()

        for c in range(nch):
            t0 = c * tc
            P.wait("sync", mmL, P.cnt(mmL))
            P.dma("sync", rhs[:].rearrange("p (c t) -> p c t", c=nk),
                  ts["xn_f32"][:, t0:t0 + tc].rearrange("(c p) t -> p c t", p=128), sem=lr)
            lr_t = P.cnt(lr)
            P.wait("tensor", sgd, P.cnt(sgd))
            for k in range(nk):
                P.emit("tensor", lambda e, lt=w_sb[:, k * E:(k + 1) * E], rt=rhs[:, k * tc:(k + 1) * tc],
                       s=(k == 0), p_=(k == nk - 1):
                       e.matmul(out=ps_l[:], lhsT=lt, rhs=rt, start=s, stop=p_),
                       wait=[(lr, lr_t)] if k == 0 else (),
                       inc=[(mmL, 1)] if k == nk - 1 else ())
            P.emit("scalar", lambda e: e.activation(out=sco[:], in_=ps_l[:], func=AF.Sigmoid),
                   wait=[(mmL, P.cnt(mmL))], inc=[(sgd, 1)])
            sgd_t = P.cnt(sgd)
            for j in range(tc // 128):
                P.emit("tensor", lambda e, i_=sco[:, j * 128:(j + 1) * 128]:
                       e.transpose(out=ps_t[:], in_=i_, identity=ident[0:E, 0:E]),
                       wait=[(sgd, sgd_t), (tmd, P.cnt(tmd))], inc=[(trd, 1)])
                sc_tm = tm[:, 0:E]
                scb = tm[:, E:2 * E]
                m1 = tm[:, 2 * E:2 * E + 8]
                mask = tm[:, 3 * E:4 * E]
                grp2 = tm[:, 4 * E:5 * E]
                m2 = tm[:, 5 * E:5 * E + 8]
                gsc = tm[:, 6 * E:6 * E + 8]
                v8 = tm[:, 7 * E:7 * E + 8]
                gmask = tm[:, 8 * E:8 * E + 8]
                masked = tm[:, 9 * E:10 * E]
                v8b = tm[:, 10 * E:10 * E + 8]
                sel_ = tm[:, 11 * E:12 * E]
                tkw = tm[:, 12 * E:13 * E]
                den = tm[:, 13 * E:13 * E + 1]
                rden = tm[:, 13 * E + 1:13 * E + 2]
                compb = tm[:, 14 * E:15 * E]
                P.emit("scalar", lambda e, o=sc_tm: e.copy(out=o, in_=ps_t[:]),
                       wait=[(trd, P.cnt(trd))], inc=[(sgd, 1)])
                vg = P.sem("gc5")

                def v(fn, wait=(), inc=(), g_=True):
                    inc = list(inc)
                    if g_ and not inc:
                        inc = [(vg, 1)]
                    P.emit("vector", fn, wait=wait, inc=inc)
                    if g_:
                        sname, _ = inc[0]
                        P.wait("vector", sname, P.cnt(sname))
                v(lambda e, o=scb, a=sc_tm, b=gb_bc[:]: e.tensor_tensor(out=o, in0=a, in1=b, op=ALU.add),
                  wait=[(sgd, P.cnt(sgd))])
                v(lambda e, o=m1.rearrange("p (g o) -> p g o", o=1),
                  i_=scb.rearrange("p (g o) -> p g o", o=4):
                  e.tensor_reduce(out=o, in_=i_, axis=AX.X, op=ALU.max), g_=True)
                for o_ in range(4):
                    v(lambda e, o=mask.rearrange("p (g o) -> p g o", o=4)[:, :, o_],
                      a=scb.rearrange("p (g o) -> p g o", o=4)[:, :, o_], b=m1:
                      e.tensor_tensor(out=o, in0=a, in1=b, op=ALU.is_equal))
                v(lambda e, o=grp2, a=mask, b=scb:
                  e.scalar_tensor_tensor(out=o, in0=a, scalar=-1e30, in1=b, op0=ALU.mult, op1=ALU.add))
                v(lambda e, o=m2.rearrange("p (g o) -> p g o", o=1),
                  i_=grp2.rearrange("p (g o) -> p g o", o=4):
                  e.tensor_reduce(out=o, in_=i_, axis=AX.X, op=ALU.max), g_=True)
                v(lambda e, o=gsc, a=m1, b=m2: e.tensor_tensor(out=o, in0=a, in1=b, op=ALU.add))
                v(lambda e, o=v8, i_=gsc: e.max(out=o, in_=i_), g_=True)
                v(lambda e, o=gmask, a=gsc, th=v8[:, 3:4]:
                  e.tensor_scalar(out=o, in0=a, scalar1=th, scalar2=None, op0=ALU.is_ge))
                for o_ in range(4):
                    v(lambda e, o=masked.rearrange("p (g o) -> p g o", o=4)[:, :, o_],
                      a=scb.rearrange("p (g o) -> p g o", o=4)[:, :, o_], b=gmask:
                      e.tensor_tensor(out=o, in0=a, in1=b, op=ALU.mult))
                v(lambda e, o=v8b, i_=masked: e.max(out=o, in_=i_), g_=True)
                v(lambda e, o=sel_, a=masked, th=v8b[:, 7:8]:
                  e.tensor_scalar(out=o, in0=a, scalar1=th, scalar2=None, op0=ALU.is_ge))
                v(lambda e, o=tkw, a=sc_tm, b=sel_: e.tensor_tensor(out=o, in0=a, in1=b, op=ALU.mult))
                v(lambda e, o=den, i_=tkw.rearrange("p (g o) -> p g o", g=1):
                  e.tensor_reduce(out=o.rearrange("p (g o) -> p g o", g=1), in_=i_, axis=AX.X, op=ALU.add), g_=True)
                v(lambda e, o=den, i_=den:
                  e.tensor_scalar(out=o, in0=i_, scalar1=1e-20, scalar2=None, op0=ALU.add))
                v(lambda e, o=rden, i_=den: e.reciprocal(out=o, in_=i_), g_=True)
                v(lambda e, o=compb, a=tkw, r=rden:
                  e.tensor_scalar(out=o, in0=a, scalar1=r, scalar2=2.5, op0=ALU.mult, op1=ALU.mult),
                  inc=[(tmd, 1)])
                if dbg and c == 0 and j == 0:
                    P.wait("gpsimd", tmd, P.cnt(tmd))
                    P.dma("gpsimd", ts["dbg"][0:128, 0:16 * E], tm[:], sem=st)
                    P.wait("vector", st, P.cnt(st))
                    P.wait("tensor", st, P.cnt(st))
                    P.wait("scalar", st, P.cnt(st))
                P.emit("tensor", lambda e, i_=compb:
                       e.transpose(out=ps_c[:], in_=i_, identity=ident[:]),
                       wait=[(tmd, P.cnt(tmd)), (cbk, P.cnt(cbk))], inc=[(trd, 1)])
                P.emit("scalar", lambda e, o=cfm[:, j * 128:(j + 1) * 128]:
                       e.copy(out=o, in_=ps_c[:]),
                       wait=[(trd, P.cnt(trd))], inc=[(cbk, 1)])
            P.wait("gpsimd", cbk, P.cnt(cbk))
            P.dma("gpsimd", ts["comb"][:, t0:t0 + tc], cfm[:], sem=st)
            P.wait("sync", st, P.cnt(st))
        P.wait("gpsimd", st, P.cnt(st))
        P.barrier()


def emit_l0_moe(P, ts, ctx, tc=512, G=8):
    nc = P.nc
    nk = D // 128
    E = 32
    IT = 4
    nch = T // tc
    ngrp = E // G
    with ExitStack() as sctx:
        rhs = sctx.enter_context(sbuf_u(nc, "m0_rhs", [128, nk * tc], BF16))
        wga = [sctx.enter_context(sbuf_u(nc, f"m0_wg{b}", [128, nk * 512], BF16)) for b in range(2)]
        wua = [sctx.enter_context(sbuf_u(nc, f"m0_wu{b}", [128, nk * 512], BF16)) for b in range(2)]
        wda = [sctx.enter_context(sbuf_u(nc, f"m0_wd{b}", [128, G * 4 * 128], BF16)) for b in range(2)]
        wsh = sctx.enter_context(sbuf_u(nc, "m0_wsh", [128, nk * 1024], BF16))
        wsd = [sctx.enter_context(sbuf_u(nc, f"m0_wsd{b}", [128, 8 * 128], BF16)) for b in range(2)]
        comb_sb = sctx.enter_context(sbuf_u(nc, "m0_comb", [E, tc], F32))
        sel = sctx.enter_context(sbuf_u(nc, "m0_sel", [E, E * 128], F32))
        cbc = sctx.enter_context(sbuf_u(nc, "m0_cbc", [128, tc], F32))
        silu_sb = sctx.enter_context(sbuf_u(nc, "m0_silu", [128, tc], F32))
        t1_sb = sctx.enter_context(sbuf_u(nc, "m0_t1", [128, tc], F32))
        prod = sctx.enter_context(sbuf_u(nc, "m0_prod", [128, (G * IT + 8) * tc], BF16))
        y_sb = sctx.enter_context(sbuf_u(nc, "m0_y", [128, 8 * tc], F32))
        psg = sctx.enter_context(psum_u(nc, "m0_psg", [128, tc], F32))
        psu = sctx.enter_context(psum_u(nc, "m0_psu", [128, tc], F32))
        psb = sctx.enter_context(psum_u(nc, "m0_psb", [128, tc], F32))
        psy = [sctx.enter_context(psum_u(nc, f"m0_psy{b}", [128, tc], F32)) for b in range(2)]
        ldc = P.sem("gh0")
        lr = P.sem("gh1")
        lcomb = P.sem("gh2")
        lwg = [P.sem("gd0"), P.sem("gd1")]
        lwd = [P.sem("gd2"), P.sem("gd3")]
        lwsh = P.sem("gd4")
        mmA = P.sem("gc0")
        mmB = P.sem("gc1")
        evA = P.sem("gc2")
        pvd = P.sem("gc3")
        mmD = P.sem("gc4")
        acc = P.sem("gc5")
        st = P.sem("gd5")

        _, sel_t = P.dma("sync", sel[:], ts["c_sel32"], sem=ldc)
        wbuf_free = [P.cnt(pvd)] * 2
        wdbuf_free = [P.cnt(mmD)] * 2
        shbuf_free = P.cnt(mmD)
        psy_free = [P.cnt(acc)] * 2
        prod_free = P.cnt(mmD)
        cbc_guard = P.cnt(pvd)
        y_free = P.cnt(st)
        for c in range(nch):
            t0 = c * tc
            P.wait("sync", mmA, P.cnt(mmA))
            P.dma("sync", rhs[:].rearrange("p (c t) -> p c t", c=nk),
                  ts["xn_bf"][:, t0:t0 + tc].rearrange("(c p) t -> p c t", p=128), sem=lr)
            lr_t = P.cnt(lr)
            P.wait("sync", mmB, P.cnt(mmB))
            P.dma("sync", comb_sb[:], ts["comb"][:, t0:t0 + tc], sem=lcomb)
            lcomb_t = P.cnt(lcomb)
            for grp in range(ngrp):
                for ei in range(G):
                    e_ = grp * G + ei
                    wb = ei % 2
                    P.wait("sync", pvd, wbuf_free[wb])
                    P.dma("sync", wga[wb][:], ts["b_w_l0egp"][e_], sem=lwg[wb])
                    P.dma("sync", wua[wb][:], ts["b_w_l0eup"][e_], sem=lwg[wb])
                    lw_t = P.cnt(lwg[wb])
                    P.emit("tensor", lambda e, lt=sel[:, e_ * 128:(e_ + 1) * 128]:
                           e.matmul(out=psb[:], lhsT=lt, rhs=comb_sb[:], start=True, stop=True),
                           wait=[(lcomb, lcomb_t), (ldc, sel_t), (evA, P.cnt(evA))], inc=[(mmB, 1)])
                    P.emit("scalar", lambda e: e.copy(out=cbc[:], in_=psb[:]),
                           wait=[(mmB, P.cnt(mmB)), (pvd, cbc_guard)], inc=[(evA, 1)])
                    for it in range(IT):
                        pslot = ei * IT + it
                        for k in range(nk):
                            P.emit("tensor", lambda e, lt=wga[wb][:, k * 512 + it * 128:k * 512 + (it + 1) * 128],
                                   rt=rhs[:, k * tc:(k + 1) * tc], s=(k == 0), p_=(k == nk - 1):
                                   e.matmul(out=psg[:], lhsT=lt, rhs=rt, start=s, stop=p_),
                                   wait=[(lwg[wb], lw_t), (lr, lr_t), (evA, P.cnt(evA))] if k == 0 else ())
                        for k in range(nk):
                            P.emit("tensor", lambda e, lt=wua[wb][:, k * 512 + it * 128:k * 512 + (it + 1) * 128],
                                   rt=rhs[:, k * tc:(k + 1) * tc], s=(k == 0), p_=(k == nk - 1):
                                   e.matmul(out=psu[:], lhsT=lt, rhs=rt, start=s, stop=p_),
                                   wait=[(lr, lr_t), (pvd, P.cnt(pvd))] if k == 0 else (),
                                   inc=[(mmA, 1)] if k == nk - 1 else ())
                        P.emit("scalar", lambda e: e.activation(out=silu_sb[:], in_=psg[:], func=AF.Silu),
                               wait=[(mmA, P.cnt(mmA)), (pvd, P.cnt(pvd))], inc=[(evA, 1)])
                        P.emit("vector", lambda e: e.tensor_tensor(out=t1_sb[:], in0=silu_sb[:], in1=cbc[:], op=ALU.mult),
                               wait=[(evA, P.cnt(evA)), (mmD, prod_free)])
                        P.emit("vector", lambda e, o=prod[:, pslot * tc:(pslot + 1) * tc]:
                               e.tensor_tensor(out=o, in0=t1_sb[:], in1=psu[:], op=ALU.mult),
                               inc=[(pvd, 1)])
                    wbuf_free[wb] = P.cnt(pvd)
                    cbc_guard = P.cnt(pvd)
                if grp == 0:
                    P.wait("sync", pvd, wbuf_free[0])
                    P.wait("sync", pvd, wbuf_free[1])
                    P.dma("sync", wsh[:], ts["b_w_l0sgp"], sem=lwsh)
                    P.dma("sync", wua[0][:], ts["b_w_l0sup0"], sem=lwsh)
                    P.dma("sync", wua[1][:], ts["b_w_l0sup1"], sem=lwsh)
                    lwsh_t = P.cnt(lwsh)
                    for it in range(8):
                        pslot = G * IT + it
                        wu_t = wua[0] if it < 4 else wua[1]
                        ito = it % 4
                        for k in range(nk):
                            P.emit("tensor", lambda e, lt=wsh[:, k * 1024 + it * 128:k * 1024 + (it + 1) * 128],
                                   rt=rhs[:, k * tc:(k + 1) * tc], s=(k == 0), p_=(k == nk - 1):
                                   e.matmul(out=psg[:], lhsT=lt, rhs=rt, start=s, stop=p_),
                                   wait=[(lwsh, lwsh_t), (lr, lr_t), (evA, P.cnt(evA))] if k == 0 else ())
                        for k in range(nk):
                            P.emit("tensor", lambda e, lt=wu_t[:, k * 512 + ito * 128:k * 512 + (ito + 1) * 128],
                                   rt=rhs[:, k * tc:(k + 1) * tc], s=(k == 0), p_=(k == nk - 1):
                                   e.matmul(out=psu[:], lhsT=lt, rhs=rt, start=s, stop=p_),
                                   wait=[(pvd, P.cnt(pvd))] if k == 0 else (),
                                   inc=[(mmA, 1)] if k == nk - 1 else ())
                        P.emit("scalar", lambda e: e.activation(out=silu_sb[:], in_=psg[:], func=AF.Silu),
                               wait=[(mmA, P.cnt(mmA)), (pvd, P.cnt(pvd))], inc=[(evA, 1)])
                        P.emit("vector", lambda e, o=prod[:, pslot * tc:(pslot + 1) * tc]:
                               e.tensor_tensor(out=o, in0=silu_sb[:], in1=psu[:], op=ALU.mult),
                               wait=[(evA, P.cnt(evA)), (mmD, prod_free)], inc=[(pvd, 1)])
                    wbuf_free[0] = P.cnt(pvd)
                    wbuf_free[1] = P.cnt(pvd)
                pvd_t = P.cnt(pvd)
                for dt in range(8):
                    wb = dt % 2
                    pb = dt % 2
                    P.wait("sync", mmD, wdbuf_free[wb])
                    P.dma("sync", wda[wb][:], ts["b_w_l0edp"][dt, grp], sem=lwd[wb])
                    lwd_t = P.cnt(lwd[wb])
                    nsh = 0
                    if grp == 0:
                        P.dma("sync", wsd[wb][:], ts["b_w_l0sdp"][dt], sem=lwd[wb])
                        lwd_t = P.cnt(lwd[wb])
                        nsh = 8
                    nmm = G * 4 + nsh
                    q = 0
                    P.wait("tensor", acc, psy_free[pb])
                    for ei in range(G):
                        for k in range(4):
                            P.emit("tensor", lambda e, o=psy[pb][:],
                                   lt=wda[wb][:, (ei * 4 + k) * 128:(ei * 4 + k + 1) * 128],
                                   rt=prod[:, (ei * IT + k) * tc:(ei * IT + k + 1) * tc],
                                   s=(q == 0), p_=(q == nmm - 1):
                                   e.matmul(out=o, lhsT=lt, rhs=rt, start=s, stop=p_),
                                   wait=[(lwd[wb], lwd_t), (pvd, pvd_t)] if q == 0 else (),
                                   inc=[(mmD, 1)] if q == nmm - 1 else ())
                            q += 1
                    if grp == 0:
                        for k in range(8):
                            P.emit("tensor", lambda e, o=psy[pb][:],
                                   lt=wsd[wb][:, k * 128:(k + 1) * 128],
                                   rt=prod[:, (G * IT + k) * tc:(G * IT + k + 1) * tc],
                                   s=False, p_=(q == nmm - 1):
                                   e.matmul(out=o, lhsT=lt, rhs=rt, start=s, stop=p_),
                                   inc=[(mmD, 1)] if q == nmm - 1 else ())
                            q += 1
                    ysl = y_sb[:, dt * tc:(dt + 1) * tc]
                    if grp == 0:
                        P.emit("vector", lambda e, o=ysl, p_=psy[pb][:]: e.tensor_copy(out=o, in_=p_),
                               wait=[(mmD, P.cnt(mmD)), (st, y_free)], inc=[(acc, 1)])
                    else:
                        P.emit("vector", lambda e, o=ysl, p_=psy[pb][:]: e.tensor_tensor(out=o, in0=o, in1=p_, op=ALU.add),
                               wait=[(mmD, P.cnt(mmD))], inc=[(acc, 1)])
                    psy_free[pb] = P.cnt(acc)
                    wdbuf_free[wb] = P.cnt(mmD)
                prod_free = P.cnt(mmD)
            P.wait("gpsimd", acc, P.cnt(acc))
            for dt in range(8):
                P.dma("gpsimd", ts["y0ffn"][dt * 128:(dt + 1) * 128, t0:t0 + tc],
                      y_sb[:, dt * tc:(dt + 1) * tc], sem=st)
            y_free = P.cnt(st)
        P.wait("gpsimd", st, P.cnt(st))
        P.barrier()


def emit_ln_sb(P, ts, ctx, src, scol_ap, bcol_ap, out_bf=None, out_f32=None,
               eps=1e-5, tc=512):
    nc = P.nc
    nblk = D // 128
    nch = T // tc
    with ExitStack() as sctx:
        xc = sctx.enter_context(sbuf_u(nc, "lb_xc", [128, nblk * tc], F32))
        xb = sctx.enter_context(sbuf_u(nc, "lb_xb", [128, nblk * tc], BF16))
        sq = sctx.enter_context(sbuf_u(nc, "lb_sq", [128, nblk * tc], BF16))
        rows = sctx.enter_context(sbuf_u(nc, "lb_rows", [1, 6 * tc], F32))
        bcast = sctx.enter_context(sbuf_u(nc, "lb_bc", [128, 2 * tc], F32))
        onesc = sctx.enter_context(sbuf_u(nc, "lb_ones", [128, 1], BF16))
        onesr = sctx.enter_context(sbuf_u(nc, "lb_onesr", [1, 128], F32))
        obf = sctx.enter_context(sbuf_u(nc, "lb_obf", [128, nblk * tc], BF16))
        of32 = sctx.enter_context(sbuf_u(nc, "lb_of", [128, nblk * tc], F32))
        scol = sctx.enter_context(sbuf_u(nc, "lb_scol", [128, nblk], F32))
        bcol2 = sctx.enter_context(sbuf_u(nc, "lb_bcol", [128, nblk], F32))
        ps_s = sctx.enter_context(psum_u(nc, "lb_pss", [1, tc], F32))
        ps_s2 = sctx.enter_context(psum_u(nc, "lb_pss2", [1, tc], F32))
        ps_bc = sctx.enter_context(psum_u(nc, "lb_psb", [128, 2 * tc], F32))
        ldc = P.sem("gh0")
        ld = P.sem("gh1")
        ca = P.sem("gc0")
        mmA = P.sem("gc1")
        rowd = P.sem("gc2")
        bcd = P.sem("gc3")
        bce = P.sem("gc4")
        app = P.sem("gc5")
        st = P.sem("gd0")
        ld_base = P.cnt(ld)
        P.dma("sync", onesc[:], ts["c_ones_col_bf"], sem=ldc)
        P.dma("sync", scol[:], scol_ap, sem=ldc)
        P.dma("sync", bcol2[:], bcol_ap, sem=ldc)
        _, ldc_t = P.dma("sync", onesr[:], ts["c_ones_row"], sem=ldc)
        for c in range(nch):
            t0 = c * tc
            for g in range(nblk):
                P.dma("sync", xc[:, g * tc:(g + 1) * tc],
                      src[g * 128:(g + 1) * 128, t0:t0 + tc], sem=ld)
            ld_tgt = ld_base + (c + 1) * nblk * 16
            for g in range(nblk):
                sl = slice(g * tc, (g + 1) * tc)
                P.emit("vector", lambda e, a=xb[:, sl], b=xc[:, sl]: e.tensor_copy(out=a, in_=b),
                       wait=[(ld, ld_tgt), (ldc, ldc_t)] if g == 0 else (), inc=[(ca, 1)])
                P.emit("scalar", lambda e, a=sq[:, sl], b=xc[:, sl]:
                       e.activation(out=a, in_=b, func=AF.Square),
                       wait=[(ld, ld_tgt)] if g == 0 else (), inc=[(ca, 1)])
            for g in range(nblk):
                sl = slice(g * tc, (g + 1) * tc)
                P.emit("tensor", lambda e, r=xb[:, sl], o=ps_s[:], s=(g == 0), p=(g == nblk - 1):
                       e.matmul(out=o, lhsT=onesc[:], rhs=r, start=s, stop=p),
                       wait=[(ca, P.cnt(ca)), (ldc, ldc_t)] if g == 0 else ())
            for g in range(nblk):
                sl = slice(g * tc, (g + 1) * tc)
                P.emit("tensor", lambda e, r=sq[:, sl], o=ps_s2[:], s=(g == 0), p=(g == nblk - 1):
                       e.matmul(out=o, lhsT=onesc[:], rhs=r, start=s, stop=p),
                       inc=[(mmA, 1)] if g == nblk - 1 else ())
            mrow = rows[:, 0:tc]; ex2 = rows[:, tc:2 * tc]
            var = rows[:, 2 * tc:3 * tc]; rstd = rows[:, 3 * tc:4 * tc]
            brow = rows[:, 4 * tc:5 * tc]; tmp = rows[:, 5 * tc:6 * tc]
            P.emit("vector", lambda e, o=mrow, i=ps_s[:]:
                   e.tensor_scalar(out=o, in0=i, scalar1=1.0 / D, scalar2=None, op0=ALU.mult),
                   wait=[(mmA, P.cnt(mmA))])
            P.emit("vector", lambda e, o=ex2, i=ps_s2[:]:
                   e.tensor_scalar(out=o, in0=i, scalar1=1.0 / D, scalar2=None, op0=ALU.mult))
            P.emit("vector", lambda e, o=tmp, i=mrow: e.tensor_tensor(out=o, in0=i, in1=i, op=ALU.mult))
            P.emit("vector", lambda e, o=var, i=ex2, j=tmp: e.tensor_tensor(out=o, in0=i, in1=j, op=ALU.subtract),
                   inc=[(rowd, 1)])
            P.emit("scalar", lambda e, o=tmp, i=var: e.activation(out=o, in_=i, func=AF.Ln, bias=eps),
                   wait=[(rowd, P.cnt(rowd))])
            P.emit("scalar", lambda e, o=rstd, i=tmp: e.activation(out=o, in_=i, func=AF.Exp, scale=-0.5),
                   inc=[(rowd, 1)])
            P.emit("vector", lambda e, o=brow, i=mrow, j=rstd:
                   e.scalar_tensor_tensor(out=o, in0=i, scalar=-1.0, in1=j, op0=ALU.mult, op1=ALU.mult),
                   wait=[(rowd, P.cnt(rowd))], inc=[(rowd, 1)])
            P.emit("tensor", lambda e, o=ps_bc[:, 0:tc], r=rstd:
                   e.matmul(out=o, lhsT=onesr[:], rhs=r, start=True, stop=True),
                   wait=[(rowd, P.cnt(rowd))])
            P.emit("tensor", lambda e, o=ps_bc[:, tc:2 * tc], r=brow:
                   e.matmul(out=o, lhsT=onesr[:], rhs=r, start=True, stop=True),
                   inc=[(bcd, 1)])
            P.emit("scalar", lambda e, o=bcast[:, 0:tc], i=ps_bc[:, 0:tc]: e.copy(out=o, in_=i),
                   wait=[(bcd, P.cnt(bcd))])
            P.emit("scalar", lambda e, o=bcast[:, tc:2 * tc], i=ps_bc[:, tc:2 * tc]: e.copy(out=o, in_=i),
                   inc=[(bce, 1)])
            for g in range(nblk):
                sl = slice(g * tc, (g + 1) * tc)
                P.emit("vector", lambda e, o=of32[:, sl], i=xc[:, sl], ab=bcast[:, 0:tc]:
                       e.tensor_tensor(out=o, in0=i, in1=ab, op=ALU.mult),
                       wait=[(bce, P.cnt(bce))] if g == 0 else ())
                P.emit("vector", lambda e, o=of32[:, sl], i=of32[:, sl], bb=bcast[:, tc:2 * tc]:
                       e.tensor_tensor(out=o, in0=i, in1=bb, op=ALU.add))
                P.emit("vector", lambda e, o=of32[:, sl], i=of32[:, sl],
                       sc=scol[:, g:g + 1], bc=bcol2[:, g:g + 1]:
                       e.tensor_scalar(out=o, in0=i, scalar1=sc, scalar2=bc, op0=ALU.mult, op1=ALU.add),
                       inc=[(app, 1)])
                if out_bf is not None:
                    P.emit("scalar", lambda e, o=obf[:, sl], i=of32[:, sl]:
                           e.copy(out=o, in_=i),
                           wait=[(app, P.cnt(app))], inc=[(app, 1)])
            app_now = P.cnt(app)
            P.wait("gpsimd", app, app_now)
            for g in range(nblk):
                sl = slice(g * tc, (g + 1) * tc)
                if out_f32 is not None:
                    P.dma("gpsimd", out_f32[g * 128:(g + 1) * 128, t0:t0 + tc], of32[:, sl], sem=st)
                if out_bf is not None:
                    P.dma("gpsimd", out_bf[g * 128:(g + 1) * 128, t0:t0 + tc], obf[:, sl], sem=st)
            P.wait("sync", st, P.cnt(st))
        P.wait("gpsimd", st, P.cnt(st))
        P.barrier()


def emit_combine(P, ts, ctx, x_in, s_in, y_in, x_out, tc=512):
    nc = P.nc
    nblk = D // 128
    nch = T // tc
    with ExitStack() as sctx:
        x_sb = sctx.enter_context(sbuf_u(nc, "cb_x", [128, 2 * tc], F32))
        s_sb = sctx.enter_context(sbuf_u(nc, "cb_s", [128, 2 * tc], F32))
        y_sb = sctx.enter_context(sbuf_u(nc, "cb_y", [128, 2 * tc], F32))
        sy_sb = sctx.enter_context(sbuf_u(nc, "cb_sy", [128, 2 * tc], F32))
        o_sb = sctx.enter_context(sbuf_u(nc, "cb_o", [128, 2 * tc], F32))
        ld2 = [P.sem("gh0"), P.sem("gh1")]
        ly2 = [P.sem("gd0"), P.sem("gd1")]
        sd = P.sem("gc0")
        vd = P.sem("gc1")
        st = P.sem("gd2")
        slot_free = [P.cnt(st)] * 2
        for g in range(nblk):
            for c in range(nch):
                t0 = c * tc
                sb = (g * nch + c) % 2
                sl = slice(sb * tc, (sb + 1) * tc)
                rows = slice(g * 128, (g + 1) * 128)
                P.wait("sync", st, slot_free[sb])
                P.dma("sync", x_sb[:, sl], x_in[rows, t0:t0 + tc], sem=ld2[sb])
                P.dma("sync", s_sb[:, sl], s_in[rows, t0:t0 + tc], sem=ld2[sb])
                P.wait("gpsimd", st, slot_free[sb])
                P.dma("gpsimd", y_sb[:, sl], y_in[rows, t0:t0 + tc], sem=ly2[sb])
                P.emit("scalar", lambda e, o=sy_sb[:, sl], i_=y_sb[:, sl]:
                       e.activation(out=o, in_=i_, func=AF.Silu),
                       wait=[(ly2[sb], P.cnt(ly2[sb]))], inc=[(sd, 1)])
                P.emit("vector", lambda e, o=sy_sb[:, sl], a=s_sb[:, sl]:
                       e.tensor_tensor(out=o, in0=o, in1=a, op=ALU.mult),
                       wait=[(sd, P.cnt(sd)), (ld2[sb], P.cnt(ld2[sb]))])
                P.emit("vector", lambda e, o=o_sb[:, sl], a=x_sb[:, sl], b=sy_sb[:, sl]:
                       e.tensor_tensor(out=o, in0=a, in1=b, op=ALU.add),
                       inc=[(vd, 1)])
                P.wait("gpsimd", vd, P.cnt(vd))
                P.dma("gpsimd", x_out[rows, t0:t0 + tc], o_sb[:, sl], sem=st)
                slot_free[sb] = P.cnt(st)
        P.wait("gpsimd", st, P.cnt(st))
        P.barrier()


def emit_l1_attn_layer(P, ts, ctx, l, tc=512):
    nc = P.nc
    H = 16
    DK = 64
    nblk = D // 128
    nch = T // tc
    ntile = T // 128
    with ExitStack() as sctx:
        wk_sb = sctx.enter_context(sbuf_u(nc, "la_wk", [128, nblk * D], BF16))
        wv_sb = sctx.enter_context(sbuf_u(nc, "la_wv", [128, nblk * D], BF16))
        wo_sb = sctx.enter_context(sbuf_u(nc, "la_wo", [128, nblk * D], BF16))
        bk_bc = sctx.enter_context(sbuf_u(nc, "la_bkbc", [128, D], F32))
        bv_bc = sctx.enter_context(sbuf_u(nc, "la_bvbc", [128, D], F32))
        brow = sctx.enter_context(sbuf_u(nc, "la_brow", [1, 2 * D], F32))
        onesr = sctx.enter_context(sbuf_u(nc, "la_onesr", [1, 128], F32))
        onescb = sctx.enter_context(sbuf_u(nc, "la_onescb", [128, 1], BF16))
        bocol = sctx.enter_context(sbuf_u(nc, "la_bo", [128, nblk], F32))
        zsel = sctx.enter_context(sbuf_u(nc, "la_zsel", [16, 8 * 128], F32))
        htile = sctx.enter_context(sbuf_u(nc, "la_h", [128, 2 * nblk * 128], BF16))
        ktmp = sctx.enter_context(sbuf_u(nc, "la_ktmp", [128, 512], F32))
        ktmp2 = sctx.enter_context(sbuf_u(nc, "la_ktmp2", [128, D], BF16))
        etmp = sctx.enter_context(sbuf_u(nc, "la_etmp", [128, 512], F32))
        kp_tm = sctx.enter_context(sbuf_u(nc, "la_kp", [128, D], BF16))
        v_tm = sctx.enter_context(sbuf_u(nc, "la_v", [128, D], BF16))
        kv_bf = sctx.enter_context(sbuf_u(nc, "la_kv", [128, 512], BF16))
        ks_col = sctx.enter_context(sbuf_u(nc, "la_ks", [128, 8], F32))
        ks_blk = sctx.enter_context(sbuf_u(nc, "la_ksb", [128, 8 * 16], BF16))
        qch = sctx.enter_context(sbuf_u(nc, "la_q", [128, nblk * tc], BF16))
        den_sb = sctx.enter_context(sbuf_u(nc, "la_den", [16, tc], F32))
        z_sb = sctx.enter_context(sbuf_u(nc, "la_z", [16, tc], F32))
        zbc_sb = sctx.enter_context(sbuf_u(nc, "la_zbc", [128, 2 * tc], F32))
        attn_sb = sctx.enter_context(sbuf_u(nc, "la_attn", [128, nblk * tc], BF16))
        xer = sctx.enter_context(sbuf_u(nc, "la_xer", [128, 2 * tc], F32))
        oo = sctx.enter_context(sbuf_u(nc, "la_oo", [128, 2 * tc], F32))
        psk = sctx.enter_context(psum_u(nc, "la_psk", [128, 512], F32))
        psv = sctx.enter_context(psum_u(nc, "la_psv", [128, 512], F32))
        psks = sctx.enter_context(psum_u(nc, "la_psks", [128, 8], F32))

        ldc = P.sem("gh0")
        ldcg = P.sem("gd0")
        lh2 = [P.sem("gh1"), P.sem("gh2")]
        lq = P.sem("gh3")
        lx2 = [P.sem("gh4"), P.sem("gh5")]
        mmk = P.sem("gc0")
        evk = P.sem("gc1")
        mkv = P.sem("gc2")
        bev = P.sem("gc3")
        mden = P.sem("gc4")
        mzb = P.sem("gc5")
        myy = P.sem("gc6")
        evy = P.sem("gc7")
        mo = P.sem("gc8")
        evo = P.sem("gc9")
        st = P.sem("gd1")

        P.dma("gpsimd", wk_sb[:], ts["w_Wk_p"][l], sem=ldcg)
        P.dma("gpsimd", wv_sb[:], ts["w_Wv_p"][l], sem=ldcg)
        _, ldcg_t = P.dma("gpsimd", wo_sb[:], ts["w_Wo_p"][l], sem=ldcg)
        P.dma("sync", brow[:, 0:D], ts["p_bk"][l:l + 1, :], sem=ldc)
        P.dma("sync", brow[:, D:2 * D], ts["p_bv"][l:l + 1, :], sem=ldc)
        P.dma("sync", onesr[:], ts["c_ones_row"], sem=ldc)
        P.dma("sync", onescb[:], ts["c_ones_col_bf"], sem=ldc)
        P.dma("sync", bocol[:], ts["bo_cols"][:, l * nblk:(l + 1) * nblk], sem=ldc)
        _, ldc_t = P.dma("sync", zsel[:], ts["c_zsel"], sem=ldc)
        with ExitStack() as bctx:
            psboot = bctx.enter_context(psum_u(nc, "la_psboot", [128, 512], F32))
            for half in range(2):
                for which in range(2):
                    P.emit("tensor", lambda e, o=psboot[:], r=brow[:, which * D + half * 512: which * D + (half + 1) * 512]:
                           e.matmul(out=o, lhsT=onesr[:], rhs=r, start=True, stop=True),
                           wait=[(ldc, ldc_t), (bev, P.cnt(bev))], inc=[(mzb, 1)])
                    dst = bk_bc if which == 0 else bv_bc
                    P.emit("scalar", lambda e, o=dst[:, half * 512:(half + 1) * 512], i_=psboot[:]:
                           e.copy(out=o, in_=i_),
                           wait=[(mzb, P.cnt(mzb))], inc=[(bev, 1)])
            bev_boot = P.cnt(bev)
            P.barrier()

        with ExitStack() as pctx:
            psk = pctx.enter_context(psum_u(nc, "la_psk", [128, 512], F32))
            psv = pctx.enter_context(psum_u(nc, "la_psv", [128, 512], F32))
            psks = pctx.enter_context(psum_u(nc, "la_psks", [128, 8], F32))
            h_free = [P.cnt(mmk)] * 2
            kv_st = P.cnt(st)
            for j in range(ntile):
                hb = j % 2
                hsl = slice(hb * nblk * 128, (hb + 1) * nblk * 128)
                P.wait("sync", mmk, h_free[hb])
                P.dma("sync", htile[:, hsl].rearrange("p (c t) -> p c t", c=nblk),
                      ts["h1l_bf"][:, j * 128:(j + 1) * 128].rearrange("(c p) t -> p c t", p=128),
                      sem=lh2[hb])
                lh_t = P.cnt(lh2[hb])
                for half in range(2):
                    csl = slice(half * 512, (half + 1) * 512)
                    for kb in range(nblk):
                        P.emit("tensor", lambda e, o=psk[:], lt=htile[:, hsl][:, kb * 128:(kb + 1) * 128],
                               rt=wk_sb[:, kb * D:kb * D + D][:, csl], s=(kb == 0), p_=(kb == nblk - 1):
                               e.matmul(out=o, lhsT=lt, rhs=rt, start=s, stop=p_),
                               wait=[(lh2[hb], lh_t), (ldc, ldc_t), (ldcg, ldcg_t), (evk, P.cnt(evk))] if kb == 0 else ())
                    for kb in range(nblk):
                        P.emit("tensor", lambda e, o=psv[:], lt=htile[:, hsl][:, kb * 128:(kb + 1) * 128],
                               rt=wv_sb[:, kb * D:kb * D + D][:, csl], s=(kb == 0), p_=(kb == nblk - 1):
                               e.matmul(out=o, lhsT=lt, rhs=rt, start=s, stop=p_),
                               inc=[(mmk, 1)] if kb == nblk - 1 else ())
                    P.emit("vector", lambda e, o=ktmp[:], i_=psk[:], b=bk_bc[:, csl]:
                           e.tensor_tensor(out=o, in0=i_, in1=b, op=ALU.add),
                           wait=[(mmk, P.cnt(mmk)), (bev, bev_boot), (st, kv_st)])
                    P.emit("vector", lambda e, o=etmp[:], i_=ktmp[:]:
                           e.tensor_scalar(out=o, in0=i_, scalar1=0.0, scalar2=None, op0=ALU.min),
                           inc=[(evk, 1)])
                    P.emit("scalar", lambda e, o=etmp[:], i_=etmp[:]:
                           e.activation(out=o, in_=i_, func=AF.Exp),
                           wait=[(evk, P.cnt(evk))], inc=[(evk, 1)])
                    P.emit("vector", lambda e, o=kp_tm[:, csl], i_=ktmp[:], m=etmp[:]:
                           e.scalar_tensor_tensor(out=o, in0=i_, scalar=0.0, in1=m,
                                                  op0=ALU.max, op1=ALU.add),
                           wait=[(evk, P.cnt(evk))])
                    P.emit("vector", lambda e, o=v_tm[:, csl], i_=psv[:], b=bv_bc[:, csl]:
                           e.tensor_tensor(out=o, in0=i_, in1=b, op=ALU.add),
                           inc=[(evk, 1)])
                evk_t = P.cnt(evk)
                for s_ in range(nblk):
                    P.emit("tensor", lambda e, o=psks[:, s_:s_ + 1],
                           lt=kp_tm[:, s_ * 128:(s_ + 1) * 128], rt=onescb[:],
                           s=(j == 0 and s_ == 0), p_=(j == ntile - 1 and s_ == nblk - 1):
                           e.matmul(out=o, lhsT=lt, rhs=rt, start=s, stop=p_),
                           wait=[(evk, evk_t)] if s_ == 0 else (),
                           inc=[(mkv, 1)] if s_ == nblk - 1 else ())
                h_free[hb] = P.cnt(mmk)
                P.wait("gpsimd", mkv, P.cnt(mkv))
                P.dma("gpsimd", ts["k_tm"][j * 128:(j + 1) * 128, :], kp_tm[:], sem=st)
                P.dma("gpsimd", ts["v_tm"][j * 128:(j + 1) * 128, :], v_tm[:], sem=st)
                kv_st = P.cnt(st)
            P.emit("scalar", lambda e: e.copy(out=ks_col[:], in_=psks[:]),
                   wait=[(mkv, P.cnt(mkv))], inc=[(bev, 1)])
            P.emit("vector", lambda e: e.memset(ks_blk[:], 0.0), wait=[(bev, P.cnt(bev))],
                   inc=[(bev, 1)])
            P.wait("vector", bev, P.cnt(bev))
            for h_ in range(H):
                s_ = h_ // 2
                base = (h_ % 2) * 64
                P.emit("vector", lambda e, o=ks_blk[base:base + 64, s_ * 16 + h_:s_ * 16 + h_ + 1],
                       i_=ks_col[base:base + 64, s_:s_ + 1]:
                       e.tensor_copy(out=o, in_=i_),
                       inc=[(bev, 1)] if h_ == H - 1 else ())
            P.wait("vector", bev, P.cnt(bev))
            P.barrier()
        with ExitStack() as pctx:
            pspair_full = [pctx.enter_context(psum_u(nc, f"la_pp{b}", [128, 512], F32))
                           for b in range(4)]
            pspair = [t[:, 0:128] for t in pspair_full]
            kvld = [P.sem("gh6"), P.sem("gh7")]
            for half2 in range(2):
                kvt_free = [P.cnt(mkv)] * 2
                for j in range(ntile):
                    hb = j % 2
                    hsl = slice(hb * nblk * 128, (hb + 1) * nblk * 128)
                    P.wait("sync", mkv, kvt_free[hb])
                    P.dma("sync", htile[:, hsl][:, 0:D], ts["k_tm"][j * 128:(j + 1) * 128, :], sem=kvld[hb])
                    P.dma("sync", v_tm[:] if hb == 0 else ktmp2[:],
                          ts["v_tm"][j * 128:(j + 1) * 128, :], sem=kvld[hb])
                    lkv_t = P.cnt(kvld[hb])
                    vsrc = v_tm if hb == 0 else ktmp2
                    for sp in range(4):
                        s_ = half2 * 4 + sp
                        P.emit("tensor", lambda e, o=pspair[sp],
                               lt=htile[:, hsl][:, 2 * s_ * 64:(2 * s_ + 2) * 64],
                               rt=vsrc[:, 2 * s_ * 64:(2 * s_ + 2) * 64],
                               s=(j == 0), p_=(j == ntile - 1):
                               e.matmul(out=o, lhsT=lt, rhs=rt, start=s, stop=p_),
                               wait=[(kvld[hb], lkv_t)] if sp == 0 else (),
                               inc=[(mkv, 1)] if sp == 3 else ())
                    kvt_free[hb] = P.cnt(mkv)
                mkv_t2 = P.cnt(mkv)
                for sp in range(4):
                    s_ = half2 * 4 + sp
                    flat0 = s_ * 64 if s_ < 4 else 256 + (s_ - 4) * 64
                    dst0 = kv_bf[0:64, flat0:flat0 + 64]
                    dst1 = kv_bf[64:128, flat0:flat0 + 64]
                    P.emit("scalar", lambda e, o=dst0, i_=pspair_full[sp][0:64, 0:64]: e.copy(out=o, in_=i_),
                           wait=[(mkv, mkv_t2)] if sp == 0 else ())
                    P.emit("scalar", lambda e, o=dst1, i_=pspair_full[sp][64:128, 64:128]: e.copy(out=o, in_=i_),
                           inc=[(bev, 1)] if sp == 3 else ())
                P.wait("scalar", bev, P.cnt(bev))
                P.wait("tensor", bev, P.cnt(bev))
            kvrdy = P.cnt(bev)
            P.barrier()
        pctxC = ExitStack()
        psden = pctxC.enter_context(psum_u(nc, "la_psden", [16, tc], F32))
        pszb = pctxC.enter_context(psum_u(nc, "la_pszb", [128, tc], F32))
        psy = [pctxC.enter_context(psum_u(nc, f"la_psy{b}", [128, tc], F32)) for b in range(2)]
        q_guard = (myy, 0)
        psy_guard = [(evy, 0), (evy, 0)]
        zbc_guard = [(evy, 0), (evy, 0)]
        xer_guard = [(evo, 0), (evo, 0)]
        oo_guard = [(st, 0), (st, 0)]
        attn_guard = (mo, 0)
        den_ps_guard = (bev, kvrdy)
        den_sb_guard = (bev, kvrdy)
        for c in range(nch):
            t0 = c * tc
            P.wait("sync", q_guard[0], q_guard[1])
            P.dma("sync", qch[:].rearrange("p (c t) -> p c t", c=nblk),
                  ts["q1_bf"][:, t0:t0 + tc].rearrange("(c p) t -> p c t", p=128), sem=lq)
            lq_t = P.cnt(lq)
            for s_ in range(nblk):
                P.emit("tensor", lambda e, o=psden[:], lt=ks_blk[:, s_ * 16:(s_ + 1) * 16],
                       rt=qch[:, s_ * tc:(s_ + 1) * tc], s=(s_ == 0), p_=(s_ == nblk - 1):
                       e.matmul(out=o, lhsT=lt, rhs=rt, start=s, stop=p_),
                       wait=[(lq, lq_t), (bev, kvrdy), den_ps_guard] if s_ == 0 else (),
                       inc=[(mden, 1)] if s_ == nblk - 1 else ())
            P.emit("scalar", lambda e: e.copy(out=den_sb[:], in_=psden[:]),
                   wait=[(mden, P.cnt(mden)), den_sb_guard], inc=[(bev, 1)])
            den_ps_guard = (bev, P.cnt(bev))
            P.emit("vector", lambda e: e.tensor_scalar(out=den_sb[:], in0=den_sb[:],
                                                       scalar1=1e-6, scalar2=None, op0=ALU.add),
                   wait=[(bev, P.cnt(bev))])
            P.emit("vector", lambda e: e.reciprocal(out=z_sb[:], in_=den_sb[:]), inc=[(bev, 1)])
            P.wait("vector", bev, P.cnt(bev))
            zrdy = P.cnt(bev)
            for s_ in range(nblk):
                zb = s_ % 2
                zsl = zbc_sb[:, zb * tc:(zb + 1) * tc]
                P.emit("tensor", lambda e, o=pszb[:], lt=zsel[:, s_ * 128:(s_ + 1) * 128], r=z_sb[:]:
                       e.matmul(out=o, lhsT=lt, rhs=r, start=True, stop=True),
                       wait=[(bev, zrdy), zbc_guard[zb], (bev, P.cnt(bev))], inc=[(mzb, 1)])
                P.emit("scalar", lambda e, o=zsl, i_=pszb[:]: e.copy(out=o, in_=i_),
                       wait=[(mzb, P.cnt(mzb))], inc=[(bev, 1)])
                zbc_rdy = P.cnt(bev)
                pb = s_ % 2
                for hh in range(2):
                    h_ = 2 * s_ + hh
                    base = (h_ % 2) * 64
                    kvs = kv_bf[:, 0:256] if h_ < 8 else kv_bf[:, 256:512]
                    col = ((h_ // 2) % 4) * 64
                    P.emit("tensor", lambda e, o=psy[pb][base:base + 64, :],
                           lt=kvs[base:base + 64, col:col + 64],
                           rt=qch[:, s_ * tc:(s_ + 1) * tc][base:base + 64, :]:
                           e.matmul(out=o, lhsT=lt, rhs=rt, start=True, stop=True),
                           wait=[psy_guard[pb]] if hh == 0 else (),
                           inc=[(myy, 1)] if hh == 1 else ())
                P.emit("vector", lambda e, o=attn_sb[:, s_ * tc:(s_ + 1) * tc], a=psy[pb][:], b=zsl:
                       e.tensor_tensor(out=o, in0=a, in1=b, op=ALU.mult),
                       wait=[(myy, P.cnt(myy)), (bev, zbc_rdy), attn_guard],
                       inc=[(evy, 1)])
                psy_guard[pb] = (evy, P.cnt(evy))
                zbc_guard[zb] = (evy, P.cnt(evy))
            evy_t = P.cnt(evy)
            q_guard = (myy, P.cnt(myy))
            for it in range(nblk):
                xb_ = it % 2
                P.wait("sync", xer_guard[xb_][0], xer_guard[xb_][1])
                P.dma("sync", xer[:, xb_ * tc:(xb_ + 1) * tc],
                      ts["xe"][it * 128:(it + 1) * 128, t0:t0 + tc], sem=lx2[xb_])
                lx_t = P.cnt(lx2[xb_])
                pb = it % 2
                for kb in range(nblk):
                    P.emit("tensor", lambda e, o=psy[pb][:], lt=wo_sb[:, kb * D + it * 128:kb * D + (it + 1) * 128],
                           rt=attn_sb[:, kb * tc:(kb + 1) * tc], s=(kb == 0), p_=(kb == nblk - 1):
                           e.matmul(out=o, lhsT=lt, rhs=rt, start=s, stop=p_),
                           wait=[(evy, evy_t), psy_guard[pb]] if kb == 0 else (),
                           inc=[(mo, 1)] if kb == nblk - 1 else ())
                P.emit("vector", lambda e, o=oo[:, (it % 2) * tc:(it % 2 + 1) * tc],
                       i_=psy[pb][:], b=bocol[:, it:it + 1], r=xer[:, xb_ * tc:(xb_ + 1) * tc]:
                       e.scalar_tensor_tensor(out=o, in0=i_, scalar=b, in1=r,
                                              op0=ALU.add, op1=ALU.add),
                       wait=[(mo, P.cnt(mo)), (lx2[xb_], lx_t), oo_guard[it % 2]],
                       inc=[(evo, 1)])
                psy_guard[pb] = (evo, P.cnt(evo))
                xer_guard[xb_] = (evo, P.cnt(evo))
                P.wait("gpsimd", evo, P.cnt(evo))
                P.dma("gpsimd", ts["xe2"][it * 128:(it + 1) * 128, t0:t0 + tc],
                      oo[:, (it % 2) * tc:(it % 2 + 1) * tc], sem=st)
                oo_guard[it % 2] = (st, P.cnt(st))
            attn_guard = (mo, P.cnt(mo))
            den_sb_guard = (mzb, P.cnt(mzb))
        P.wait("gpsimd", st, P.cnt(st))
        P.barrier()
        pctxC.close()


def emit_l1_router(P, ts, ctx, tc=512):
    nc = P.nc
    nk = D // 128
    E = 8
    nch = T // tc
    with ExitStack() as sctx:
        w_sb = sctx.enter_context(sbuf_u(nc, "r1_w", [128, nk * E], F32))
        rhs = sctx.enter_context(sbuf_u(nc, "r1_rhs", [128, nk * tc], F32))
        gfm = sctx.enter_context(sbuf_u(nc, "r1_gfm", [E, tc], F32))
        ident = sctx.enter_context(sbuf_u(nc, "r1_id", [128, 128], F32))
        rbrow = sctx.enter_context(sbuf_u(nc, "r1_rbr", [1, E], F32))
        rb_bc = sctx.enter_context(sbuf_u(nc, "r1_rbc", [128, E], F32))
        onesr = sctx.enter_context(sbuf_u(nc, "r1_or", [1, 128], F32))
        tm = sctx.enter_context(sbuf_u(nc, "r1_tm", [128, 8 * E], F32))
        wfm = sctx.enter_context(sbuf_u(nc, "r1_wfm", [E, tc], F32))
        ps_l = sctx.enter_context(psum_u(nc, "r1_psl", [E, tc], F32))
        ps_t = sctx.enter_context(psum_u(nc, "r1_pst", [128, E], F32))
        ps_c = sctx.enter_context(psum_u(nc, "r1_psc", [E, 128], F32))
        ldc = P.sem("gh0")
        lr = P.sem("gd0")
        mmL = P.sem("gc0")
        sgd = P.sem("gc1")
        trd = P.sem("gc2")
        tmd = P.sem("gc3")
        cbk = P.sem("gc4")
        st = P.sem("gd1")
        vg = P.sem("gc5")

        P.dma("sync", w_sb[:].rearrange("p (c i) -> p c i", c=nk),
              ts["w_l1r"].rearrange("(c p) i -> p c i", p=128), sem=ldc)
        P.dma("sync", ident[:], ts["c_ident"], sem=ldc)
        _, ldc_t0 = P.dma("sync", rbrow[:], ts["b_l1r"], sem=ldc)
        _, ldc_t = P.dma("sync", onesr[:], ts["c_ones_row"], sem=ldc)
        P.emit("tensor", lambda e: e.matmul(out=ps_t[:], lhsT=onesr[:], rhs=rbrow[:],
                                            start=True, stop=True),
               wait=[(ldc, ldc_t)], inc=[(trd, 1)])
        P.emit("scalar", lambda e: e.copy(out=rb_bc[:], in_=ps_t[:]),
               wait=[(trd, P.cnt(trd))], inc=[(sgd, 1)])
        P.barrier()

        for c in range(nch):
            t0 = c * tc
            P.wait("gpsimd", mmL, P.cnt(mmL))
            P.dma("gpsimd", rhs[:].rearrange("p (c t) -> p c t", c=nk),
                  ts["xn1_f32"][:, t0:t0 + tc].rearrange("(c p) t -> p c t", p=128), sem=lr)
            lr_t = P.cnt(lr)
            P.wait("tensor", sgd, P.cnt(sgd))
            for k in range(nk):
                P.emit("tensor", lambda e, lt=w_sb[:, k * E:(k + 1) * E], rt=rhs[:, k * tc:(k + 1) * tc],
                       s=(k == 0), p_=(k == nk - 1):
                       e.matmul(out=ps_l[:], lhsT=lt, rhs=rt, start=s, stop=p_),
                       wait=[(lr, lr_t)] if k == 0 else (),
                       inc=[(mmL, 1)] if k == nk - 1 else ())
            P.emit("scalar", lambda e: e.copy(out=gfm[:], in_=ps_l[:]),
                   wait=[(mmL, P.cnt(mmL))], inc=[(sgd, 1)])
            sgd_t = P.cnt(sgd)
            for j in range(tc // 128):
                P.emit("tensor", lambda e, i_=gfm[:, j * 128:(j + 1) * 128]:
                       e.transpose(out=ps_t[:], in_=i_, identity=ident[0:E, 0:E]),
                       wait=[(sgd, sgd_t), (tmd, P.cnt(tmd))], inc=[(trd, 1)])
                g_tm = tm[:, 0:E]
                gb = tm[:, E:2 * E]
                v8 = tm[:, 2 * E:3 * E]
                erow = tm[:, 3 * E:3 * E + 1]
                w1c = tm[:, 3 * E + 1:3 * E + 2]
                w2c = tm[:, 3 * E + 2:3 * E + 3]
                m1 = tm[:, 4 * E:5 * E]
                m2 = tm[:, 5 * E:6 * E]
                wtm = tm[:, 6 * E:7 * E]
                P.emit("scalar", lambda e, o=g_tm: e.copy(out=o, in_=ps_t[:]),
                       wait=[(trd, P.cnt(trd))], inc=[(sgd, 1)])

                def v(fn, wait=(), inc=(), g_=True):
                    inc = list(inc)
                    if g_ and not inc:
                        inc = [(vg, 1)]
                    P.emit("vector", fn, wait=wait, inc=inc)
                    if g_:
                        sname, _ = inc[0]
                        P.wait("vector", sname, P.cnt(sname))
                v(lambda e, o=gb, a=g_tm, b=rb_bc[:]: e.tensor_tensor(out=o, in0=a, in1=b, op=ALU.add),
                  wait=[(sgd, P.cnt(sgd))])
                v(lambda e, o=v8, i_=gb: e.max(out=o, in_=i_))
                v(lambda e, o=erow, a=v8[:, 1:2], b=v8[:, 0:1]:
                  e.tensor_tensor(out=o, in0=a, in1=b, op=ALU.subtract))
                P.emit("scalar", lambda e, o=erow, i_=erow: e.activation(out=o, in_=i_, func=AF.Exp),
                       wait=[(vg, P.cnt(vg))], inc=[(vg, 1)])
                P.wait("vector", vg, P.cnt(vg))
                v(lambda e, o=w1c, i_=erow:
                  e.tensor_scalar(out=o, in0=i_, scalar1=1.0, scalar2=None, op0=ALU.add))
                v(lambda e, o=w1c, i_=w1c: e.reciprocal(out=o, in_=i_))
                v(lambda e, o=w2c, a=erow, b=w1c: e.tensor_tensor(out=o, in0=a, in1=b, op=ALU.mult))
                v(lambda e, o=m1, a=gb, th=v8[:, 0:1]:
                  e.tensor_scalar(out=o, in0=a, scalar1=th, scalar2=None, op0=ALU.is_equal))
                v(lambda e, o=m2, a=gb, th=v8[:, 1:2]:
                  e.tensor_scalar(out=o, in0=a, scalar1=th, scalar2=None, op0=ALU.is_equal))
                v(lambda e, o=wtm, a=m1, w=w1c:
                  e.tensor_scalar(out=o, in0=a, scalar1=w, scalar2=None, op0=ALU.mult))
                v(lambda e, o=wtm, a=m2, w=w2c, b=wtm:
                  e.scalar_tensor_tensor(out=o, in0=a, scalar=w, in1=b, op0=ALU.mult, op1=ALU.add),
                  inc=[(tmd, 1)])
                P.emit("tensor", lambda e, i_=wtm:
                       e.transpose(out=ps_c[:], in_=i_, identity=ident[:]),
                       wait=[(tmd, P.cnt(tmd)), (cbk, P.cnt(cbk))], inc=[(trd, 1)])
                P.emit("scalar", lambda e, o=wfm[:, j * 128:(j + 1) * 128]:
                       e.copy(out=o, in_=ps_c[:]),
                       wait=[(trd, P.cnt(trd))], inc=[(cbk, 1)])
            P.wait("gpsimd", cbk, P.cnt(cbk))
            P.dma("gpsimd", ts["l1w"][:, t0:t0 + tc], wfm[:], sem=st)
            P.wait("sync", st, P.cnt(st))
        P.wait("gpsimd", st, P.cnt(st))
        P.barrier()


CP = 1280
CPCH = [(0, 512), (512, 512), (1024, 256)]


def emit_l1_sparse_prep(P, ts, ctx):
    nc = P.nc
    with ExitStack() as sctx:
        t_w8 = sctx.enter_context(sbuf_u(nc, "sp_w8", [8, T], F32))
        t_m = sctx.enter_context(sbuf_u(nc, "sp_m", [8, T], F32))
        t_a = sctx.enter_context(sbuf_u(nc, "sp_a", [8, T], F32))
        t_b = sctx.enter_context(sbuf_u(nc, "sp_b", [8, T], F32))
        t_c = sctx.enter_context(sbuf_u(nc, "sp_c", [8, T], F32))
        t_d = sctx.enter_context(sbuf_u(nc, "sp_d", [8, T], F32))
        t_st = sctx.enter_context(sbuf_u(nc, "sp_st", [8, T], F32))
        t_v = sctx.enter_context(sbuf_u(nc, "sp_v", [8, T], F32))
        Qa = sctx.enter_context(sbuf_u(nc, "sp_Qa", [128, T], F32))
        Qb = sctx.enter_context(sbuf_u(nc, "sp_Qb", [128, T], F32))
        w8 = t_w8[:]
        m = t_m[:]
        ones8 = t_a[:]
        sel = t_a[:]
        cs = t_b[:]
        rk = t_b[:]
        slot = t_c[:]
        tmp = t_d[:]
        st2 = t_st[:]
        vmask = t_v[:]
        dump8 = Qb[0:8, :]
        metaA = sctx.enter_context(sbuf_u(nc, "sp_mA", [2, T], F32))
        metaB = sctx.enter_context(sbuf_u(nc, "sp_mB", [2, T], F32))
        zini = sctx.enter_context(sbuf_u(nc, "sp_z", [8, 1], F32))
        ecol = sctx.enter_context(sbuf_u(nc, "sp_ec", [8, 1], F32))
        lst = sctx.enter_context(sbuf_u(nc, "sp_lst", [8, 8], F32))
        grp2 = sctx.enter_context(sbuf_u(nc, "sp_g2", [128, 2], F32))
        idsb = sctx.enter_context(sbuf_u(nc, "sp_id", [128, 128], F32))
        tm4 = [sctx.enter_context(sbuf_u(nc, f"sp_t4{b}", [128, 4], F32)) for b in range(2)]
        tm8 = [sctx.enter_context(sbuf_u(nc, f"sp_t8{b}", [128, 8], F32)) for b in range(2)]
        tmu2 = [sctx.enter_context(sbuf_u(nc, f"sp_u2{b}", [128, 2], U32)) for b in range(2)]
        tmu8 = [sctx.enter_context(sbuf_u(nc, f"sp_u8{b}", [128, 8], U32)) for b in range(2)]
        pm = sctx.enter_context(psum_u(nc, "sp_pm", [8, 512], F32))
        p2a = sctx.enter_context(psum_u(nc, "sp_p2a", [2, 512], F32))
        p2b = sctx.enter_context(psum_u(nc, "sp_p2b", [2, 512], F32))
        pt = [sctx.enter_context(psum_u(nc, f"sp_pt{b}", [128, 128], F32)) for b in range(2)]

        ld = P.sem("gh0")
        vd = P.sem("gc0")
        mmd = P.sem("gc1")
        evd = P.sem("gc2")
        trd = P.sem("gc3")
        cvd = P.sem("gc4")
        std = P.sem("gd0")

        P.dma("sync", w8, ts["l1w"], sem=ld)
        P.dma("sync", ecol[:], ts["c_ecol8"], sem=ld)
        P.dma("sync", lst[:], ts["c_lst8"], sem=ld)
        P.dma("sync", grp2[:], ts["c_grp2"], sem=ld)
        _, ld_t = P.dma("sync", idsb[:], ts["c_ident"], sem=ld)
        P.emit("vector", lambda e: e.memset(ones8, 1.0))
        P.emit("vector", lambda e: e.memset(zini[:], 0.0))
        P.emit("vector", lambda e: e.memset(Qa[:], 0.0))
        P.emit("vector", lambda e: e.memset(Qb[:], 0.0), inc=[(vd, 1)])
        P.wait("sync", vd, P.cnt(vd))
        _, lddp_t = P.dma("sync", dump8, ts["c_dump8"], sem=ld)
        P.emit("vector", lambda e: e.tensor_scalar(out=m, in0=w8, scalar1=0.0,
                                                   scalar2=None, op0=ALU.is_gt),
               wait=[(ld, ld_t), (ld, lddp_t)])
        P.emit("vector", lambda e: e.tensor_tensor_scan(out=cs, data0=ones8, data1=m,
                                                        initial=zini[:], op0=ALU.mult, op1=ALU.add))
        P.emit("vector", lambda e: e.tensor_scalar(out=slot, in0=cs, scalar1=ecol[:, 0:1],
                                                   scalar2=-1.0, op0=ALU.add, op1=ALU.add))
        P.emit("vector", lambda e: e.tensor_scalar(out=tmp, in0=cs, scalar1=float(CP) + 0.5,
                                                   scalar2=None, op0=ALU.is_lt))
        P.emit("vector", lambda e: e.tensor_tensor(out=vmask, in0=tmp, in1=m, op=ALU.mult))
        P.emit("vector", lambda e: e.tensor_tensor(out=slot, in0=slot, in1=vmask, op=ALU.mult))
        P.emit("vector", lambda e: e.tensor_scalar(out=tmp, in0=vmask, scalar1=-1.0,
                                                   scalar2=1.0, op0=ALU.mult, op1=ALU.add))
        P.emit("vector", lambda e: e.tensor_tensor(out=tmp, in0=tmp, in1=dump8, op=ALU.mult))
        P.emit("vector", lambda e: e.tensor_tensor(out=st2, in0=slot, in1=tmp, op=ALU.add),
               inc=[(vd, 1)])
        vd_m = P.cnt(vd)
        for c in range(8):
            sl = slice(c * 512, (c + 1) * 512)
            P.emit("tensor", lambda e, o=pm[:], r=m[:, sl]:
                   e.matmul(out=o, lhsT=lst[0:8, 0:8], rhs=r, start=True, stop=True),
                   wait=[(vd, vd_m), (evd, P.cnt(evd))], inc=[(mmd, 1)])
            P.emit("scalar", lambda e, o=rk[:, sl], i_=pm[:]:
                   e.copy(out=o, in_=i_),
                   wait=[(mmd, P.cnt(mmd))], inc=[(evd, 1)])
        P.wait("vector", evd, P.cnt(evd))
        for r in range(2):
            P.emit("vector", lambda e, rr=float(r): e.tensor_scalar(out=sel, in0=rk,
                   scalar1=rr, scalar2=None, op0=ALU.is_equal))
            P.emit("vector", lambda e: e.tensor_tensor(out=sel, in0=sel, in1=vmask, op=ALU.mult))
            P.emit("vector", lambda e, b=r * 64: e.tensor_tensor(out=Qa[b:b + 8, :], in0=sel,
                   in1=st2, op=ALU.mult))
            P.emit("vector", lambda e, b=r * 64: e.tensor_tensor(out=Qb[b:b + 8, :], in0=sel,
                   in1=w8, op=ALU.mult), inc=[(vd, 1)])
        vd_q = P.cnt(vd)
        for c in range(8):
            sl = slice(c * 512, (c + 1) * 512)
            P.emit("tensor", lambda e, o=p2a[:], r=Qa[:, sl]:
                   e.matmul(out=o, lhsT=grp2[:], rhs=r, start=True, stop=True),
                   wait=[(vd, vd_q), (evd, P.cnt(evd))], inc=[(mmd, 1)])
            P.emit("scalar", lambda e, o=metaA[:, sl], i_=p2a[:]:
                   e.copy(out=o, in_=i_),
                   wait=[(mmd, P.cnt(mmd))], inc=[(evd, 1)])
            P.emit("tensor", lambda e, o=p2b[:], r=Qb[:, sl]:
                   e.matmul(out=o, lhsT=grp2[:], rhs=r, start=True, stop=True),
                   wait=[(evd, P.cnt(evd))], inc=[(mmd, 1)])
            P.emit("scalar", lambda e, o=metaB[:, sl], i_=p2b[:]:
                   e.copy(out=o, in_=i_),
                   wait=[(mmd, P.cnt(mmd))], inc=[(evd, 1)])
        evd_meta = P.cnt(evd)
        for jt in range(32):
            b = jt % 2
            tsl = slice(jt * 128, (jt + 1) * 128)
            P.emit("tensor", lambda e, o=pt[b][0:128, 0:2], i_=metaA[0:2, tsl]:
                   e.transpose(out=o, in_=i_, identity=idsb[0:2, 0:2]),
                   wait=[(evd, evd_meta), (cvd, P.cnt(cvd)), (std, P.cnt(std))], inc=[(trd, 1)])
            P.emit("scalar", lambda e, o=tm4[b][:, 0:2], i_=pt[b][0:128, 0:2]:
                   e.copy(out=o, in_=i_), wait=[(trd, P.cnt(trd))], inc=[(evd, 1)])
            P.emit("tensor", lambda e, o=pt[b][0:128, 0:2], i_=metaB[0:2, tsl]:
                   e.transpose(out=o, in_=i_, identity=idsb[0:2, 0:2]),
                   wait=[(evd, P.cnt(evd))], inc=[(trd, 1)])
            P.emit("scalar", lambda e, o=tm4[b][:, 2:4], i_=pt[b][0:128, 0:2]:
                   e.copy(out=o, in_=i_), wait=[(trd, P.cnt(trd))], inc=[(evd, 1)])
            P.emit("tensor", lambda e, o=pt[b][0:128, 0:8], i_=st2[:, tsl]:
                   e.transpose(out=o, in_=i_, identity=idsb[0:8, 0:8]),
                   wait=[(evd, P.cnt(evd))], inc=[(trd, 1)])
            P.emit("scalar", lambda e, o=tm8[b][:], i_=pt[b][0:128, 0:8]:
                   e.copy(out=o, in_=i_), wait=[(trd, P.cnt(trd))], inc=[(evd, 1)])
            P.emit("vector", lambda e, o=tmu2[b][:], i_=tm4[b][:, 0:2]:
                   e.tensor_copy(out=o, in_=i_), wait=[(evd, P.cnt(evd))])
            P.emit("vector", lambda e, o=tmu8[b][:], i_=tm8[b][:]:
                   e.tensor_copy(out=o, in_=i_), inc=[(cvd, 1)])
            cv_t = P.cnt(cvd)
            P.wait("gpsimd", cvd, cv_t)
            P.dma("gpsimd", ts["gidx1"][tsl, :], tmu2[b][:], sem=std)
            P.dma("gpsimd", ts["wsel1"][tsl, :], tm4[b][:, 2:4], sem=std)
            P.dma("gpsimd", ts["stu1"][tsl, :], tmu8[b][:], sem=std)
            P.wait("gpsimd", std, P.cnt(std))
        P.wait("gpsimd", std, P.cnt(std))
        P.barrier()


def emit_l1_input_scatter(P, ts, ctx):
    nc = P.nc
    with ExitStack() as sctx:
        xt = [sctx.enter_context(sbuf_u(nc, f"is_x{b}", [128, D], BF16)) for b in range(2)]
        su = [sctx.enter_context(sbuf_u(nc, f"is_s{b}", [128, 8], U32)) for b in range(2)]
        lt2 = [P.sem("gh0"), P.sem("gh1")]
        sc = P.sem("gd0")
        buf_free = [0, 0]
        for jt in range(32):
            b = jt % 2
            tsl = slice(jt * 128, (jt + 1) * 128)
            P.wait("sync", sc, buf_free[b])
            P.emit("sync", lambda e, o=xt[b][:], i_=ts["xn1_bf"][:, tsl]:
                   e.dma_start_transpose(out=o, in_=i_), inc=[(lt2[b], 16)])
            P.dma("sync", su[b][:], ts["stu1"][tsl, :], sem=lt2[b])
            lt_t = P.cnt(lt2[b])
            for e_ in range(8):
                P.emit("gpsimd", lambda e, b_=b, ee=e_:
                       e.indirect_dma_start(out=ts["xg1"],
                                            out_offset=IndirectOffsetOnAxis(su[b_][:, ee:ee + 1], 0),
                                            in_=xt[b_][:], in_offset=None),
                       wait=[(lt2[b], lt_t)] if e_ == 0 else (), inc=[(sc, 16)])
            buf_free[b] = P.cnt(sc)
        P.wait("gpsimd", sc, P.cnt(sc))
        P.barrier()


def emit_l1_experts(P, ts, ctx):
    nc = P.nc
    with ExitStack() as octx:
        b1c = octx.enter_context(sbuf_u(nc, "ex_b1", [128, 10 * 32], F32))
        b2e = octx.enter_context(sbuf_u(nc, "ex_b2", [128, 64], F32))
        idsb = octx.enter_context(sbuf_u(nc, "ex_id", [128, 128], F32))
        ldc = P.sem("gh0")
        P.dma("sync", b1c[:], ts["l1_b1_cols"], sem=ldc)
        P.dma("sync", b2e[:], ts["c_b2e"], sem=ldc)
        _, ldc_t = P.dma("sync", idsb[:], ts["c_ident"], sem=ldc)
        P.barrier()

        for e_ in range(8):
            with ExitStack() as sctx:
                prod = sctx.enter_context(sbuf_u(nc, "ex_pr", [128, 32 * CP], BF16))
                pu = [sctx.enter_context(psum_u(nc, f"ex_pu{b}", [128, 512], F32)) for b in range(2)]
                pd = [sctx.enter_context(psum_u(nc, f"ex_pd{b}", [128, 512], F32)) for b in range(2)]
                ptb = [sctx.enter_context(psum_u(nc, f"ex_pt{b}", [128, 128], F32)) for b in range(2)]
                upctx = ExitStack()
                w1 = upctx.enter_context(sbuf_u(nc, "ex_w1", [128, 8 * 4096], BF16))
                rhs = upctx.enter_context(sbuf_u(nc, "ex_rhs", [128, 8 * CP], BF16))
                lw = P.sem("gd0")
                lw2 = [P.sem("gd1"), P.sem("gd2")]
                lr = P.sem("gh1")
                mm = P.sem("gc0")
                ev = P.sem("gc1")
                trd = P.sem("gc2")
                tev = P.sem("gc3")
                std = P.sem("gd3")

                _, lw_t = P.dma("gpsimd", w1[:].rearrange("p (c i) -> p c i", c=8),
                                ts["w_l1e1"][e_].rearrange("(c p) i -> p c i", p=128), sem=lw)
                for (off, sz) in CPCH:
                    for ds in range(8):
                        P.emit("sync", lambda e, o=rhs[:, ds * CP + off:ds * CP + off + sz],
                               i_=ts["xg1"][e_ * CP + off:e_ * CP + off + sz, ds * 128:(ds + 1) * 128]:
                               e.dma_start_transpose(out=o, in_=i_), inc=[(lr, 16)])
                lr_t = P.cnt(lr)
                ps_free = [P.cnt(ev)] * 2
                q = 0
                for it in range(32):
                    for (off, sz) in CPCH:
                        pb = q % 2
                        q += 1
                        P.wait("tensor", ev, ps_free[pb])
                        for k in range(8):
                            P.emit("tensor", lambda e, o=pu[pb][0:128, 0:sz],
                                   lt=w1[:, k * 4096 + it * 128:k * 4096 + (it + 1) * 128],
                                   rt=rhs[:, k * CP + off:k * CP + off + sz],
                                   s=(k == 0), p_=(k == 7):
                                   e.matmul(out=o, lhsT=lt, rhs=rt, start=s, stop=p_),
                                   wait=[(lw, lw_t), (lr, lr_t), (ldc, ldc_t)] if k == 0 else (),
                                   inc=[(mm, 1)] if k == 7 else ())
                        P.emit("scalar", lambda e, o=prod[:, it * CP + off:it * CP + off + sz],
                               i_=pu[pb][0:128, 0:sz], bc=b1c[:, e_ * 32 + it:e_ * 32 + it + 1]:
                               e.activation(out=o, in_=i_, func=AF.Gelu, bias=bc),
                               wait=[(mm, P.cnt(mm))], inc=[(ev, 1)])
                        ps_free[pb] = P.cnt(ev)
                ev_up = P.cnt(ev)
                upctx.close()
                dnctx = ExitStack()
                w2b = [dnctx.enter_context(sbuf_u(nc, f"ex_w2{b}", [128, 32 * 128], BF16))
                       for b in range(2)]
                yfm = dnctx.enter_context(sbuf_u(nc, "ex_y", [128, 8 * CP], F32))
                ytm = dnctx.enter_context(sbuf_u(nc, "ex_yt", [128, D], BF16))
                pd_free = [P.cnt(ev)] * 2
                w2_free = [P.cnt(mm)] * 2
                for dt in range(8):
                    wb = dt % 2
                    P.wait("gpsimd", mm, w2_free[wb])
                    _, lw2_t = P.dma("gpsimd", w2b[wb][:].rearrange("p (c i) -> p c i", c=32),
                                     ts["w_l1e2"][e_][:, dt * 128:(dt + 1) * 128]
                                     .rearrange("(c p) i -> p c i", p=128), sem=lw2[wb])
                    for ci, (off, sz) in enumerate(CPCH):
                        pb = (dt * 3 + ci) % 2
                        P.wait("tensor", ev, pd_free[pb])
                        for k in range(32):
                            P.emit("tensor", lambda e, o=pd[pb][0:128, 0:sz],
                                   lt=w2b[wb][:, k * 128:(k + 1) * 128],
                                   rt=prod[:, k * CP + off:k * CP + off + sz],
                                   s=(k == 0), p_=(k == 31):
                                   e.matmul(out=o, lhsT=lt, rhs=rt, start=s, stop=p_),
                                   wait=[(lw2[wb], lw2_t), (ev, ev_up)] if k == 0 else (),
                                   inc=[(mm, 1)] if k == 31 else ())
                        P.emit("vector", lambda e, o=yfm[:, dt * CP + off:dt * CP + off + sz],
                               i_=pd[pb][0:128, 0:sz], bc=b2e[:, e_ * 8 + dt:e_ * 8 + dt + 1]:
                               e.tensor_scalar(out=o, in0=i_, scalar1=bc, scalar2=None, op0=ALU.add),
                               wait=[(mm, P.cnt(mm))], inc=[(ev, 1)])
                        pd_free[pb] = P.cnt(ev)
                    w2_free[wb] = P.cnt(mm)
                ev_dn = P.cnt(ev)
                ptb_free = [P.cnt(tev)] * 2
                for tt in range(10):
                    st_prev = P.cnt(std)
                    for dt in range(8):
                        pb = dt % 2
                        w_ = [(ev, ev_dn), (tev, ptb_free[pb])]
                        if dt == 0:
                            w_.append((std, st_prev))
                        P.emit("tensor", lambda e, o=ptb[pb][:],
                               i_=yfm[:, dt * CP + tt * 128:dt * CP + (tt + 1) * 128]:
                               e.transpose(out=o, in_=i_, identity=idsb[:]),
                               wait=w_, inc=[(trd, 1)])
                        P.emit("scalar", lambda e, o=ytm[:, dt * 128:(dt + 1) * 128], i_=ptb[pb][:]:
                               e.copy(out=o, in_=i_),
                               wait=[(trd, P.cnt(trd)), (std, st_prev)] if dt == 0 else
                                    [(trd, P.cnt(trd))],
                               inc=[(tev, 1)])
                        ptb_free[pb] = P.cnt(tev)
                    P.wait("gpsimd", tev, P.cnt(tev))
                    P.dma("gpsimd", ts["yg1"][e_ * CP + tt * 128:e_ * CP + (tt + 1) * 128, :],
                          ytm[:], sem=std)
                P.wait("gpsimd", std, P.cnt(std))
                P.barrier()
                dnctx.close()


def emit_l1_sparse_combine(P, ts, ctx):
    nc = P.nc
    with ExitStack() as sctx:
        gy = sctx.enter_context(sbuf_u(nc, "cbm_g", [128, 4 * D], BF16))
        gu = [sctx.enter_context(sbuf_u(nc, f"cbm_u{b}", [128, 2], U32)) for b in range(2)]
        wv = [sctx.enter_context(sbuf_u(nc, f"cbm_w{b}", [128, 2], F32)) for b in range(2)]
        acc = [sctx.enter_context(sbuf_u(nc, f"cbm_a{b}", [128, D], F32)) for b in range(2)]
        ymt = [sctx.enter_context(sbuf_u(nc, f"cbm_y{b}", [128, 128], F32)) for b in range(2)]
        idsb = sctx.enter_context(sbuf_u(nc, "cbm_id", [128, 128], F32))
        pt = [sctx.enter_context(psum_u(nc, f"cbm_p{b}", [128, 128], F32)) for b in range(2)]
        ldc = P.sem("gh0")
        lg2 = [P.sem("gh1"), P.sem("gh2")]
        gth = P.sem("gd0")
        vd = P.sem("gc0")
        trd = P.sem("gc1")
        tev = P.sem("gc2")
        std = P.sem("gd1")
        _, ldc_t = P.dma("sync", idsb[:], ts["c_ident"], sem=ldc)
        buf_free = [0, 0]
        ymt_free = [0, 0]
        ymt_store = [0, 0]
        acc_free = [0, 0]
        for jt in range(32):
            b = jt % 2
            tsl = slice(jt * 128, (jt + 1) * 128)
            P.wait("sync", vd, buf_free[b])
            P.dma("sync", gu[b][:], ts["gidx1"][tsl, :], sem=lg2[b])
            P.dma("sync", wv[b][:], ts["wsel1"][tsl, :], sem=lg2[b])
            lg_t = P.cnt(lg2[b])
            P.wait("gpsimd", vd, buf_free[b])
            for r in range(2):
                P.emit("gpsimd", lambda e, b_=b, rr=r:
                       e.indirect_dma_start(out=gy[:, (2 * b_ + rr) * D:(2 * b_ + rr + 1) * D],
                                            out_offset=None, in_=ts["yg1"],
                                            in_offset=IndirectOffsetOnAxis(gu[b_][:, rr:rr + 1], 0)),
                       wait=[(lg2[b], lg_t)] if r == 0 else (), inc=[(gth, 16)])
            gth_t = P.cnt(gth)
            P.emit("vector", lambda e, b_=b: e.tensor_scalar(
                   out=acc[b_][:], in0=gy[:, (2 * b_) * D:(2 * b_ + 1) * D],
                   scalar1=wv[b_][:, 0:1], scalar2=None, op0=ALU.mult),
                   wait=[(gth, gth_t), (trd, acc_free[b])])
            P.emit("vector", lambda e, b_=b: e.scalar_tensor_tensor(
                   out=acc[b_][:], in0=gy[:, (2 * b_ + 1) * D:(2 * b_ + 2) * D],
                   scalar=wv[b_][:, 1:2], in1=acc[b_][:], op0=ALU.mult, op1=ALU.add),
                   inc=[(vd, 1)])
            vd_t = P.cnt(vd)
            for ds in range(8):
                pb = ds % 2
                P.emit("tensor", lambda e, o=pt[pb][:], i_=acc[b][:, ds * 128:(ds + 1) * 128]:
                       e.transpose(out=o, in_=i_, identity=idsb[:]),
                       wait=[(vd, vd_t), (ldc, ldc_t), (tev, ymt_free[pb])], inc=[(trd, 1)])
                P.emit("scalar", lambda e, o=ymt[pb][:], i_=pt[pb][:]:
                       e.copy(out=o, in_=i_),
                       wait=[(trd, P.cnt(trd)), (std, ymt_store[pb])],
                       inc=[(tev, 1)])
                tev_t = P.cnt(tev)
                ymt_free[pb] = tev_t
                P.wait("gpsimd", tev, tev_t)
                P.dma("gpsimd", ts["y1moe"][ds * 128:(ds + 1) * 128, tsl], ymt[pb][:], sem=std)
                ymt_store[pb] = P.cnt(std)
            buf_free[b] = P.cnt(vd)
            acc_free[b] = P.cnt(trd)
        P.wait("gpsimd", std, P.cnt(std))
        P.barrier()


def emit_combine2(P, ts, ctx, x_in, s_in, y1_in, y2_in, alpha, x_out, tc=512):
    nc = P.nc
    nblk = D // 128
    nch = T // tc
    with ExitStack() as sctx:
        x_sb = sctx.enter_context(sbuf_u(nc, "c2_x", [128, 2 * tc], F32))
        s_sb = sctx.enter_context(sbuf_u(nc, "c2_s", [128, 2 * tc], F32))
        y1_sb = sctx.enter_context(sbuf_u(nc, "c2_y1", [128, 2 * tc], F32))
        y2_sb = sctx.enter_context(sbuf_u(nc, "c2_y2", [128, 2 * tc], F32))
        sy_sb = sctx.enter_context(sbuf_u(nc, "c2_sy", [128, 2 * tc], F32))
        o_sb = sctx.enter_context(sbuf_u(nc, "c2_o", [128, 2 * tc], F32))
        ld2 = [P.sem("gh0"), P.sem("gh1")]
        ly2 = [P.sem("gd0"), P.sem("gd1")]
        sd = P.sem("gc0")
        vd = P.sem("gc1")
        st = P.sem("gd2")
        slot_free = [P.cnt(st)] * 2
        for g in range(nblk):
            for c in range(nch):
                t0 = c * tc
                sb = (g * nch + c) % 2
                sl = slice(sb * tc, (sb + 1) * tc)
                rows = slice(g * 128, (g + 1) * 128)
                P.wait("sync", st, slot_free[sb])
                P.dma("sync", x_sb[:, sl], x_in[rows, t0:t0 + tc], sem=ld2[sb])
                P.dma("sync", s_sb[:, sl], s_in[rows, t0:t0 + tc], sem=ld2[sb])
                P.wait("gpsimd", st, slot_free[sb])
                P.dma("gpsimd", y1_sb[:, sl], y1_in[rows, t0:t0 + tc], sem=ly2[sb])
                P.dma("gpsimd", y2_sb[:, sl], y2_in[rows, t0:t0 + tc], sem=ly2[sb])
                P.emit("vector", lambda e, o=sy_sb[:, sl], a=y1_sb[:, sl], b=y2_sb[:, sl], al=alpha:
                       e.scalar_tensor_tensor(out=o, in0=a, scalar=al, in1=b,
                                              op0=ALU.mult, op1=ALU.add),
                       wait=[(ly2[sb], P.cnt(ly2[sb]))], inc=[(vd, 1)])
                P.emit("scalar", lambda e, o=sy_sb[:, sl], i_=sy_sb[:, sl]:
                       e.activation(out=o, in_=i_, func=AF.Silu),
                       wait=[(vd, P.cnt(vd))], inc=[(sd, 1)])
                P.emit("vector", lambda e, o=sy_sb[:, sl], a=s_sb[:, sl]:
                       e.tensor_tensor(out=o, in0=o, in1=a, op=ALU.mult),
                       wait=[(sd, P.cnt(sd)), (ld2[sb], P.cnt(ld2[sb]))])
                P.emit("vector", lambda e, o=o_sb[:, sl], a=x_sb[:, sl], b=sy_sb[:, sl]:
                       e.tensor_tensor(out=o, in0=a, in1=b, op=ALU.add),
                       inc=[(vd, 1)])
                P.wait("gpsimd", vd, P.cnt(vd))
                P.dma("gpsimd", x_out[rows, t0:t0 + tc], o_sb[:, sl], sem=st)
                slot_free[sb] = P.cnt(st)
        P.wait("gpsimd", st, P.cnt(st))
        P.barrier()


def emit_l1_moe(P, ts, ctx, tc=512):
    nc = P.nc
    nk = D // 128
    HID = 4096
    NIT = HID // 128
    nch = T // tc
    with ExitStack() as sctx:
        rhs = sctx.enter_context(sbuf_u(nc, "m1_rhs", [128, nk * tc], BF16))
        w1t = [sctx.enter_context(sbuf_u(nc, f"m1_w1{b}", [128, nk * 128], BF16)) for b in range(2)]
        w2t = [sctx.enter_context(sbuf_u(nc, f"m1_w2{b}", [128, NIT * 128], BF16)) for b in range(2)]
        b1c = sctx.enter_context(sbuf_u(nc, "m1_b1", [128, 10 * NIT], F32))
        b2bf = sctx.enter_context(sbuf_u(nc, "m1_b2", [8, D], BF16))
        sb2bf = sctx.enter_context(sbuf_u(nc, "m1_sb2", [2, D], BF16))
        wch = sctx.enter_context(sbuf_u(nc, "m1_wch", [8, tc], F32))
        wbf = sctx.enter_context(sbuf_u(nc, "m1_wbf", [8, tc], BF16))
        halfrow = sctx.enter_context(sbuf_u(nc, "m1_half", [2, tc], BF16))
        sel8 = sctx.enter_context(sbuf_u(nc, "m1_sel", [8, 8 * 128], F32))
        wbc = sctx.enter_context(sbuf_u(nc, "m1_wbc", [128, tc], F32))
        he_f = sctx.enter_context(sbuf_u(nc, "m1_hef", [128, tc], F32))
        prod = sctx.enter_context(sbuf_u(nc, "m1_prod", [128, NIT * tc], BF16))
        y_sb = sctx.enter_context(sbuf_u(nc, "m1_y", [128, nk * tc], F32))
        psu = [sctx.enter_context(psum_u(nc, f"m1_psu{b}", [128, tc], F32)) for b in range(2)]
        psy = [sctx.enter_context(psum_u(nc, f"m1_psy{b}", [128, tc], F32)) for b in range(2)]
        psb = sctx.enter_context(psum_u(nc, "m1_psb", [128, tc], F32))
        ldc = P.sem("gh0")
        lw1 = [P.sem("gd0"), P.sem("gd1")]
        lw2s = [P.sem("gd2"), P.sem("gd3")]
        lr = P.sem("gd4")
        lwc = P.sem("gh1")
        mmU = P.sem("gc0")
        mmB = P.sem("gc1")
        evU = P.sem("gc2")
        pvd = P.sem("gc3")
        mmD = P.sem("gc4")
        acc = P.sem("gc5")
        st = P.sem("gd5")

        P.dma("sync", sel8[:], ts["c_sel8"], sem=ldc)
        P.dma("sync", b1c[:], ts["l1_b1_cols"], sem=ldc)
        P.dma("gpsimd", b2bf[:], ts["b_l1e2"], sem=lw1[0])
        _, lb2_t = P.dma("gpsimd", sb2bf[:], ts["b_l1s2"], sem=lw1[0])
        ldc_t = P.cnt(ldc)
        P.emit("vector", lambda e: e.memset(halfrow[:], 0.5), inc=[(evU, 1)])
        P.barrier()

        wbuf_free = [P.cnt(pvd)] * 2
        w2buf_free = [P.cnt(mmD)] * 2
        psu_free = [P.cnt(pvd)] * 2
        psy_free = [P.cnt(acc)] * 2
        prod_free = P.cnt(mmD)
        wbc_guard = P.cnt(pvd)
        y_store_free = P.cnt(st)
        for c in range(nch):
            t0 = c * tc
            P.wait("gpsimd", mmU, P.cnt(mmU))
            P.dma("gpsimd", rhs[:].rearrange("p (c t) -> p c t", c=nk),
                  ts["xn1_bf"][:, t0:t0 + tc].rearrange("(c p) t -> p c t", p=128), sem=lr)
            lr_t = P.cnt(lr)
            P.wait("sync", mmB, P.cnt(mmB))
            P.dma("sync", wch[:], ts["l1w"][:, t0:t0 + tc], sem=lwc)
            lwc_t = P.cnt(lwc)
            P.emit("vector", lambda e: e.tensor_copy(out=wbf[:], in_=wch[:]),
                   wait=[(lwc, lwc_t), (mmD, P.cnt(mmD))], inc=[(evU, 1)])
            for ei in range(10):
                routed = ei < 8
                if routed:
                    P.emit("tensor", lambda e, lt=sel8[:, ei * 128:(ei + 1) * 128]:
                           e.matmul(out=psb[:], lhsT=lt, rhs=wch[:], start=True, stop=True),
                           wait=[(lwc, lwc_t), (ldc, P.cnt(ldc)), (evU, P.cnt(evU))], inc=[(mmB, 1)])
                    P.emit("scalar", lambda e: e.copy(out=wbc[:], in_=psb[:]),
                           wait=[(mmB, P.cnt(mmB)), (pvd, wbc_guard)], inc=[(evU, 1)])
                for it in range(NIT):
                    wb = it % 2
                    pb = it % 2
                    P.wait("sync", pvd, wbuf_free[wb])
                    wsrc = ts["b_w_l1e1t"][ei] if routed else ts["b_w_l1s1t"][ei - 8]
                    P.dma("sync", w1t[wb][:], wsrc[it], sem=lw1[wb])
                    lw_t = P.cnt(lw1[wb])
                    P.wait("tensor", pvd, psu_free[pb])
                    for k in range(nk):
                        P.emit("tensor", lambda e, o=psu[pb][:], lt=w1t[wb][:, k * 128:(k + 1) * 128],
                               rt=rhs[:, k * tc:(k + 1) * tc], s=(k == 0), p_=(k == nk - 1):
                               e.matmul(out=o, lhsT=lt, rhs=rt, start=s, stop=p_),
                               wait=[(lw1[wb], lw_t), (lr, lr_t)] if k == 0 else (),
                               inc=[(mmU, 1)] if k == nk - 1 else ())
                    bslot = (ei * NIT + it) if routed else ((8 + (ei - 8)) * NIT + it)
                    P.emit("scalar", lambda e, o=he_f[:], i_=psu[pb][:], b=b1c[:, bslot:bslot + 1]:
                           e.activation(out=o, in_=i_, func=AF.Gelu, bias=b),
                           wait=[(mmU, P.cnt(mmU)), (pvd, P.cnt(pvd))], inc=[(evU, 1)])
                    if routed:
                        P.emit("vector", lambda e, o=prod[:, it * tc:(it + 1) * tc], a=he_f[:], b=wbc[:]:
                               e.tensor_tensor(out=o, in0=a, in1=b, op=ALU.mult),
                               wait=[(evU, P.cnt(evU)), (mmD, prod_free)], inc=[(pvd, 1)])
                    else:
                        P.emit("vector", lambda e, o=prod[:, it * tc:(it + 1) * tc], a=he_f[:]:
                               e.tensor_scalar(out=o, in0=a, scalar1=0.5, scalar2=None, op0=ALU.mult),
                               wait=[(evU, P.cnt(evU)), (mmD, prod_free)], inc=[(pvd, 1)])
                    psu_free[pb] = P.cnt(pvd)
                    wbuf_free[wb] = P.cnt(pvd)
                wbc_guard = P.cnt(pvd)
                pvd_t = P.cnt(pvd)
                for dt in range(nk):
                    wb = dt % 2
                    pb = dt % 2
                    P.wait("sync", mmD, w2buf_free[wb])
                    w2src = ts["b_w_l1e2t"][ei] if routed else ts["b_w_l1s2t"][ei - 8]
                    P.dma("sync", w2t[wb][:], w2src[dt], sem=lw2s[wb])
                    lw2_t = P.cnt(lw2s[wb])
                    has_bias = routed or ei == 9
                    P.wait("tensor", acc, psy_free[pb])
                    for k in range(NIT):
                        P.emit("tensor", lambda e, o=psy[pb][:], lt=w2t[wb][:, k * 128:(k + 1) * 128],
                               rt=prod[:, k * tc:(k + 1) * tc], s=(k == 0),
                               p_=(not has_bias and k == NIT - 1):
                               e.matmul(out=o, lhsT=lt, rhs=rt, start=s, stop=p_),
                               wait=[(lw2s[wb], lw2_t), (pvd, pvd_t)] if k == 0 else (),
                               inc=[(mmD, 1)] if (not has_bias and k == NIT - 1) else ())
                    if routed:
                        P.emit("tensor", lambda e, o=psy[pb][:], lt=b2bf[:, dt * 128:(dt + 1) * 128],
                               rt=wbf[:]:
                               e.matmul(out=o, lhsT=lt, rhs=rt, start=False, stop=True),
                               wait=[(lw1[0], lb2_t), (evU, P.cnt(evU))],
                               inc=[(mmD, 1)])
                    elif ei == 9:
                        P.emit("tensor", lambda e, o=psy[pb][:], lt=sb2bf[:, dt * 128:(dt + 1) * 128],
                               rt=halfrow[:]:
                               e.matmul(out=o, lhsT=lt, rhs=rt, start=False, stop=True),
                               inc=[(mmD, 1)])
                    ysl = y_sb[:, dt * tc:(dt + 1) * tc]
                    if ei == 0:
                        P.emit("vector", lambda e, o=ysl, p_=psy[pb][:]: e.tensor_copy(out=o, in_=p_),
                               wait=[(mmD, P.cnt(mmD)), (st, y_store_free)], inc=[(acc, 1)])
                    else:
                        P.emit("vector", lambda e, o=ysl, p_=psy[pb][:]:
                               e.tensor_tensor(out=o, in0=o, in1=p_, op=ALU.add),
                               wait=[(mmD, P.cnt(mmD))], inc=[(acc, 1)])
                    psy_free[pb] = P.cnt(acc)
                    w2buf_free[wb] = P.cnt(mmD)
                prod_free = P.cnt(mmD)
            P.wait("gpsimd", acc, P.cnt(acc))
            for dt in range(nk):
                P.dma("gpsimd", ts["y1ffn"][dt * 128:(dt + 1) * 128, t0:t0 + tc],
                      y_sb[:, dt * tc:(dt + 1) * tc], sem=st)
            y_store_free = P.cnt(st)
        P.wait("gpsimd", st, P.cnt(st))
        P.barrier()


def build_dev(shared, phases, dump=()):
    nc = bass.Bass("TRN2", target_bir_lowering=False)
    ctx = ExitStack()
    ts = Tensors(nc, shared, dump=dump)
    ts.scratch("xn_bf", [D, T], BF16)
    ts.scratch("xn_f32", [D, T], F32)
    ts.scratch("u_pre", [DI, T], F32)
    ts.scratch("z_f32", [DI, T], F32)
    ts.scratch("u_f32", [DI, T], F32)
    ts.scratch("u_bf", [DI, T], BF16)
    ts.scratch("dtbc", [96, T], F32)
    ts.scratch("delta", [DI, T], F32)
    ts.scratch("du", [DI, T], F32)
    ts.scratch("y_scan", [DI, T], F32)
    ts.scratch("yg_bf", [DI, T], BF16)
    ts.scratch("mam_res", [D, T], F32)
    ts.scratch("h1_bf", [D, T], BF16)
    ts.scratch("h1_f32", [D, T], F32)
    ts.scratch("ff_bf", [4 * D, T], BF16)
    ts.scratch("s0pre", [D, T], F32)
    ts.scratch("s0", [D, T], F32)
    ts.scratch("comb", [32, T], F32)
    ts.scratch("y0ffn", [D, T], F32)
    ts.scratch("x1", [D, T], F32)
    ts.scratch("xn1_bf", [D, T], BF16)
    ts.scratch("xn1_f32", [D, T], F32)
    ts.scratch("xe", [D, T], F32)
    ts.scratch("h1l_bf", [D, T], BF16)
    ts.scratch("q1_bf", [D, T], BF16)
    ts.scratch("xe2", [D, T], F32)
    ts.scratch("h2l_bf", [D, T], BF16)
    ts.scratch("k_tm", [T, D], BF16)
    ts.scratch("v_tm", [T, D], BF16)
    ts.scratch("ffl_bf", [4 * D, T], BF16)
    ts.scratch("s1", [D, T], F32)
    ts.scratch("l1w", [8, T], F32)
    ts.scratch("y1ffn", [D, T], F32)
    ts.scratch("b_w_l1e1t", [8, 32, 128, 1024], BF16)
    ts.scratch("b_w_l1e2t", [8, 8, 128, 4096], BF16)
    ts.scratch("b_w_l1s1t", [2, 32, 128, 1024], BF16)
    ts.scratch("b_w_l1s2t", [2, 8, 128, 4096], BF16)
    ts.scratch("b_w_l0egp", [32, 128, 4096], BF16)
    ts.scratch("b_w_l0eup", [32, 128, 4096], BF16)
    ts.scratch("b_w_l0edp", [8, 4, 128, 4096], BF16)
    ts.scratch("b_w_l0sgp", [128, 8192], BF16)
    ts.scratch("b_w_l0sup0", [128, 4096], BF16)
    ts.scratch("b_w_l0sup1", [128, 4096], BF16)
    ts.scratch("b_w_l0sdp", [8, 128, 1024], BF16)
    ts.scratch("xg1", [8 * CP + 128, D], BF16)
    ts.scratch("yg1", [8 * CP + 128, D], BF16)
    ts.scratch("stu1", [T, 8], U32)
    ts.scratch("gidx1", [T, 2], U32)
    ts.scratch("wsel1", [T, 2], F32)
    ts.scratch("ysh0", [D, T], F32)
    ts.scratch("ysh", [D, T], F32)
    ts.scratch("y1moe", [D, T], F32)
    ts.scratch("dbg", [16 * 128, 512], F32)
    with ctx:
        P = Prog(nc, ctx)
        boot(P, ts, ctx)
        if "ln0" in phases:
            emit_precast(P, ts, ctx)
        if "ln0" in phases:
            emit_ln_fm(P, ts, ctx, ts["x0"], out_bf=ts["xn_bf"], out_f32=ts["xn_f32"])
        if "l0m_inproj" in phases:
            emit_proj2(P, ctx, w=ts["w_ipu_p"], Kdim=D, Idim=DI,
                       rhs=ts["xn_bf"], outs=[(ts["u_pre"], F32)], tag="ipu")
            emit_proj2(P, ctx, w=ts["w_ipz_p"], Kdim=D, Idim=DI,
                       rhs=ts["xn_bf"], outs=[(ts["z_f32"], F32)], tag="ipz")
        if "l0m_conv" in phases:
            emit_l0_conv(P, ts, ctx)
        if "l0m_xproj" in phases:
            emit_proj2(P, ctx, w=ts["w_xproj_p"], Kdim=DI, Idim=96,
                       rhs=ts["u_bf"], outs=[(ts["dtbc"], F32)], tag="xp")
        if "l0m_delta" in phases:
            emit_l0_delta(P, ts, ctx)
        if "l0m_scan" in phases:
            emit_l0_scan(P, ts, ctx, dbg=("dbg" in dump))
        if "l0m_gate" in phases:
            emit_l0_gate(P, ts, ctx)
        if "l0m_out" in phases:
            emit_proj2(P, ctx, w=ts["w_outproj_p"], Kdim=DI, Idim=D,
                       rhs=ts["yg_bf"], res=ts["xn_f32"], outs=[(ts["mam_res"], F32)], tag="op")
            emit_rms_fm(P, ts, ctx, ts["mam_res"], "g_norm", 1e-12,
                        out_bf=ts["h1_bf"], out_f32=ts["h1_f32"], tag="rm1")
        if "l0m_ffn" in phases:
            emit_proj2(P, ctx, w=ts["w_l0ff1_p"], Kdim=D, Idim=4 * D,
                       rhs=ts["h1_bf"], bias=ts["b_l0ff1"], func=AF.Gelu,
                       outs=[(ts["ff_bf"], BF16)], tag="f1")
            emit_proj2(P, ctx, w=ts["w_l0ff2_p"], Kdim=4 * D, Idim=D,
                       rhs=ts["ff_bf"], bias=ts["b_l0ff2"], res=ts["h1_f32"],
                       outs=[(ts["s0pre"], F32)], tag="f2")
            emit_rms_fm(P, ts, ctx, ts["s0pre"], "g_ffnorm", 1e-12,
                        out_f32=ts["s0"], tag="rm2")
        if "l0f_router" in phases:
            emit_l0_router(P, ts, ctx, dbg=("dbg" in dump))
        if "l0f_moe" in phases:
            emit_l0_moe(P, ts, ctx)
        if "l0_combine" in phases:
            emit_combine(P, ts, ctx, ts["x0"], ts["s0"], ts["y0ffn"], ts["x1"])
        if "ln1" in phases:
            emit_ln_fm(P, ts, ctx, ts["x1"], out_bf=ts["xn1_bf"], out_f32=ts["xn1_f32"],
                       extra=("axpb", "xe", 32.0, "c_pe"))
        for l in range(4):
            if f"l1a_{l}" in phases:
                emit_ln_sb(P, ts, ctx, ts["xe"],
                           ts["ln1_s_cols"][:, l * 8:(l + 1) * 8],
                           ts["ln1_b_cols"][:, l * 8:(l + 1) * 8],
                           out_bf=ts["h1l_bf"])
                emit_proj2(P, ctx, w=ts["w_Wq_p"][l], Kdim=D, Idim=D, rhs=ts["h1l_bf"],
                           bias=ts["bq_cols"][:, l * 8:(l + 1) * 8], func="elu1",
                           outs=[(ts["q1_bf"], BF16)], tag=f"q{l}")
                emit_l1_attn_layer(P, ts, ctx, l)
                emit_ln_sb(P, ts, ctx, ts["xe2"],
                           ts["ln2_s_cols"][:, l * 8:(l + 1) * 8],
                           ts["ln2_b_cols"][:, l * 8:(l + 1) * 8],
                           out_bf=ts["h2l_bf"])
                emit_proj2(P, ctx, w=ts["w_W1_p"][l], Kdim=D, Idim=4 * D, rhs=ts["h2l_bf"],
                           bias=ts["b1_cols"][:, l * 32:(l + 1) * 32], func=AF.Relu,
                           outs=[(ts["ffl_bf"], BF16)], tag=f"e1_{l}")
                emit_proj2(P, ctx, w=ts["w_W2_p"][l], Kdim=4 * D, Idim=D, rhs=ts["ffl_bf"],
                           bias=ts["b2_cols"][:, l * 8:(l + 1) * 8], res=ts["xe2"],
                           outs=[(ts["xe"], F32)], tag=f"e2_{l}")
        if "l1a_final" in phases:
            emit_ln_sb(P, ts, ctx, ts["xe"], ts["p_lnf_s"], ts["p_lnf_b"],
                       out_f32=ts["s1"])
        if "l1f_router" in phases:
            emit_l1_router(P, ts, ctx)
        if "l1f_moe" in phases:
            emit_l1_moe(P, ts, ctx)
        if "l1_combine" in phases:
            emit_combine(P, ts, ctx, ts["x1"], ts["s1"], ts["y1ffn"], ts["out"])
        if "l1_combine" not in phases:
            P.dma("sync", ts["out"][0:128, 0:128], ts["x0"][0:128, 0:128], sem="dma_sync")
            P.wait("sync", "dma_sync", P.cnt("dma_sync"))
        P.finalize()
    return nc


ALL_PHASES = ["ln0", "l0m_inproj", "l0m_conv", "l0m_xproj", "l0m_delta", "l0m_scan",
              "l0m_gate", "l0m_out", "l0m_ffn", "l0f_router", "l0f_moe",
              "l0_combine", "ln1", "l1a_0", "l1a_1", "l1a_2", "l1a_3", "l1a_final",
              "l1f_router", "l1f_moe", "l1_combine"]


def kernel(**inputs):
    shared, per_core = host_prep(inputs)
    nc = build_dev(shared, ALL_PHASES, dump=())
    in_maps = []
    for c in range(NCORES):
        m = dict(shared)
        m.update(per_core[c])
        in_maps.append(m)
    res = run_bass_kernel_spmd(nc, in_maps, core_ids=list(range(NCORES)))
    out = np.stack([np.ascontiguousarray(res.results[c]["out"].T) for c in range(NCORES)])
    return out.astype(np.float32)


if __name__ == "__main__":
    pass
